# version 29
# baseline (speedup 1.0000x reference)
import numpy as np
import concourse.bass as bass
import concourse.mybir as mybir
from concourse.bass_utils import run_bass_kernel_spmd

F32 = mybir.dt.float32
BF16 = mybir.dt.bfloat16
I32 = mybir.dt.int32
AF = mybir.ActivationFunctionType
ALU = mybir.AluOpType

D = 1024
DI = 2048
NH = 16
DS = 16
NCH = 16
CHUNK = 64
WIN = 1024
OWN = 2048
TOT = WIN + OWN
G = 512
NG = TOT // G
FH = 4096
LN_EPS = 1e-5
ARF = 44048

ENGS = ("sync", "gpsimd", "tensor", "vector", "scalar")


class P:

    def __init__(self):
        self.q = {e: [] for e in ENGS}
        self.cnt = {}

    def on(self, eng):
        def deco(fn):
            self.q[eng].append(fn)
        return deco

    def add(self, name, k=1):
        self.cnt[name] = self.cnt.get(name, 0) + k
        return self.cnt[name]

    def get(self, name):
        return self.cnt.get(name, 0)


def build(debug=False, cut=None, p4bar=False):
    nc = bass.Bass("TRN2", target_bir_lowering=False)

    xs = nc.dram_tensor("xs", [TOT, D], F32, kind="ExternalInput").ap()
    W_in = nc.dram_tensor("W_in", [D, 2 * DI], F32, kind="ExternalInput").ap()
    conv_w = nc.dram_tensor("conv_w", [128, 64], F32, kind="ExternalInput").ap()
    conv_b = nc.dram_tensor("conv_b", [DI // 128, 128], F32, kind="ExternalInput").ap()
    W_xp = nc.dram_tensor("W_xp", [DI, 48], F32, kind="ExternalInput").ap()
    W_dt = nc.dram_tensor("W_dt", [NH, DI], F32, kind="ExternalInput").ap()
    b_dt = nc.dram_tensor("b_dt", [DI // 128, 128], F32, kind="ExternalInput").ap()
    A_log = nc.dram_tensor("A_log", [1, NH], F32, kind="ExternalInput").ap()
    D_skip = nc.dram_tensor("D_skip", [1, NH], F32, kind="ExternalInput").ap()
    W_out = nc.dram_tensor("W_out", [DI, D], F32, kind="ExternalInput").ap()
    ln_g = nc.dram_tensor("ln_g", [DI // 128, 128], F32, kind="ExternalInput").ap()
    ln_b = nc.dram_tensor("ln_b", [DI // 128, 128], F32, kind="ExternalInput").ap()
    ln1_g = nc.dram_tensor("ln1_g", [D // 128, 128], F32, kind="ExternalInput").ap()
    ln1_b = nc.dram_tensor("ln1_b", [D // 128, 128], F32, kind="ExternalInput").ap()
    w1T = nc.dram_tensor("w1T", [D, FH], F32, kind="ExternalInput").ap()
    b1 = nc.dram_tensor("b1", [FH // 128, 128], F32, kind="ExternalInput").ap()
    ln2_g = nc.dram_tensor("ln2_g", [FH // 128, 128], F32, kind="ExternalInput").ap()
    ln2_b = nc.dram_tensor("ln2_b", [FH // 128, 128], F32, kind="ExternalInput").ap()
    w2T = nc.dram_tensor("w2T", [FH, D], F32, kind="ExternalInput").ap()
    b2 = nc.dram_tensor("b2", [D // 128, 128], F32, kind="ExternalInput").ap()
    out = nc.dram_tensor("out", [OWN, D], F32, kind="ExternalOutput").ap()

    sp_xn = nc.dram_tensor("sp_xn", [TOT, D], BF16, kind="Internal").ap()
    sp_xc = nc.dram_tensor("sp_xc", [NCH, 128, TOT], F32, kind="Internal").ap()
    sp_z = nc.dram_tensor("sp_z", [NCH, 128, OWN], F32, kind="Internal").ap()
    sp_y2 = nc.dram_tensor("sp_y2", [NCH, 128, OWN], BF16, kind="Internal").ap()
    sp_mix = nc.dram_tensor("sp_mix", [D // 128, 128, OWN], F32, kind="Internal").ap()
    sp_h = nc.dram_tensor("sp_h", [FH // 128, 128, OWN], BF16, kind="Internal").ap()
    wq1d = nc.dram_tensor("wq1d", [8, 128, FH], BF16, kind="Internal").ap()
    sp_st = nc.dram_tensor("sp_st", [2, OWN], F32, kind="Internal").ap()
    sp_st2 = nc.dram_tensor("sp_st2", [2, OWN], F32, kind="Internal").ap()
    if debug:
        dbg_y2 = nc.dram_tensor("dbg_y2", [NCH, 128, OWN], BF16, kind="ExternalOutput").ap()
        dbg_st = nc.dram_tensor("dbg_st", [2, OWN], F32, kind="ExternalOutput").ap()
        dbg_st2 = nc.dram_tensor("dbg_st2", [2, OWN], F32, kind="ExternalOutput").ap()
        dbg_mix = nc.dram_tensor("dbg_mix", [D // 128, 128, OWN], F32, kind="ExternalOutput").ap()
        dbg_xc = nc.dram_tensor("dbg_xc", [NCH, 128, TOT], F32, kind="ExternalOutput").ap()
        dbg_h = nc.dram_tensor("dbg_h", [FH // 128, 128, OWN], BF16, kind="ExternalOutput").ap()
        dbg_xn = nc.dram_tensor("dbg_xn", [TOT, D], BF16, kind="ExternalOutput").ap()

    pr = P()
    add, get = pr.add, pr.get

    import contextlib
    with contextlib.ExitStack() as stk:
        en = stk.enter_context
        AR = en(nc.sbuf_tensor("AR", [128, ARF], F32))
        projp = en(nc.sbuf_tensor("projp", [128, TOT], F32))
        identf = en(nc.sbuf_tensor("identf", [128, 128], F32))
        identb = en(nc.sbuf_tensor("identb", [128, 128], BF16))
        ones1 = en(nc.sbuf_tensor("ones1", [1, 128], F32))
        onesf = en(nc.sbuf_tensor("onesf", [128, 1], F32))
        ones1b = en(nc.sbuf_tensor("ones1b", [1, 128], BF16))
        onescb = en(nc.sbuf_tensor("onescb", [128, 1], BF16))
        vcol = en(nc.sbuf_tensor("vcol", [128, 400], F32))
        rb2 = en(nc.sbuf_tensor("rb2", [128, 512], BF16))
        cw = en(nc.sbuf_tensor("cw", [128, 64], F32))
        carry = en(nc.sbuf_tensor("carry", [128, 256], F32))
        wdtb = en(nc.sbuf_tensor("wdtb", [16, DI], BF16))
        cmask = en(nc.sbuf_tensor("cmask", [128, G], F32))
        ck = en(nc.sbuf_tensor("ck", [128, 2], F32))
        pA = en(nc.psum_tensor("pA", [128, 512], F32))
        pB = en(nc.psum_tensor("pB", [128, 512], F32))
        pC = en(nc.psum_tensor("pC", [128, 512], F32))
        pD = en(nc.psum_tensor("pD", [128, 512], F32))
        pE = en(nc.psum_tensor("pE", [128, 512], F32))
        pF = en(nc.psum_tensor("pF", [128, 512], F32))
        pG = en(nc.psum_tensor("pG", [128, 512], F32))
        pH = en(nc.psum_tensor("pH", [128, 512], F32))
        qs = en(nc.semaphore("qs"))
        qg = en(nc.semaphore("qg"))
        qc = en(nc.semaphore("qc"))
        mm = en(nc.semaphore("mm"))
        ev = en(nc.semaphore("ev"))
        sp = en(nc.semaphore("sp"))
        gp = en(nc.semaphore("gp"))
        ph = en(nc.semaphore("ph"))
        qt = en(nc.semaphore("qt"))
        yr = en(nc.semaphore("yr"))
        qh = en(nc.semaphore("qh"))
        qi = en(nc.semaphore("qi"))
        qm = en(nc.semaphore("qm"))
        ql = en(nc.semaphore("ql"))
        blk = en(nc.Block())
        ARB = AR.bitcast(BF16)
        nc.const_aps.register(F32, LN_EPS, ck[:, 0:1])
        nc.const_aps.register(F32, 1.0, ck[:, 1:2])
        SEM = dict(qs=qs, qg=qg, qc=qc, mm=mm, ev=ev, sp=sp, gp=gp, ph=ph, qt=qt,
                   yr=yr, qh=qh, qi=qi, qm=qm, ql=ql)
        nbar = [0]

        barrier_marks = []

        def barrier():
            nbar[0] += 1
            target = 5 * nbar[0]
            qsv, qgv, qcv, gpv_b, qtv = (get("qs"), get("qg"), get("qc"),
                                         get("gp"), get("qt"))
            qhv_b = get("qh")
            yrv_b, qlv_b, qiv_b, qmv_b = (get("yr"), get("ql"), get("qi"),
                                          get("qm"))
            for eng in ENGS:
                @pr.on(eng)
                def _(e, eng=eng, target=target, qsv=qsv, qgv=qgv, qcv=qcv,
                      gpv_b=gpv_b, qtv=qtv, qhv_b=qhv_b):
                    if eng == "sync":
                        e.wait_ge(qs, qsv)
                        e.wait_ge(qc, qcv)
                        e.wait_ge(qt, qtv)
                        e.wait_ge(yr, yrv_b)
                        e.wait_ge(ql, qlv_b)
                    elif eng == "gpsimd":
                        e.wait_ge(qg, qgv)
                        e.wait_ge(qh, qhv_b)
                        e.wait_ge(qi, qiv_b)
                        e.wait_ge(qm, qmv_b)
                        e.wait_ge(gp, gpv_b)
                    elif eng == "scalar":
                        e.wait_ge(qc, qcv)
                    e.drain().then_inc(ph, 1)
                    e.wait_ge(ph, target)
            barrier_marks.append({eng: len(pr.q[eng]) for eng in ENGS})

        def dma(eng, outap, inap, sem="qs"):
            add(sem, 16)
            s = SEM[sem]
            @pr.on(eng)
            def _(e, outap=outap, inap=inap, s=s):
                e.dma_start(out=outap, in_=inap).then_inc(s, 16)

        def dmat(outap, inap, sem="qs"):
            add(sem, 16)
            s = SEM[sem]
            @pr.on("sync")
            def _(e, outap=outap, inap=inap, s=s):
                e.dma_start_transpose(out=outap, in_=inap).then_inc(s, 16)

        VOFF = {}
        _o = 0
        for nmc, ncols in [("A", 16), ("Dsk", 16), ("bdt", 16), ("cvb", 16),
                           ("lng", 16), ("lnb", 16), ("g1", 8), ("bt1", 8),
                           ("b1c", 32), ("g2", 32), ("bt2", 32), ("b2c", 8),
                           ("Wc", 8), ("Wb", 8), ("W2c", 8), ("W2b", 8),
                           ("m1", 1), ("m2", 1), ("hm1", 1), ("hm2", 1),
                           ("hm1n", 1), ("hm2n", 1), ("Wcn", 8), ("W2cn", 8),
                           ("W2bb", 8), ("l1r", 8)]:
            VOFF[nmc] = _o
            _o += ncols
        assert _o <= 400

        def VC(nmc, n=1, j=0):
            o = VOFF[nmc] + j
            return vcol[:, o:o + n]

        stage_specs = [(conv_b, 16, "cvb"), (b_dt, 16, "bdt"), (ln_g, 16, "lng"),
                       (ln_b, 16, "lnb"), (ln1_g, 8, "g1"), (ln1_b, 8, "bt1"),
                       (b1, 32, "b1c"), (ln2_g, 32, "g2"), (ln2_b, 32, "bt2"),
                       (b2, 8, "b2c")]
        o_st = 128 * len(stage_specs)

        for i, (ap_, rows, _) in enumerate(stage_specs):
            dma("sync", AR[0:rows, i * 128:i * 128 + 128], ap_[:, :])
        dma("sync", AR[0:1, o_st:o_st + 16], A_log[:, :])
        dma("sync", AR[0:1, o_st + 16:o_st + 32], D_skip[:, :])
        dma("sync", cw[:, :], conv_w[:, :])
        dma("gpsimd", wdtb[:, :], W_dt[:, :], sem="qg")

        @pr.on("gpsimd")
        def _(e):
            e.memset(ones1[:], 1.0).then_inc(gp, 1)
            e.memset(ck[:, 0:1], LN_EPS).then_inc(gp, 1)
            e.memset(ck[:, 1:2], 1.0).then_inc(gp, 1)
            e.memset(onesf[:], 1.0).then_inc(gp, 1)
            e.memset(ones1b[:], 1.0).then_inc(gp, 1)
            e.memset(onescb[:], 1.0).then_inc(gp, 1)
            e.memset(cmask[0:128, 0:G].rearrange("p (a b) -> p a b", b=CHUNK)[:, :, 1:CHUNK],
                     1.0).then_inc(gp, 1)
            e.memset(cmask[:, 0:G:CHUNK], 0.0).then_inc(gp, 1)
            e.memset(carry[:], 0.0).then_inc(gp, 1)
            e.iota(AR[:, o_st + 128:o_st + 256].bitcast(I32),
                   pattern=[[-1, 128]], base=0, channel_multiplier=1).then_inc(gp, 1)
        add("gp", 10)

        gpv = get("gp")

        @pr.on("vector")
        def _(e, gpv=gpv):
            e.wait_ge(gp, gpv)
            e.tensor_single_scalar(identf[:], AR[:, o_st + 128:o_st + 256].bitcast(I32),
                                   0, op=ALU.is_equal)
            e.drain()
            e.tensor_copy(identb[:], identf[:])
            e.drain().then_inc(ev, 1)
        add("ev")

        qsv = get("qs")

        @pr.on("scalar")
        def _(e, qsv=qsv):
            e.wait_ge(qs, qsv)
            e.activation(AR[0:1, o_st + 32:o_st + 48], AR[0:1, o_st:o_st + 16],
                         AF.Exp).then_inc(sp, 1)
        add("sp")

        spv = get("sp")

        @pr.on("vector")
        def _(e, spv=spv):
            e.wait_ge(sp, spv)
            e.tensor_scalar_mul(AR[0:1, o_st + 32:o_st + 48], AR[0:1, o_st + 32:o_st + 48], -1.0)
            e.drain().then_inc(ev, 1)
        add("ev")

        evv = get("ev")
        mmv0 = get("mm")

        @pr.on("tensor")
        def _(e, evv=evv, mmv0=mmv0, qsv=get("qs")):
            e.wait_ge(ev, evv)
            e.wait_ge(qs, qsv)
            for i, (_a, rows, _n) in enumerate(stage_specs):
                if i > 0:
                    e.wait_ge(ev, evv + i)
                e.transpose(pG[:, 0:rows], AR[0:rows, i * 128:i * 128 + 128],
                            identf[0:rows, 0:rows])
                e.drain().then_inc(mm, 1)
            e.wait_ge(ev, evv + len(stage_specs))
            e.matmul(pH[:, 0:16], ones1[:], AR[0:1, o_st + 32:o_st + 48], start=True, stop=True)
            e.matmul(pH[:, 16:32], ones1[:], AR[0:1, o_st + 16:o_st + 32], start=True, stop=True)
            e.drain().then_inc(mm, 1)
        add("mm", len(stage_specs) + 1)

        @pr.on("vector")
        def _(e, mmv0=mmv0, specs=tuple(stage_specs)):
            for i, (_a, rows, nmc) in enumerate(specs):
                e.wait_ge(mm, mmv0 + i + 1)
                e.tensor_copy(vcol[:, VOFF[nmc]:VOFF[nmc] + rows], pG[:, 0:rows])
                e.drain().then_inc(ev, 1)
            e.wait_ge(mm, mmv0 + len(specs) + 1)
            e.tensor_copy(VC("A", 16), pH[:, 0:16])
            e.tensor_copy(VC("Dsk", 16), pH[:, 16:32])
            e.drain().then_inc(ev, 1)
        add("ev", len(stage_specs) + 1)

        barrier()

        QW, QSUM = 0, 8192
        wtiles = [(w1T, k, 4096, k) for k in range(8)] + \
                 [(w2T, k, 1024, 8 + k) for k in range(32)]
        for i, (wt, k, wid, col) in enumerate(wtiles):
            par = i % 2
            if i >= 1:
                evv = get("ev")

                @pr.on("sync")
                def _(e, evv=evv):
                    e.wait_ge(ev, evv)
            dma("sync", AR[:, QW + par * 4096:QW + par * 4096 + wid],
                wt[k * 128:(k + 1) * 128, :])
            qsv = get("qs")

            @pr.on("vector")
            def _(e, par=par, wid=wid, col=col, qsv=qsv):
                e.wait_ge(qs, qsv)
                e.tensor_reduce(AR[:, QSUM + col:QSUM + col + 1],
                                AR[:, QW + par * 4096:QW + par * 4096 + wid],
                                axis=mybir.AxisListType.X, op=ALU.add,
                                apply_absolute_value=True)
                e.drain().then_inc(ev, 1)
            add("ev")

        evv = get("ev")

        @pr.on("vector")
        def _(e, evv=evv):
            e.wait_ge(ev, evv)
            e.tensor_reduce(AR[:, QSUM + 40:QSUM + 41], AR[:, QSUM:QSUM + 8],
                            axis=mybir.AxisListType.X, op=ALU.add)
            e.tensor_reduce(AR[:, QSUM + 41:QSUM + 42], AR[:, QSUM + 8:QSUM + 40],
                            axis=mybir.AxisListType.X, op=ALU.add)
            e.drain().then_inc(ev, 1)
        add("ev")
        evv = get("ev")

        @pr.on("tensor")
        def _(e, evv=evv):
            e.wait_ge(ev, evv)
            e.matmul(pG[0:1, 0:2], onesf[:], AR[:, QSUM + 40:QSUM + 42],
                     start=True, stop=True)
            e.drain().then_inc(mm, 1)
        add("mm")
        mmv = get("mm")

        @pr.on("vector")
        def _(e, mmv=mmv):
            e.wait_ge(mm, mmv)
            R = AR[0:1, QSUM + 44:QSUM + 56]
            e.tensor_scalar(R[:, 0:1], pG[0:1, 0:1], 1.0 / (FH * D), 1e-5,
                            op0=ALU.mult, op1=ALU.max)
            e.tensor_scalar(R[:, 1:2], pG[0:1, 1:2], 1.0 / (FH * D), 1e-5,
                            op0=ALU.mult, op1=ALU.max)
            e.drain()
            e.tensor_scalar_mul(R[:, 2:3], R[:, 0:1], 0.5)
            e.tensor_scalar_mul(R[:, 3:4], R[:, 1:2], 0.5)
            e.tensor_scalar_mul(R[:, 4:5], R[:, 0:1], -0.5)
            e.tensor_scalar_mul(R[:, 5:6], R[:, 1:2], -0.5)
            e.drain().then_inc(ev, 1)
        add("ev")
        evv = get("ev")

        @pr.on("tensor")
        def _(e, evv=evv):
            e.wait_ge(ev, evv)
            e.matmul(pG[:, 8:14], ones1[:], AR[0:1, QSUM + 44:QSUM + 50],
                     start=True, stop=True)
            e.drain().then_inc(mm, 1)
        add("mm")
        mmv = get("mm")

        @pr.on("vector")
        def _(e, mmv=mmv):
            e.wait_ge(mm, mmv)
            for j, nmc in enumerate(["m1", "m2", "hm1", "hm2", "hm1n", "hm2n"]):
                e.tensor_copy(VC(nmc), pG[:, 8 + j:9 + j])
            e.drain().then_inc(ev, 1)
        add("ev")

        barrier()

        XTM, XSQ1, XNB1, RWS = 0, 2048, 4096, 5120
        ev_n1 = {}
        sp_sq1 = {}
        for tt in range(TOT // 128):
            p1 = tt % 2
            psem = "qs" if p1 == 0 else "qc"
            slot = XTM + p1 * 1024
            sqs = XSQ1 + p1 * 1024
            nsl = XNB1 + p1 * 512
            R0 = RWS + p1 * 16

            lsem = "qg" if p1 == 0 else "qh"

            @pr.on("gpsimd")
            def _(e, gate=ev_n1.get(tt - 2, 0)):
                if gate:
                    e.wait_ge(ev, gate)
            dma("gpsimd", AR[:, slot:slot + 1024],
                xs[tt * 128:(tt + 1) * 128, :], sem=lsem)
            qlv = get(lsem)

            @pr.on("scalar")
            def _(e, tt=tt, qlv=qlv, qsm=SEM[lsem], slot=slot, sqs=sqs, R0=R0,
                  gate=ev_n1.get(tt - 2, 0)):
                e.wait_ge(qsm, qlv)
                if gate:
                    e.wait_ge(ev, gate)
                XT = AR[:, slot:slot + 1024]
                e.activation(AR[:, sqs:sqs + 1024], XT, AF.Copy,
                             accum_out=AR[:, R0:R0 + 1])
                e.drain()
                e.activation(AR[:, sqs:sqs + 1024], XT, AF.Square,
                             accum_out=AR[:, R0 + 1:R0 + 2])
                e.drain().then_inc(sp, 1)
            add("sp")
            sp_sq1[tt] = get("sp")

            @pr.on("vector")
            def _(e, tt=tt, spv=sp_sq1[tt], R0=R0):
                e.wait_ge(sp, spv)
                MU = AR[:, R0:R0 + 1]
                S2 = AR[:, R0 + 1:R0 + 2]
                mu = AR[:, R0 + 2:R0 + 3]
                va = AR[:, R0 + 3:R0 + 4]
                e.tensor_scalar_mul(mu, MU, 1.0 / D)
                e.drain()
                e.tensor_scalar_mul(va, S2, 1.0 / D)
                e.drain()
                mus = AR[:, R0 + 6:R0 + 7]
                e.tensor_mul(mus, mu, mu)
                e.drain()
                e.tensor_sub(va, va, mus)
                e.drain().then_inc(ev, 1)
            add("ev")
            evv = get("ev")

            @pr.on("scalar")
            def _(e, evv=evv, R0=R0):
                e.wait_ge(ev, evv)
                e.activation(AR[:, R0 + 4:R0 + 5], AR[:, R0 + 3:R0 + 4],
                             AF.Sqrt, bias=LN_EPS)
                e.drain().then_inc(sp, 1)
            add("sp")
            spv = get("sp")
            qfv = get(psem)

            @pr.on("vector")
            def _(e, tt=tt, spv=spv, R0=R0, slot=slot, nsl=nsl, qfv=qfv,
                  qsm=SEM[psem]):
                e.wait_ge(sp, spv)
                e.wait_ge(qsm, qfv)
                rs = AR[:, R0 + 5:R0 + 6]
                e.reciprocal(rs, AR[:, R0 + 4:R0 + 5])
                e.drain()
                e.tensor_scalar(ARB[:, 2 * nsl:2 * nsl + 1024],
                                AR[:, slot:slot + 1024],
                                AR[:, R0 + 2:R0 + 3], rs,
                                op0=ALU.subtract, op1=ALU.mult)
                e.drain().then_inc(ev, 1)
            add("ev")
            ev_n1[tt] = get("ev")
            evv = get("ev")

            @pr.on("sync")
            def _(e, evv=evv):
                e.wait_ge(ev, evv)
            dma("sync", sp_xn[tt * 128:(tt + 1) * 128, :],
                ARB[:, 2 * nsl:2 * nsl + 1024], sem=psem)

        barrier()

        WB1, XNBG, XEW, XTL, CVT, XCS, ZS, XCB, XPWB = (
            0, 16384, 18432, 18952, 19016, 19528, 20040, 20552, 24648)

        for k in range(8):
            dma("gpsimd", ARB[:, 2 * WB1 + k * 4096:2 * WB1 + (k + 1) * 4096],
                W_in[k * 128:(k + 1) * 128, :], sem="qg")
        dma("gpsimd", ARB[:, 2 * XPWB:2 * XPWB + 768].rearrange("p (k c) -> p k c", c=48),
            W_xp.rearrange("(k p) c -> p k c", p=128), sem="qg")

        @pr.on("gpsimd")
        def _(e):
            e.memset(AR[:, XTL:XTL + 64], 0.0).then_inc(gp, 1)
        add("gp")

        def WB1t(k, mt):
            o = 2 * WB1 + k * 4096 + mt * 128
            return ARB[:, o:o + 128]

        def XCBt(ct):
            o = 2 * XCB + ct * 512
            return ARB[:, o:o + 512]

        qgv_w = get("qg")
        gpv_t = get("gp")

        QWAq, QPq, QNq, WQSq = 28672, 32768, 36864, 40960
        ev_q6 = {}
        evq6 = get("ev")

        def emit_quant1(k):
            @pr.on("sync")
            def _(e, evv=ev_q6.get(k - 1, evq6)):
                e.wait_ge(ev, evv)
            dma("sync", AR[:, QWAq:QWAq + 4096], w1T[k * 128:(k + 1) * 128, :],
                sem="ql")
            qsv = get("ql")

            @pr.on("vector")
            def _(e, k=k, qsv=qsv):
                e.wait_ge(ql, qsv)
                e.tensor_scalar(AR[:, QPq:QPq + 4096], AR[:, QWAq:QWAq + 4096],
                                VC("hm1"), VC("m1"), op0=ALU.is_gt, op1=ALU.mult)
                e.tensor_scalar(AR[:, QNq:QNq + 4096], AR[:, QWAq:QWAq + 4096],
                                VC("hm1n"), VC("m1"), op0=ALU.is_lt, op1=ALU.mult)
                e.drain()
                e.tensor_tensor(ARB[:, 2 * WQSq:2 * WQSq + 4096],
                                AR[:, QPq:QPq + 4096], AR[:, QNq:QNq + 4096],
                                op=ALU.subtract)
                e.drain().then_inc(ev, 1)
            add("ev")
            ev_q6[k] = get("ev")
            evv = get("ev")

            @pr.on("sync")
            def _(e, evv=evv):
                e.wait_ge(ev, evv)
            dma("sync", wq1d[k, :, :], ARB[:, 2 * WQSq:2 * WQSq + 4096], sem="ql")

        CVT2, XCS2b, ZS2b = 25088, 25600, 26112
        XNBG2 = 26624
        ev_psfree = {}
        ev_zpsfree = {}
        sp_silu = {}
        sp_zsilu = {}
        qfull23 = {}
        mm_glast = {}

        for g in range(NG):
            gown = g >= 2
            XNBp = XNBG if g % 2 == 0 else XNBG2
            tsem = "qt" if g % 2 == 0 else "yr"

            @pr.on("sync")
            def _(e, gate=mm_glast.get(g - 2, 0)):
                if gate:
                    e.wait_ge(mm, gate)
            for f in range(8):
                dmat(ARB[:, 2 * XNBp + f * 512:2 * XNBp + (f + 1) * 512],
                     sp_xn[g * G:(g + 1) * G, f * 128:(f + 1) * 128], sem=tsem)
            qgv = get(tsem)

            for ct in range(NCH):
                gct = g * NCH + ct
                p2 = ct % 2
                psem = "qs" if p2 == 0 else "qc"
                CVp = CVT if p2 == 0 else CVT2
                XCp = XCS if p2 == 0 else XCS2b
                ZSp = ZS if p2 == 0 else ZS2b
                @pr.on("tensor")
                def _(e, ct=ct, qgv=qgv, evv=ev_psfree.get(gct - 2, 0), gpv_t=gpv_t,
                      XNBp=XNBp, tsm=SEM[tsem]):
                    e.wait_ge(qg, qgv_w)
                    if ct == 0:
                        e.wait_ge(tsm, qgv)
                    e.wait_ge(gp, gpv_t)
                    if evv:
                        e.wait_ge(ev, evv)
                    pb = pA if ct % 2 == 0 else pB
                    for k in range(8):
                        e.matmul(pb[:, 0:512], WB1t(k, ct),
                                 ARB[:, 2 * XNBp + k * 512:2 * XNBp + (k + 1) * 512],
                                 start=(k == 0), stop=(k == 7))
                    e.drain().then_inc(mm, 1)
                add("mm")
                mmv = get("mm")

                @pr.on("vector")
                def _(e, ct=ct, mmv=mmv, spcv=sp_silu.get(gct - 2, 0), CVp=CVp):
                    e.wait_ge(mm, mmv)
                    if spcv:
                        e.wait_ge(sp, spcv)
                    pb = pA if ct % 2 == 0 else pB
                    XEWv = AR[:, XEW:XEW + 520]
                    e.tensor_copy(XEWv[:, 3:515], pb[:, 0:512])
                    e.drain().then_inc(ev, 1)
                    e.tensor_copy(XEWv[:, 0:3], AR[:, XTL + ct * 4:XTL + ct * 4 + 3])
                    e.drain()
                    CV = AR[:, CVp:CVp + 512]
                    e.tensor_scalar_mul(CV, XEWv[:, 0:512], cw[:, ct * 4:ct * 4 + 1])
                    e.drain()
                    e.scalar_tensor_tensor(CV, XEWv[:, 1:513], cw[:, ct * 4 + 1:ct * 4 + 2],
                                           CV, op0=ALU.mult, op1=ALU.add)
                    e.drain()
                    e.scalar_tensor_tensor(CV, XEWv[:, 2:514], cw[:, ct * 4 + 2:ct * 4 + 3],
                                           CV, op0=ALU.mult, op1=ALU.add)
                    e.drain()
                    e.scalar_tensor_tensor(CV, XEWv[:, 3:515], cw[:, ct * 4 + 3:ct * 4 + 4],
                                           CV, op0=ALU.mult, op1=ALU.add)
                    e.tensor_copy(AR[:, XTL + ct * 4:XTL + ct * 4 + 3], XEWv[:, 512:515])
                    e.drain().then_inc(ev, 1)
                add("ev", 2)
                ev_psfree[gct] = get("ev") - 1
                evv = get("ev")

                qsv_x = (get(psem) if ct < 2 else qfull23.get(gct - 2, 0))

                @pr.on("scalar")
                def _(e, ct=ct, evv=evv, qsv_x=qsv_x,
                      qsm=SEM[psem], CVp=CVp, XCp=XCp):
                    e.wait_ge(ev, evv)
                    if qsv_x:
                        e.wait_ge(qsm, qsv_x)
                    e.activation(AR[:, XCp:XCp + 512], AR[:, CVp:CVp + 512], AF.Silu,
                                 bias=VC("cvb", 1, ct))
                    e.drain()
                    e.copy(XCBt(ct), AR[:, XCp:XCp + 512]).then_inc(sp, 1)
                add("sp")
                sp_silu[gct] = get("sp")
                spv = get("sp")

                @pr.on("sync")
                def _(e, spv=spv):
                    e.wait_ge(sp, spv)
                dma("sync", sp_xc[ct, :, g * G:(g + 1) * G], AR[:, XCp:XCp + 512],
                    sem=psem)

                if gown:
                    @pr.on("tensor")
                    def _(e, ct=ct, spz=sp_zsilu.get(gct - 2, 0), XNBp=XNBp):
                        if spz:
                            e.wait_ge(sp, spz)
                        pz = pC if ct % 2 == 0 else pD
                        for k in range(8):
                            e.matmul(pz[:, 0:512], WB1t(k, 16 + ct),
                                     ARB[:, 2 * XNBp + k * 512:2 * XNBp + (k + 1) * 512],
                                     start=(k == 0), stop=(k == 7))
                        e.drain().then_inc(mm, 1)
                    add("mm")
                    mmv = get("mm")

                    qsv_z = (get(psem) if ct < 2 else qfull23.get(gct - 2, 0))

                    @pr.on("scalar")
                    def _(e, ct=ct, mmv=mmv, qsv=qsv_z,
                          qsm=SEM[psem], ZSp=ZSp):
                        e.wait_ge(mm, mmv)
                        if qsv:
                            e.wait_ge(qsm, qsv)
                        pz = pC if ct % 2 == 0 else pD
                        e.activation(AR[:, ZSp:ZSp + 512], pz[:, 0:512],
                                     AF.Silu).then_inc(sp, 1)
                        e.drain()
                    add("sp")
                    sp_zsilu[gct] = get("sp")
                    spv = get("sp")

                    @pr.on("sync")
                    def _(e, spv=spv):
                        e.wait_ge(sp, spv)
                    dma("sync", sp_z[ct, :, (g - 2) * G:(g - 1) * G], AR[:, ZSp:ZSp + 512],
                        sem=psem)
                qfull23[gct] = get(psem)

            mm_glast[g] = get("mm")
            for kq in ([2 * (g - 1), 2 * (g - 1) + 1] if 1 <= g <= 4 else []):
                emit_quant1(kq)
            spv = get("sp")

            @pr.on("tensor")
            def _(e, g=g, spv=spv):
                e.wait_ge(sp, spv)
                for p_i, pz in enumerate((pE, pF, pG)):
                    for k in range(NCH):
                        e.matmul(pz[0:16, 0:512],
                                 ARB[:, 2 * XPWB + k * 48 + p_i * 16:2 * XPWB + k * 48 + (p_i + 1) * 16],
                                 XCBt(k), start=(k == 0), stop=(k == NCH - 1))
                e.drain().then_inc(mm, 1)
            add("mm")
            mmv = get("mm")

            @pr.on("vector")
            def _(e, g=g, mmv=mmv):
                e.wait_ge(mm, mmv)
                e.tensor_copy(projp[0:16, g * G:(g + 1) * G], pE[0:16, 0:512])
                e.tensor_copy(projp[32:48, g * G:(g + 1) * G], pF[0:16, 0:512])
                e.tensor_copy(projp[64:80, g * G:(g + 1) * G], pG[0:16, 0:512])
                e.drain().then_inc(ev, 1)
            add("ev")

        barrier()

        qlv_q = get("ql")

        UBB, BREP, CREP, SALL, DECR = 0, 4096, 8192, 12288, 20480
        XCS2, DTH4, CUMD, U4 = 36864, 37888, 38912, 39936
        YA4, ZS4, Y24, DTBG, DC0, Y2B, SQ2 = (40960, 41472, 42496, 43008, 43264,
                                              43280, 43792)

        TM4 = DECR + 8192

        def ubbv(p=0):
            return AR[:, DECR:DECR + 8192].rearrange("p (n t) -> p n t", n=16)

        def sallv():
            return AR[:, SALL:SALL + 8192].rearrange("p (n t) -> p n t", n=16)

        F4XN, F4XT, F4SH, F4HB = 256, 2304, 2816, 3840
        mm_tr4 = {}
        ev_n4 = {}
        ev_ev4 = {}
        evfb4 = {}
        sp_st4 = {}
        sp_gel4 = {}
        mm_w14 = {}
        qm_t4 = {}
        qi_s4 = {}
        mm_w1l4 = {}

        @pr.on("vector")
        def _(e, qlv_q=qlv_q):
            e.wait_ge(ql, qlv_q)

        ev_u = {}
        ev_scan = {}
        ev_y2 = {}
        gp_pool = {}
        gp_mult = {}
        sp_soft = {}
        sp_dec = {}
        qs_y2sp = {}
        mm_st = {}

        for tb in range(NG):
            town = tb >= 2
            ts0 = tb * G
            mmv = get("mm")
            evfl = get("ev")
            gpfl = get("gp")
            qsfl = get("qs")

            qtfl = get("qt")

            @pr.on("gpsimd")
            def _(e, mmv=mmv, evfl=evfl, qtfl=qtfl, gpfl=gpfl):
                e.wait_ge(mm, mmv)
                e.wait_ge(ev, evfl)
                e.wait_ge(gp, gpfl)
                e.wait_ge(qt, qtfl)
            dma("gpsimd", ARB[0:1, 2 * DECR:2 * DECR + 8192],
                projp[32:48, ts0:ts0 + G], sem="qg")
            if town:
                dma("gpsimd", ARB[0:1, 2 * DECR + 8192:2 * DECR + 16384],
                    projp[64:80, ts0:ts0 + G], sem="qg")
            qgv = get("qg")
            evv = get("ev")

            @pr.on("tensor")
            def _(e, qgv=qgv, evv=evv):
                e.wait_ge(qg, qgv)
                for n in range(16):
                    if evv + n - 1 > 0:
                        e.wait_ge(ev, evv + n - 1)
                    pb = pE if n % 2 == 0 else pF
                    e.matmul(pb[:, 0:512], ones1b[:],
                             ARB[0:1, 2 * DECR + n * 512:2 * DECR + (n + 1) * 512],
                             start=True, stop=True)
                    e.drain().then_inc(mm, 1)
            add("mm", 16)
            mmv = get("mm") - 16

            @pr.on("vector")
            def _(e, mmv=mmv):
                for n in range(16):
                    e.wait_ge(mm, mmv + n + 1)
                    pb = pE if n % 2 == 0 else pF
                    e.tensor_copy(ARB[:, 2 * BREP + n * 512:2 * BREP + (n + 1) * 512],
                                  pb[:, 0:512])
                    e.drain().then_inc(ev, 1)
            add("ev", 16)
            if town:
                evv = get("ev")

                @pr.on("tensor")
                def _(e, evv=evv):
                    for n in range(16):
                        e.wait_ge(ev, evv + n - 1)
                        pb = pE if n % 2 == 0 else pF
                        e.matmul(pb[:, 0:512], ones1b[:],
                                 ARB[0:1, 2 * DECR + 8192 + n * 512:2 * DECR + 8192 + (n + 1) * 512],
                                 start=True, stop=True)
                        e.drain().then_inc(mm, 1)
                add("mm", 16)
                mmv = get("mm") - 16

                @pr.on("vector")
                def _(e, mmv=mmv):
                    for n in range(16):
                        e.wait_ge(mm, mmv + n + 1)
                        pb = pE if n % 2 == 0 else pF
                        e.tensor_copy(ARB[:, 2 * CREP + n * 512:2 * CREP + (n + 1) * 512],
                                      pb[:, 0:512])
                        e.drain().then_inc(ev, 1)
                add("ev", 16)

            evv = get("ev")
            mmdt = get("mm")

            @pr.on("scalar")
            def _(e, ts0=ts0, evv=evv, mmdt=mmdt):
                e.wait_ge(ev, evv)
                e.wait_ge(mm, mmdt)
                e.copy(ARB[0:16, 2 * DTBG:2 * DTBG + 512],
                       projp[0:16, ts0:ts0 + G]).then_inc(sp, 1)
            add("sp")
            spdtb = get("sp")

            def emit_front(ct):
                gidx = tb * NCH + ct
                p = ct % 2
                qsem = "qs" if p == 0 else "qc"
                gate = ev_y2.get(gidx - 2, ev_u.get(gidx - 2, 0))

                @pr.on("sync")
                def _(e, gate=gate):
                    if gate:
                        e.wait_ge(ev, gate)
                dma("sync", AR[:, XCS2 + p * 512:XCS2 + (p + 1) * 512],
                    sp_xc[ct, :, ts0:ts0 + G], sem=qsem)
                if town:
                    dma("sync", AR[:, ZS4 + p * 512:ZS4 + (p + 1) * 512],
                        sp_z[ct, :, ts0 - WIN:ts0 - WIN + G], sem=qsem)
                qsv = get(qsem)

                spg = sp_soft.get(gidx - 2, spdtb)

                @pr.on("tensor")
                def _(e, ct=ct, p=p, spg=max(spg, spdtb)):
                    e.wait_ge(sp, spg)
                    pb = pA if p == 0 else pB
                    e.matmul(pb[:, 0:512], wdtb[0:16, ct * 128:(ct + 1) * 128],
                             ARB[0:16, 2 * DTBG:2 * DTBG + 512],
                             start=True, stop=True)
                    e.drain().then_inc(mm, 1)
                add("mm")
                mmv = get("mm")

                gu = ev_scan.get(gidx - 2, ev_u.get(gidx - 2, 0))

                @pr.on("scalar")
                def _(e, ct=ct, p=p, mmv=mmv, gu=gu):
                    e.wait_ge(mm, mmv)
                    if gu:
                        e.wait_ge(ev, gu)
                    pb = pA if p == 0 else pB
                    T = AR[:, CUMD + p * 512:CUMD + (p + 1) * 512]
                    e.activation(T, pb[:, 0:512], AF.Exp, bias=VC("bdt", 1, ct))
                    e.drain()
                    e.activation(AR[:, DTH4 + p * 512:DTH4 + (p + 1) * 512], T,
                                 AF.Ln, bias=1.0).then_inc(sp, 1)
                add("sp")
                sp_soft[gidx] = get("sp")

                @pr.on("vector")
                def _(e, ct=ct, p=p, spv=sp_soft[gidx]):
                    e.wait_ge(sp, spv)
                    e.tensor_tensor_scan(AR[:, CUMD + p * 512:CUMD + (p + 1) * 512],
                                         cmask[:, :],
                                         AR[:, DTH4 + p * 512:DTH4 + (p + 1) * 512],
                                         0.0, op0=ALU.mult, op1=ALU.add)
                    e.drain().then_inc(ev, 1)
                add("ev")
                evcum = get("ev")

                @pr.on("scalar")
                def _(e, ct=ct, p=p, evv=evcum):
                    e.wait_ge(ev, evv)
                    T = AR[:, CUMD + p * 512:CUMD + (p + 1) * 512]
                    e.activation(T, T, AF.Exp, scale=VC("A", 1, ct)).then_inc(sp, 1)
                add("sp")
                sp_dec[gidx] = get("sp")

                @pr.on("vector")
                def _(e, ct=ct, p=p, spd=sp_dec[gidx], sps=sp_soft[gidx],
                      qsv=qsv, qsm=SEM[qsem]):
                    e.wait_ge(sp, sps)
                    e.wait_ge(qsm, qsv)
                    e.tensor_mul(AR[:, U4 + p * 512:U4 + (p + 1) * 512],
                                 AR[:, DTH4 + p * 512:DTH4 + (p + 1) * 512],
                                 AR[:, XCS2 + p * 512:XCS2 + (p + 1) * 512])
                    e.drain()
                    e.tensor_tensor(ubbv(p),
                                    AR[:, U4 + p * 512:U4 + (p + 1) * 512]
                                    .unsqueeze(1).broadcast_to((128, 16, 512)),
                                    ARB[:, 2 * BREP:2 * BREP + 8192]
                                    .rearrange("p (n t) -> p n t", n=16), op=ALU.mult)
                    e.drain()
                    e.wait_ge(sp, spd)
                    T = AR[:, CUMD + p * 512:CUMD + (p + 1) * 512]
                    e.tensor_copy(AR[:, DC0 + p * 8:DC0 + p * 8 + 1], T[:, 0:1])
                    e.drain()
                    e.memset(T[:, 0:1], 0.0)
                    e.drain()
                    e.scalar_tensor_tensor(ubbv(p)[:, :, 0], carry[:, ct * 16:ct * 16 + 16],
                                           AR[:, DC0 + p * 8:DC0 + p * 8 + 1],
                                           ubbv(p)[:, :, 0], op0=ALU.mult, op1=ALU.add)
                    e.drain().then_inc(ev, 1)
                add("ev")
                ev_u[gidx] = get("ev")

                @pr.on("vector")
                def _(e, ct=ct, p=p):
                    T = AR[:, CUMD + p * 512:CUMD + (p + 1) * 512]
                    for n in range(16):
                        e.tensor_tensor_scan(AR[:, SALL + n * 512:SALL + (n + 1) * 512],
                                             T,
                                             AR[:, DECR + n * 512:DECR + (n + 1) * 512],
                                             0.0, op0=ALU.mult, op1=ALU.add)
                    e.drain()
                    e.tensor_copy(carry[:, ct * 16:ct * 16 + 16], sallv()[:, :, 511])
                    e.drain().then_inc(ev, 1)
                add("ev")
                ev_scan[gidx] = get("ev")

                if town:
                    qy = get(qsem)

                    @pr.on("vector")
                    def _(e, ct=ct, p=p, qy=qy, qsm=SEM[qsem],
                          mmg=mm_st.get(gidx - 2, 0)):
                        TMv = AR[:, TM4:TM4 + 8192]
                        e.tensor_mul(TMv, AR[:, SALL:SALL + 8192],
                                     ARB[:, 2 * CREP:2 * CREP + 8192])
                        e.drain()
                        e.tensor_add(TMv[:, 0:4096], TMv[:, 0:4096], TMv[:, 4096:8192])
                        e.drain()
                        e.tensor_add(TMv[:, 0:2048], TMv[:, 0:2048], TMv[:, 2048:4096])
                        e.drain()
                        e.tensor_add(TMv[:, 0:1024], TMv[:, 0:1024], TMv[:, 1024:2048])
                        e.drain()
                        e.tensor_add(AR[:, YA4:YA4 + 512], TMv[:, 0:512], TMv[:, 512:1024])
                        e.drain()
                        Y2v = AR[:, Y24:Y24 + 512]
                        e.scalar_tensor_tensor(Y2v, AR[:, XCS2 + p * 512:XCS2 + (p + 1) * 512],
                                               VC("Dsk", 1, ct), AR[:, YA4:YA4 + 512],
                                               op0=ALU.mult, op1=ALU.add)
                        e.drain()
                        e.tensor_mul(Y2v, Y2v, AR[:, ZS4 + p * 512:ZS4 + (p + 1) * 512])
                        e.drain()
                        e.wait_ge(qsm, qy)
                        e.tensor_copy(ARB[:, 2 * (Y2B + p * 256):2 * (Y2B + p * 256) + 512], Y2v)
                        e.drain()
                        if mmg:
                            e.wait_ge(mm, mmg)
                        SQp = SQ2 if p == 0 else 0
                        e.tensor_mul(ARB[:, 2 * SQp:2 * SQp + 512], Y2v, Y2v)
                        e.drain().then_inc(ev, 1)
                    add("ev")
                ev_y2[gidx] = get("ev")

            def emit_tail(ct):
                gidx = tb * NCH + ct
                p = ct % 2
                qsem = "qs" if p == 0 else "qc"
                if town:
                    @pr.on("tensor")
                    def _(e, ct=ct, p=p, spv=ev_y2[gidx]):
                        e.wait_ge(ev, spv)
                        SQp = SQ2 if p == 0 else 0
                        e.matmul(pC[0:1, 0:512], onescb[:],
                                 ARB[:, 2 * (Y2B + p * 256):2 * (Y2B + p * 256) + 512],
                                 start=(ct == 0), stop=(ct == NCH - 1))
                        e.matmul(pD[0:1, 0:512], onescb[:], ARB[:, 2 * SQp:2 * SQp + 512],
                                 start=(ct == 0), stop=(ct == NCH - 1))
                        e.drain().then_inc(mm, 1)
                    add("mm")
                    mm_st[gidx] = get("mm")

                    @pr.on("sync")
                    def _(e, mmv=mm_st[gidx]):
                        e.wait_ge(mm, mmv)
                    dma("sync", sp_y2[ct, :, ts0 - WIN:ts0 - WIN + G],
                        ARB[:, 2 * (Y2B + p * 256):2 * (Y2B + p * 256) + 512], sem=qsem)
                    qs_y2sp[gidx] = get(qsem)

            def emit_ln1(gi, tt):
                tg = gi * 4 + tt

                @pr.on("sync")
                def _(e, g=mm_tr4.get(tg - 1, 0), tg=tg):
                    if tg == 0:
                        e.wait_ge(ql, qlv_q)
                    if g:
                        e.wait_ge(mm, g)
                dma("sync", ARB[:, 2 * F4XT:2 * F4XT + 1024],
                    sp_xn[WIN + gi * G + tt * 128:WIN + gi * G + (tt + 1) * 128, :],
                    sem="yr")
                yv = get("yr")

                @pr.on("scalar")
                def _(e, yv=yv, tt=tt, mg=mm_w1l4.get(gi - 1, 0)):
                    e.wait_ge(yr, yv)
                    if tt == 0 and mg:
                        e.wait_ge(mm, mg)
                    XT = ARB[:, 2 * F4XT:2 * F4XT + 1024]
                    e.activation(AR[:, F4SH:F4SH + 1024], XT, AF.Copy,
                                 accum_out=VC("l1r", 1, 0))
                    e.drain()
                    e.activation(AR[:, F4SH:F4SH + 1024], XT, AF.Square,
                                 accum_out=VC("l1r", 1, 1))
                    e.drain()
                    e.activation(VC("l1r", 1, 2), VC("l1r", 1, 0), AF.Copy,
                                 scale=1.0 / D)
                    e.drain()
                    e.activation(VC("l1r", 1, 6), VC("l1r", 1, 2), AF.Square)
                    e.drain()
                    e.activation(VC("l1r", 1, 7), VC("l1r", 1, 6), AF.Copy,
                                 scale=-1.0, bias=LN_EPS)
                    e.drain()
                    e.activation(VC("l1r", 1, 4), VC("l1r", 1, 1), AF.Sqrt,
                                 scale=1.0 / D, bias=VC("l1r", 1, 7))
                    e.drain().then_inc(sp, 1)
                add("sp")
                sp_st4[tg] = get("sp")
                spv4 = get("sp")

                @pr.on("vector")
                def _(e, spv=spv4):
                    e.wait_ge(sp, spv)
                    rs = VC("l1r", 1, 5)
                    e.reciprocal(rs, VC("l1r", 1, 4))
                    e.drain()
                    XT = ARB[:, 2 * F4XT:2 * F4XT + 1024]
                    e.tensor_scalar(XT, XT, VC("l1r", 1, 2), rs,
                                    op0=ALU.subtract, op1=ALU.mult)
                    e.drain().then_inc(ev, 1)
                add("ev")
                ev_n4[tg] = get("ev")
                evb = get("ev")

                @pr.on("tensor")
                def _(e, evn=ev_n4[tg], evb=evb, gpr=ev_ev4.get(tg - 1, 0),
                      spg=sp_gel4.get(gi * 32 + 31, 0) if tt == 0 else 0):
                    e.wait_ge(ev, evn)
                    if gpr:
                        e.wait_ge(ev, gpr)
                    if spg:
                        e.wait_ge(sp, spg)
                    for f in range(8):
                        if evb + f - 1 > 0:
                            e.wait_ge(ev, evb + f - 1)
                        pb = pG if f % 2 == 0 else pH
                        e.transpose(pb[:, 0:64].bitcast(BF16),
                                    ARB[:, 2 * F4XT + f * 128:2 * F4XT + (f + 1) * 128],
                                    identb[:])
                        e.drain().then_inc(mm, 1)
                add("mm", 8)
                mm_tr4[tg] = get("mm")
                mmv4 = get("mm") - 8

                @pr.on("vector")
                def _(e, tt=tt, mmv=mmv4, mg=mm_w1l4.get(gi - 1, 0), tg=tg):
                    if tg % 4 == 0 and mg:
                        e.wait_ge(mm, mg)
                    for f in range(8):
                        e.wait_ge(mm, mmv + f + 1)
                        pb = pG if f % 2 == 0 else pH
                        e.tensor_scalar(
                            ARB[:, 2 * F4XN + f * 512 + tt * 128:2 * F4XN + f * 512 + (tt + 1) * 128],
                            pb[:, 0:64].bitcast(BF16), VC("g1", 1, f),
                            VC("bt1", 1, f), op0=ALU.mult, op1=ALU.add)
                        e.drain().then_inc(ev, 1)
                add("ev", 8)
                ev_ev4[tg] = get("ev")
                if tt == 3:
                    evfb4[gi] = get("ev")

            def emit_mt(gi, mt):
                gmt = gi * 32 + mt

                @pr.on("gpsimd")
                def _(e, mt=mt, spg=sp_st4.get(gi * 4 + 3, 0),
                      mg=mm_w14.get(gmt - 1, 0)):
                    if mt == 0:
                        e.wait_ge(sp, spg)
                    if mg:
                        e.wait_ge(mm, mg)
                dma("gpsimd",
                    ARB[:, 2 * F4SH:2 * F4SH + 1024]
                    .rearrange("p (k c) -> p k c", c=128),
                    wq1d[:, :, mt * 128:(mt + 1) * 128]
                    .rearrange("k p c -> p k c"), sem="qm")
                qm_t4[gmt] = get("qm")

                @pr.on("tensor")
                def _(e, mt=mt, qmv=qm_t4[gmt], evf=evfb4[gi],
                      spfree=sp_gel4.get(gmt - 2, 0)):
                    e.wait_ge(qm, qmv)
                    if mt < 2:
                        e.wait_ge(ev, evf)
                    if spfree:
                        e.wait_ge(sp, spfree)
                    pb = pG if mt % 2 == 0 else pH
                    for k in range(8):
                        e.matmul(pb[:, 0:512],
                                 ARB[:, 2 * F4SH + k * 128:2 * F4SH + (k + 1) * 128],
                                 ARB[:, 2 * F4XN + k * 512:2 * F4XN + (k + 1) * 512],
                                 start=(k == 0), stop=(k == 7))
                    e.drain().then_inc(mm, 1)
                add("mm")
                mm_w14[gmt] = get("mm")
                if mt == 31:
                    mm_w1l4[gi] = get("mm")

                @pr.on("scalar")
                def _(e, mt=mt, mmv=mm_w14[gmt], qiv=qi_s4.get(gmt - 1, 0)):
                    e.wait_ge(mm, mmv)
                    if qiv:
                        e.wait_ge(qi, qiv)
                    pb = pG if mt % 2 == 0 else pH
                    e.activation(ARB[:, 2 * F4HB:2 * F4HB + 512], pb[:, 0:512],
                                 AF.Gelu, bias=VC("b1c", 1, mt))
                    e.drain().then_inc(sp, 1)
                add("sp")
                sp_gel4[gmt] = get("sp")

                @pr.on("gpsimd")
                def _(e, spv=sp_gel4[gmt]):
                    e.wait_ge(sp, spv)
                dma("gpsimd", sp_h[mt, :, gi * G:(gi + 1) * G],
                    ARB[:, 2 * F4HB:2 * F4HB + 512], sem="qi")
                qi_s4[gmt] = get("qi")

            for ctx in range(NCH + 1):
                if ctx < NCH:
                    emit_front(ctx)
                if ctx >= 1:
                    emit_tail(ctx - 1)
                if town and ctx < NCH:
                    gi6 = tb - 2
                    if ctx < 4:
                        emit_ln1(gi6, ctx)
                    else:
                        mtb = (ctx - 4) * 3
                        for j in range(3):
                            if mtb + j < 32:
                                emit_mt(gi6, mtb + j)

            if town:
                mmv = get("mm")
                evv = get("ev")

                @pr.on("vector")
                def _(e, mmv=mmv, evv=evv):
                    e.wait_ge(mm, mmv)
                    e.wait_ge(ev, evv)
                    MU = AR[0:1, DECR:DECR + 512]
                    S2 = AR[0:1, DECR + 512:DECR + 1024]
                    VA = AR[0:1, DECR + 1024:DECR + 1536]
                    e.tensor_scalar_mul(MU, pC[0:1, 0:512], 1.0 / DI)
                    e.tensor_scalar_mul(S2, pD[0:1, 0:512], 1.0 / DI)
                    e.drain()
                    e.tensor_mul(VA, MU, MU)
                    e.drain()
                    e.tensor_sub(VA, S2, VA)
                    e.drain().then_inc(ev, 1)
                add("ev")
                evv = get("ev")

                @pr.on("scalar")
                def _(e, evv=evv):
                    e.wait_ge(ev, evv)
                    e.activation(AR[0:1, DECR + 1536:DECR + 2048],
                                 AR[0:1, DECR + 1024:DECR + 1536], AF.Sqrt,
                                 bias=LN_EPS).then_inc(sp, 1)
                add("sp")
                spv = get("sp")

                @pr.on("vector")
                def _(e, spv=spv):
                    e.wait_ge(sp, spv)
                    e.reciprocal(AR[0:1, DECR + 512:DECR + 1024],
                                 AR[0:1, DECR + 1536:DECR + 2048])
                    e.drain().then_inc(ev, 1)
                add("ev")
                evv = get("ev")

                @pr.on("sync")
                def _(e, evv=evv):
                    e.wait_ge(ev, evv)
                dma("sync", sp_st[0:1, ts0 - WIN:ts0 - WIN + G], AR[0:1, DECR:DECR + 512],
                    sem="qt")
                dma("sync", sp_st[1:2, ts0 - WIN:ts0 - WIN + G],
                    AR[0:1, DECR + 512:DECR + 1024], sem="qt")
            if p4bar:
                barrier()


        WOB, Y2G, MRS, RSR, MIXG, RW5, QW2 = (0, 8192, 12288, 12800, 13312,
                                              17408, 20480)

        evv = get("ev")

        @pr.on("vector")
        def _(e, evv=evv):
            e.wait_ge(ev, evv)
            rv = rb2.rearrange("p (k w) -> p k w", w=16)
            e.memset(rb2[:, :], 0.0)
            e.drain()
            e.tensor_copy(rv[:, 0:16, 0], VC("lng", 16))
            e.tensor_copy(rv[:, 0:16, 1], VC("lnb", 16))
            e.drain().then_inc(ev, 1)
        add("ev")

        qtP5 = get("qt")
        qcP5 = get("qc")
        evP4 = get("ev")
        mmP4 = get("mm")
        qiP4, qmP4, yrP4, spP4 = get("qi"), get("qm"), get("yr"), get("sp")
        QW2B = 18432
        ev_cast5 = {}
        for k in range(NCH):
            QWp = QW2 if k % 2 == 0 else QW2B
            lsem = "qs" if k % 2 == 0 else "qc"

            @pr.on("sync")
            def _(e, evv=ev_cast5.get(k - 2, evP4), qtP5=qtP5, qcP5=qcP5, k=k):
                e.wait_ge(ev, evv)
                if k < 2:
                    e.wait_ge(qt, qtP5)
                    e.wait_ge(qc, qcP5)
            dma("sync", AR[:, QWp:QWp + 1024], W_out[k * 128:(k + 1) * 128, :],
                sem=lsem)
            qsv = get(lsem)

            @pr.on("vector")
            def _(e, k=k, qsv=qsv, QWp=QWp, qsm=SEM[lsem], mmP4=mmP4):
                if k < 2:
                    e.wait_ge(mm, mmP4)
                    e.wait_ge(qi, qiP4)
                    e.wait_ge(qm, qmP4)
                    e.wait_ge(yr, yrP4)
                    e.wait_ge(sp, spP4)
                e.wait_ge(qsm, qsv)
                e.tensor_copy(ARB[:, 2 * WOB + k * 1024:2 * WOB + (k + 1) * 1024],
                              AR[:, QWp:QWp + 1024])
                e.drain().then_inc(ev, 1)
            add("ev")
            ev_cast5[k] = get("ev")
        evv = get("ev")

        evb5 = get("ev")

        @pr.on("tensor")
        def _(e, evv=evv, evb5=evb5):
            e.wait_ge(ev, evv)
            for m in range(8):
                if m >= 2:
                    e.wait_ge(ev, evb5 + m - 1)
                pb = pE if m % 2 == 0 else pF
                for k in range(NCH):
                    e.matmul(pb[0:128, 0:16],
                             ARB[:, 2 * WOB + k * 1024 + m * 128:2 * WOB + k * 1024 + (m + 1) * 128],
                             rb2.rearrange("p (k w) -> p k w", w=16)[:, k, :],
                             start=(k == 0), stop=(k == NCH - 1))
                e.drain().then_inc(mm, 1)
        add("mm", 8)
        mmv = get("mm") - 8

        @pr.on("vector")
        def _(e, mmv=mmv):
            for m in range(8):
                e.wait_ge(mm, mmv + m + 1)
                pb = pE if m % 2 == 0 else pF
                e.tensor_copy(VC("Wc", 1, m), pb[:, 0:1])
                e.tensor_copy(VC("Wb", 1, m), pb[:, 1:2])
                e.drain().then_inc(ev, 1)
            e.tensor_scalar_mul(VC("Wcn", 8), VC("Wc", 8), -1.0)
            e.drain()
            for k in range(NCH):
                e.tensor_scalar_mul(ARB[:, 2 * WOB + k * 1024:2 * WOB + (k + 1) * 1024],
                                    ARB[:, 2 * WOB + k * 1024:2 * WOB + (k + 1) * 1024],
                                    VC("lng", 1, k))
            e.drain().then_inc(ev, 1)
        add("ev", 9)
        if p4bar:
            barrier()

        qcP4 = get("qc")
        qtP4 = get("qt")
        Y2G2 = 21504
        evprep5 = get("ev")
        ts5 = {}
        mm_ml5 = {}

        def y2buf(gi):
            return Y2G if gi % 2 == 0 else Y2G2

        def emit_y2rel(gi, gate_mm):
            tsem = "qt" if gi % 2 == 0 else "yr"
            Yb = y2buf(gi)

            @pr.on("sync")
            def _(e, gate_mm=gate_mm):
                if gate_mm:
                    e.wait_ge(mm, gate_mm)
            for ct in range(NCH):
                dma("sync", ARB[:, 2 * Yb + ct * 512:2 * Yb + (ct + 1) * 512],
                    sp_y2[ct, :, gi * G:(gi + 1) * G], sem=tsem)
            ts5[gi] = get(tsem)

        for gi in range(4):
            ts0 = gi * G
            evv = get("ev")

            @pr.on("sync")
            def _(e, evv=evv, qcP4=qcP4, qtP4=qtP4):
                e.wait_ge(ev, evv)
                e.wait_ge(qc, qcP4)
                e.wait_ge(qt, qtP4)
            if gi == 0:
                emit_y2rel(0, 0)
            dma("sync", AR[0:1, RW5:RW5 + 512], sp_st[0:1, ts0:ts0 + G])
            dma("sync", AR[0:1, RW5 + 512:RW5 + 1024], sp_st[1:2, ts0:ts0 + G])
            qsv = get("qs")

            @pr.on("vector")
            def _(e, qsv=qsv):
                e.wait_ge(qs, qsv)
                e.tensor_mul(AR[0:1, RW5:RW5 + 512], AR[0:1, RW5:RW5 + 512],
                             AR[0:1, RW5 + 512:RW5 + 1024])
                e.drain().then_inc(ev, 1)
            add("ev")
            evv = get("ev")

            @pr.on("tensor")
            def _(e, evv=evv):
                e.wait_ge(ev, evv)
                e.matmul(pF[:, 0:512], ones1[:], AR[0:1, RW5:RW5 + 512],
                         start=True, stop=True)
                e.matmul(pG[:, 0:512], ones1[:], AR[0:1, RW5 + 512:RW5 + 1024],
                         start=True, stop=True)
                e.drain().then_inc(mm, 1)
            add("mm")
            mmv = get("mm")

            @pr.on("vector")
            def _(e, mmv=mmv):
                e.wait_ge(mm, mmv)
                e.tensor_copy(AR[:, MRS:MRS + 512], pF[:, 0:512])
                e.tensor_copy(AR[:, RSR:RSR + 512], pG[:, 0:512])
                e.drain().then_inc(ev, 1)
            add("ev")
            evrep = get("ev")

            ev_fold5 = {}
            Yb5 = y2buf(gi)
            tsm5 = SEM["qt" if gi % 2 == 0 else "yr"]
            PB5 = (pA, pB, pC, pD)
            for m in range(8):
                evv = ev_fold5.get(m - 4, evprep5)

                @pr.on("tensor")
                def _(e, m=m, evv=evv, tsv=ts5[gi], tsm5=tsm5, Yb5=Yb5):
                    if m < 4:
                        e.wait_ge(tsm5, tsv)
                    e.wait_ge(ev, evv)
                    pb = PB5[m % 4]
                    for k in range(NCH):
                        e.matmul(pb[:, 0:512],
                                 ARB[:, 2 * WOB + k * 1024 + m * 128:2 * WOB + k * 1024 + (m + 1) * 128],
                                 ARB[:, 2 * Yb5 + k * 512:2 * Yb5 + (k + 1) * 512],
                                 start=(k == 0), stop=(k == NCH - 1))
                    e.drain().then_inc(mm, 1)
                add("mm")
                mmv = get("mm")

                @pr.on("vector")
                def _(e, m=m, mmv=mmv, evrep=evrep):
                    e.wait_ge(mm, mmv)
                    e.wait_ge(ev, evrep)
                    pb = PB5[m % 4]
                    MX = AR[:, MIXG + m * 512:MIXG + (m + 1) * 512]
                    e.tensor_tensor(MX, pb[:, 0:512], AR[:, RSR:RSR + 512], op=ALU.mult)
                    e.drain()
                    e.scalar_tensor_tensor(MX, AR[:, MRS:MRS + 512], VC("Wcn", 1, m),
                                           MX, op0=ALU.mult, op1=ALU.add)
                    e.drain()
                    e.tensor_scalar_add(MX, MX, VC("Wb", 1, m))
                    e.drain().then_inc(ev, 1)
                add("ev")
                ev_fold5[m] = get("ev")
            mm_ml5[gi] = get("mm")
            if gi < 3:
                emit_y2rel(gi + 1, mm_ml5.get(gi - 1, 0))
            evv = get("ev")

            @pr.on("sync")
            def _(e, evv=evv):
                e.wait_ge(ev, evv)
            for f in range(8):
                dma("sync", sp_mix[f, :, ts0:ts0 + G],
                    AR[:, MIXG + f * 512:MIXG + (f + 1) * 512])
            if p4bar:
                barrier()

        barrier()

        barrier()

        WQ2, HGB, R17, R27, MIX2, OG, XTOK = (0, 16384, 24576, 25088, 25600,
                                              29696, 30208)
        QWA2, QP2, QN2 = 31232, 33280, 34304
        OT7 = 31232
        RW7 = 30208
        RW7 = 33280

        evv = get("ev")

        @pr.on("vector")
        def _(e, evv=evv):
            e.wait_ge(ev, evv)
            rv = rb2.rearrange("p (k w) -> p k w", w=16)
            e.memset(rb2[:, :], 0.0)
            e.drain()
            e.tensor_copy(rv[:, 0:32, 0], VC("g2", 32))
            e.tensor_copy(rv[:, 0:32, 1], VC("bt2", 32))
            e.drain().then_inc(ev, 1)
        add("ev")

        ev_q7 = {}
        evq0 = get("ev")
        for k in range(32):
            QWp = QWA2 if k % 2 == 0 else QWA2 + 1024
            lsem = "qs" if k % 2 == 0 else "qc"

            @pr.on("sync")
            def _(e, evv=ev_q7.get(k - 2, evq0)):
                e.wait_ge(ev, evv)
            dma("sync", AR[:, QWp:QWp + 1024], w2T[k * 128:(k + 1) * 128, :],
                sem=lsem)
            qsv = get(lsem)

            @pr.on("vector")
            def _(e, k=k, qsv=qsv, QWp=QWp, qsm=SEM[lsem]):
                e.wait_ge(qsm, qsv)
                e.tensor_scalar(AR[:, QP2:QP2 + 1024], AR[:, QWp:QWp + 1024],
                                VC("hm2"), VC("m2"), op0=ALU.is_gt, op1=ALU.mult)
                e.tensor_scalar(AR[:, QN2:QN2 + 1024], AR[:, QWp:QWp + 1024],
                                VC("hm2n"), VC("m2"), op0=ALU.is_lt, op1=ALU.mult)
                e.drain()
                e.tensor_tensor(ARB[:, 2 * WQ2 + k * 1024:2 * WQ2 + (k + 1) * 1024],
                                AR[:, QP2:QP2 + 1024], AR[:, QN2:QN2 + 1024],
                                op=ALU.subtract)
                e.drain().then_inc(ev, 1)
            add("ev")
            ev_q7[k] = get("ev")
        evv = get("ev")

        evb7 = get("ev")

        @pr.on("tensor")
        def _(e, evv=evv, evb7=evb7):
            e.wait_ge(ev, evv)
            for m in range(8):
                if m >= 2:
                    e.wait_ge(ev, evb7 + m - 1)
                pb = pE if m % 2 == 0 else pF
                for k in range(32):
                    e.matmul(pb[0:128, 0:16],
                             ARB[:, 2 * WQ2 + k * 1024 + m * 128:2 * WQ2 + k * 1024 + (m + 1) * 128],
                             rb2.rearrange("p (k w) -> p k w", w=16)[:, k, :],
                             start=(k == 0), stop=(k == 31))
                e.drain().then_inc(mm, 1)
        add("mm", 8)
        mmv = get("mm") - 8

        @pr.on("vector")
        def _(e, mmv=mmv):
            for m in range(8):
                e.wait_ge(mm, mmv + m + 1)
                pb = pE if m % 2 == 0 else pF
                e.tensor_copy(VC("W2c", 1, m), pb[:, 0:1])
                e.tensor_copy(VC("W2b", 1, m), pb[:, 1:2])
                e.drain().then_inc(ev, 1)
            e.tensor_scalar_mul(VC("W2cn", 8), VC("W2c", 8), -1.0)
            e.tensor_add(VC("W2bb", 8), VC("W2b", 8), VC("b2c", 8))
            e.drain()
            for k in range(32):
                e.tensor_scalar_mul(ARB[:, 2 * WQ2 + k * 1024:2 * WQ2 + (k + 1) * 1024],
                                    ARB[:, 2 * WQ2 + k * 1024:2 * WQ2 + (k + 1) * 1024],
                                    VC("g2", 1, k))
            e.drain().then_inc(ev, 1)
        add("ev", 9)
        evw2ready = get("ev")

        ev_add8 = {}
        st_q8 = {}
        mm_m7 = {}
        for gi in range(4):
            ts0 = gi * G
            evv = get("ev")

            @pr.on("gpsimd")
            def _(e, g7=mm_m7.get(gi - 1, 0)):
                if g7:
                    e.wait_ge(mm, g7)
            for ct in range(32):
                dma("gpsimd", ARB[:, 2 * HGB + ct * 512:2 * HGB + (ct + 1) * 512],
                    sp_h[ct, :, ts0:ts0 + G], sem="qg")
            qg7 = get("qg")

            @pr.on("sync")
            def _(e, evv=evv):
                e.wait_ge(ev, evv)
            for f in range(8):
                dma("sync", AR[:, MIX2 + f * 512:MIX2 + (f + 1) * 512],
                    sp_mix[f, :, ts0:ts0 + G])
            qsv = get("qs")

            def emit_stats7():
                SQH7 = 36352
                sq7_sp = {}
                st7_mm = {}
                for ct in range(32):
                    p7 = ct % 2
                    sqs = SQH7 + p7 * 256

                    @pr.on("scalar")
                    def _(e, ct=ct, sqs=sqs, qsv=qg7, mmg=st7_mm.get(ct - 2, 0)):
                        if ct == 0:
                            e.wait_ge(qg, qsv)
                        if mmg:
                            e.wait_ge(mm, mmg)
                        e.activation(ARB[:, 2 * sqs:2 * sqs + 512],
                                     ARB[:, 2 * HGB + ct * 512:2 * HGB + (ct + 1) * 512],
                                     AF.Square)
                        e.drain().then_inc(sp, 1)
                    add("sp")
                    sq7_sp[ct] = get("sp")

                    @pr.on("tensor")
                    def _(e, ct=ct, sqs=sqs, spv=sq7_sp[ct], qsv=qg7):
                        if ct == 0:
                            e.wait_ge(qg, qsv)
                        e.matmul(pC[0:1, 0:512], onescb[:],
                                 ARB[:, 2 * HGB + ct * 512:2 * HGB + (ct + 1) * 512],
                                 start=(ct == 0), stop=(ct == 31))
                        e.wait_ge(sp, spv)
                        e.matmul(pD[0:1, 0:512], onescb[:], ARB[:, 2 * sqs:2 * sqs + 512],
                                 start=(ct == 0), stop=(ct == 31))
                        e.drain().then_inc(mm, 1)
                    add("mm")
                    st7_mm[ct] = get("mm")
                mmv = get("mm")

                @pr.on("vector")
                def _(e, mmv=mmv, qsv=qsv):
                    e.wait_ge(mm, mmv)
                    e.wait_ge(qs, qsv)
                    MU = AR[0:1, RW7:RW7 + 512]
                    S2 = AR[0:1, RW7 + 512:RW7 + 1024]
                    VA = AR[0:1, 36864:37376]
                    e.tensor_scalar_mul(MU, pC[0:1, 0:512], 1.0 / FH)
                    e.tensor_scalar_mul(S2, pD[0:1, 0:512], 1.0 / FH)
                    e.drain()
                    e.tensor_mul(VA, MU, MU)
                    e.drain()
                    e.tensor_sub(VA, S2, VA)
                    e.drain().then_inc(ev, 1)
                add("ev")
                evv = get("ev")

                @pr.on("scalar")
                def _(e, evv=evv):
                    e.wait_ge(ev, evv)
                    e.activation(AR[0:1, 37376:37888],
                                 AR[0:1, 36864:37376], AF.Sqrt, bias=LN_EPS)
                    e.drain().then_inc(sp, 1)
                add("sp")
                spv = get("sp")

                @pr.on("vector")
                def _(e, spv=spv):
                    e.wait_ge(sp, spv)
                    e.reciprocal(AR[0:1, RW7 + 512:RW7 + 1024], AR[0:1, 37376:37888])
                    e.drain()
                    e.tensor_mul(AR[0:1, RW7:RW7 + 512], AR[0:1, RW7:RW7 + 512],
                                 AR[0:1, RW7 + 512:RW7 + 1024])
                    e.drain().then_inc(ev, 1)
                add("ev")
                evv = get("ev")

                @pr.on("tensor")
                def _(e, evv=evv):
                    e.wait_ge(ev, evv)
                    e.matmul(pE[:, 0:512], ones1[:], AR[0:1, RW7:RW7 + 512], start=True, stop=True)
                    e.matmul(pF[:, 0:512], ones1[:], AR[0:1, RW7 + 512:RW7 + 1024],
                             start=True, stop=True)
                    e.drain().then_inc(mm, 1)
                add("mm")
                mmv = get("mm")

                @pr.on("vector")
                def _(e, mmv=mmv):
                    e.wait_ge(mm, mmv)
                    e.tensor_copy(AR[:, R17:R17 + 512], pE[:, 0:512])
                    e.tensor_copy(AR[:, R27:R27 + 512], pF[:, 0:512])
                    e.drain().then_inc(ev, 1)
                add("ev")
                evrep = get("ev")


            ev_fold = {}
            PB7 = (pA, pB, pE, pF)
            rep_box = {}
            fold_mm = {}
            for m in range(8):
                evv = ev_fold.get(m - 4,
                                  evw2ready if m < 2 else rep_box["evrep"])

                @pr.on("tensor")
                def _(e, m=m, evv=evv, qg7=qg7):
                    if m == 0:
                        e.wait_ge(qg, qg7)
                    e.wait_ge(ev, evv)
                    pb = PB7[m % 4]
                    for k in range(32):
                        e.matmul(pb[:, 0:512],
                                 ARB[:, 2 * WQ2 + k * 1024 + m * 128:2 * WQ2 + k * 1024 + (m + 1) * 128],
                                 ARB[:, 2 * HGB + k * 512:2 * HGB + (k + 1) * 512],
                                 start=(k == 0), stop=(k == 31))
                    e.drain().then_inc(mm, 1)
                add("mm")
                mm_w2_7 = get("mm")
                if m == 1:
                    emit_stats7()
                    rep_box["evrep"] = get("ev")

                def emit_fold7(m, mmv):
                    @pr.on("vector")
                    def _(e, m=m, mmv=mmv, evrep=rep_box["evrep"]):
                        e.wait_ge(mm, mmv)
                        e.wait_ge(ev, evrep)
                        pb = PB7[m % 4]
                        OV = AR[:, OG:OG + 512]
                        e.tensor_tensor(OV, pb[:, 0:512], AR[:, R27:R27 + 512], op=ALU.mult)
                        e.drain()
                        e.scalar_tensor_tensor(OV, AR[:, R17:R17 + 512], VC("W2cn", 1, m),
                                               OV, op0=ALU.mult, op1=ALU.add)
                        e.drain()
                        e.tensor_scalar_add(OV, OV, VC("W2bb", 1, m))
                        e.drain()
                        e.tensor_add(AR[:, MIX2 + m * 512:MIX2 + (m + 1) * 512],
                                     AR[:, MIX2 + m * 512:MIX2 + (m + 1) * 512], OV)
                        e.drain().then_inc(ev, 1)
                    add("ev")
                    ev_fold[m] = get("ev")

                fold_mm[m] = mm_w2_7
                if m >= 2:
                    emit_fold7(m - 2, fold_mm[m - 2])
            emit_fold7(6, fold_mm[6])
            emit_fold7(7, fold_mm[7])
            mm_m7[gi] = get("mm")
            evsum = get("ev")

            XTOK2, OT72 = 34304, 35328
            for tsub in range(4):
                g8 = gi * 4 + tsub
                XTp = XTOK if tsub % 2 == 0 else XTOK2
                OTp = OT7 if tsub % 2 == 0 else OT72
                ssem = "qt" if tsub % 2 == 0 else "qc"

                @pr.on("sync")
                def _(e, gate=ev_add8.get(g8 - 2, 0)):
                    if gate:
                        e.wait_ge(ev, gate)
                dma("sync", AR[:, XTp:XTp + 1024],
                    xs[WIN + ts0 + tsub * 128:WIN + ts0 + tsub * 128 + 128, :],
                    sem=ssem)
                xlq = get(ssem)

                evbase8 = get("ev")

                @pr.on("tensor")
                def _(e, tsub=tsub, evsum=evsum, evbase8=evbase8):
                    e.wait_ge(ev, evsum)
                    for f in range(8):
                        if evbase8 + f - 1 > 0:
                            e.wait_ge(ev, evbase8 + f - 1)
                        pb = pG if f % 2 == 0 else pH
                        e.transpose(pb[:, 0:128],
                                    AR[:, MIX2 + f * 512 + tsub * 128:MIX2 + f * 512 + tsub * 128 + 128],
                                    identf[:])
                        e.drain().then_inc(mm, 1)
                add("mm", 8)
                mmv = get("mm") - 8

                @pr.on("vector")
                def _(e, tsub=tsub, mmv=mmv, OTp=OTp, xlq=xlq, stm=SEM[ssem]):
                    e.wait_ge(stm, xlq)
                    for f in range(8):
                        e.wait_ge(mm, mmv + f + 1)
                        pb = pG if f % 2 == 0 else pH
                        e.tensor_copy(AR[:, OTp + f * 128:OTp + (f + 1) * 128],
                                      pb[:, 0:128])
                        e.drain().then_inc(ev, 1)
                add("ev", 8)

                @pr.on("vector")
                def _(e, OTp=OTp, XTp=XTp):
                    e.tensor_add(AR[:, OTp:OTp + 1024], AR[:, OTp:OTp + 1024],
                                 AR[:, XTp:XTp + 1024])
                    e.drain().then_inc(ev, 1)
                add("ev")
                ev_add8[g8] = get("ev")
                evv = get("ev")

                @pr.on("sync")
                def _(e, evv=evv):
                    e.wait_ge(ev, evv)
                dma("sync", out[ts0 + tsub * 128:ts0 + tsub * 128 + 128, :],
                    AR[:, OTp:OTp + 1024], sem=ssem)
                st_q8[g8] = get(ssem)

        barrier()
        if debug:
            for nmout, src in [(dbg_y2, sp_y2), (dbg_st, sp_st), (dbg_st2, sp_st2),
                               (dbg_mix, sp_mix), (dbg_xc, sp_xc),
                               (dbg_h, sp_h), (dbg_xn, sp_xn)]:
                dma("sync", nmout[:], src[:])
            qsv = get("qs")

            @pr.on("sync")
            def _(e, qsv=qsv):
                e.wait_ge(qs, qsv)
        _unused = (debug,)

        for eng in ENGS:
            qq = pr.q[eng]
            if cut is not None and cut < len(barrier_marks):
                qq = qq[:barrier_marks[cut][eng]]
            getattr(blk, eng)(lambda e, qq=qq: [f(e) for f in qq])

    return nc


_NC_CACHE = [None]


def kernel(**inputs):
    x = np.asarray(inputs["L0_attn_x"])
    B, L, _ = x.shape

    def rs(v, r):
        return np.ascontiguousarray(np.asarray(v).reshape(r, 128))

    base = {
        "W_in": np.ascontiguousarray(inputs["L0_attn_W_in"]),
        "conv_w": np.ascontiguousarray(
            np.asarray(inputs["L0_attn_conv_w"]).reshape(16, 128, 4)
            .transpose(1, 0, 2).reshape(128, 64)),
        "conv_b": rs(inputs["L0_attn_conv_b"], 16),
        "W_xp": np.ascontiguousarray(inputs["L0_attn_W_xproj"]),
        "W_dt": np.ascontiguousarray(inputs["L0_attn_W_dt"]),
        "b_dt": rs(inputs["L0_attn_b_dt"], 16),
        "A_log": np.ascontiguousarray(np.asarray(inputs["L0_attn_A_log"]).reshape(1, NH)),
        "D_skip": np.ascontiguousarray(np.asarray(inputs["L0_attn_D_skip"]).reshape(1, NH)),
        "W_out": np.ascontiguousarray(inputs["L0_attn_W_out"]),
        "ln_g": rs(inputs["L0_attn_ln_g"], 16),
        "ln_b": rs(inputs["L0_attn_ln_b"], 16),
        "ln1_g": rs(inputs["L0_ffn_ln1_g"], 8),
        "ln1_b": rs(inputs["L0_ffn_ln1_b"], 8),
        "w1T": np.ascontiguousarray(np.asarray(inputs["L0_ffn_w1"]).T),
        "b1": rs(inputs["L0_ffn_b1"], 32),
        "ln2_g": rs(inputs["L0_ffn_ln2_g"], 32),
        "ln2_b": rs(inputs["L0_ffn_ln2_b"], 32),
        "w2T": np.ascontiguousarray(np.asarray(inputs["L0_ffn_w2"]).T),
        "b2": rs(inputs["L0_ffn_b2"], 8),
    }
    in_maps = []
    for c in range(8):
        b, h = c // 2, c % 2
        s0 = h * OWN
        xsl = np.zeros((TOT, D), np.float32)
        lo = s0 - WIN
        src_lo = max(lo, 0)
        xsl[src_lo - lo:, :] = x[b, src_lo:s0 + OWN, :]
        in_maps.append(dict(base, xs=xsl))

    if _NC_CACHE[0] is None:
        _NC_CACHE[0] = build()
    res = run_bass_kernel_spmd(_NC_CACHE[0], in_maps, core_ids=list(range(8)))
    outp = np.empty((B, L, D), np.float32)
    for c in range(8):
        b, h = c // 2, c % 2
        outp[b, h * OWN:(h + 1) * OWN, :] = res.results[c]["out"]
    return outp


if __name__ == "__main__":
    nc = build()
    print("built ok;", "SBUF part bytes:", nc.SBUF_PARTITION_SIZE_BYTES)


# version 30
# speedup vs baseline: 1.0028x; 1.0028x over previous
import numpy as np
import concourse.bass as bass
import concourse.mybir as mybir
from concourse.bass_utils import run_bass_kernel_spmd

F32 = mybir.dt.float32
BF16 = mybir.dt.bfloat16
I32 = mybir.dt.int32
AF = mybir.ActivationFunctionType
ALU = mybir.AluOpType

D = 1024
DI = 2048
NH = 16
DS = 16
NCH = 16
CHUNK = 64
WIN = 1024
OWN = 2048
TOT = WIN + OWN
G = 512
NG = TOT // G
FH = 4096
LN_EPS = 1e-5
ARF = 44048

ENGS = ("sync", "gpsimd", "tensor", "vector", "scalar")


class P:

    def __init__(self):
        self.q = {e: [] for e in ENGS}
        self.cnt = {}

    def on(self, eng):
        def deco(fn):
            self.q[eng].append(fn)
        return deco

    def add(self, name, k=1):
        self.cnt[name] = self.cnt.get(name, 0) + k
        return self.cnt[name]

    def get(self, name):
        return self.cnt.get(name, 0)


def build(debug=False, cut=None, p4bar=False):
    nc = bass.Bass("TRN2", target_bir_lowering=False)

    xs = nc.dram_tensor("xs", [TOT, D], F32, kind="ExternalInput").ap()
    W_in = nc.dram_tensor("W_in", [D, 2 * DI], F32, kind="ExternalInput").ap()
    conv_w = nc.dram_tensor("conv_w", [128, 64], F32, kind="ExternalInput").ap()
    conv_b = nc.dram_tensor("conv_b", [DI // 128, 128], F32, kind="ExternalInput").ap()
    W_xp = nc.dram_tensor("W_xp", [DI, 48], F32, kind="ExternalInput").ap()
    W_dt = nc.dram_tensor("W_dt", [NH, DI], F32, kind="ExternalInput").ap()
    b_dt = nc.dram_tensor("b_dt", [DI // 128, 128], F32, kind="ExternalInput").ap()
    A_log = nc.dram_tensor("A_log", [1, NH], F32, kind="ExternalInput").ap()
    D_skip = nc.dram_tensor("D_skip", [1, NH], F32, kind="ExternalInput").ap()
    W_out = nc.dram_tensor("W_out", [DI, D], F32, kind="ExternalInput").ap()
    ln_g = nc.dram_tensor("ln_g", [DI // 128, 128], F32, kind="ExternalInput").ap()
    ln_b = nc.dram_tensor("ln_b", [DI // 128, 128], F32, kind="ExternalInput").ap()
    ln1_g = nc.dram_tensor("ln1_g", [D // 128, 128], F32, kind="ExternalInput").ap()
    ln1_b = nc.dram_tensor("ln1_b", [D // 128, 128], F32, kind="ExternalInput").ap()
    w1T = nc.dram_tensor("w1T", [D, FH], F32, kind="ExternalInput").ap()
    b1 = nc.dram_tensor("b1", [FH // 128, 128], F32, kind="ExternalInput").ap()
    ln2_g = nc.dram_tensor("ln2_g", [FH // 128, 128], F32, kind="ExternalInput").ap()
    ln2_b = nc.dram_tensor("ln2_b", [FH // 128, 128], F32, kind="ExternalInput").ap()
    w2T = nc.dram_tensor("w2T", [FH, D], F32, kind="ExternalInput").ap()
    b2 = nc.dram_tensor("b2", [D // 128, 128], F32, kind="ExternalInput").ap()
    out = nc.dram_tensor("out", [OWN, D], F32, kind="ExternalOutput").ap()

    sp_xn = nc.dram_tensor("sp_xn", [TOT, D], BF16, kind="Internal").ap()
    sp_xc = nc.dram_tensor("sp_xc", [NCH, 128, TOT], F32, kind="Internal").ap()
    sp_z = nc.dram_tensor("sp_z", [NCH, 128, OWN], F32, kind="Internal").ap()
    sp_y2 = nc.dram_tensor("sp_y2", [NCH, 128, OWN], BF16, kind="Internal").ap()
    sp_mix = nc.dram_tensor("sp_mix", [D // 128, 128, OWN], F32, kind="Internal").ap()
    sp_h = nc.dram_tensor("sp_h", [FH // 128, 128, OWN], BF16, kind="Internal").ap()
    wq1d = nc.dram_tensor("wq1d", [8, 128, FH], BF16, kind="Internal").ap()
    sp_st = nc.dram_tensor("sp_st", [2, OWN], F32, kind="Internal").ap()
    sp_st2 = nc.dram_tensor("sp_st2", [2, OWN], F32, kind="Internal").ap()
    if debug:
        dbg_y2 = nc.dram_tensor("dbg_y2", [NCH, 128, OWN], BF16, kind="ExternalOutput").ap()
        dbg_st = nc.dram_tensor("dbg_st", [2, OWN], F32, kind="ExternalOutput").ap()
        dbg_st2 = nc.dram_tensor("dbg_st2", [2, OWN], F32, kind="ExternalOutput").ap()
        dbg_mix = nc.dram_tensor("dbg_mix", [D // 128, 128, OWN], F32, kind="ExternalOutput").ap()
        dbg_xc = nc.dram_tensor("dbg_xc", [NCH, 128, TOT], F32, kind="ExternalOutput").ap()
        dbg_h = nc.dram_tensor("dbg_h", [FH // 128, 128, OWN], BF16, kind="ExternalOutput").ap()
        dbg_xn = nc.dram_tensor("dbg_xn", [TOT, D], BF16, kind="ExternalOutput").ap()

    pr = P()
    add, get = pr.add, pr.get

    import contextlib
    with contextlib.ExitStack() as stk:
        en = stk.enter_context
        AR = en(nc.sbuf_tensor("AR", [128, ARF], F32))
        projp = en(nc.sbuf_tensor("projp", [128, TOT], F32))
        identf = en(nc.sbuf_tensor("identf", [128, 128], F32))
        identb = en(nc.sbuf_tensor("identb", [128, 128], BF16))
        ones1 = en(nc.sbuf_tensor("ones1", [1, 128], F32))
        onesf = en(nc.sbuf_tensor("onesf", [128, 1], F32))
        ones1b = en(nc.sbuf_tensor("ones1b", [1, 128], BF16))
        onescb = en(nc.sbuf_tensor("onescb", [128, 1], BF16))
        vcol = en(nc.sbuf_tensor("vcol", [128, 400], F32))
        rb2 = en(nc.sbuf_tensor("rb2", [128, 512], BF16))
        cw = en(nc.sbuf_tensor("cw", [128, 64], F32))
        carry = en(nc.sbuf_tensor("carry", [128, 256], F32))
        wdtb = en(nc.sbuf_tensor("wdtb", [16, DI], BF16))
        cmask = en(nc.sbuf_tensor("cmask", [128, G], F32))
        ck = en(nc.sbuf_tensor("ck", [128, 2], F32))
        pA = en(nc.psum_tensor("pA", [128, 512], F32))
        pB = en(nc.psum_tensor("pB", [128, 512], F32))
        pC = en(nc.psum_tensor("pC", [128, 512], F32))
        pD = en(nc.psum_tensor("pD", [128, 512], F32))
        pE = en(nc.psum_tensor("pE", [128, 512], F32))
        pF = en(nc.psum_tensor("pF", [128, 512], F32))
        pG = en(nc.psum_tensor("pG", [128, 512], F32))
        pH = en(nc.psum_tensor("pH", [128, 512], F32))
        qs = en(nc.semaphore("qs"))
        qg = en(nc.semaphore("qg"))
        qc = en(nc.semaphore("qc"))
        mm = en(nc.semaphore("mm"))
        ev = en(nc.semaphore("ev"))
        sp = en(nc.semaphore("sp"))
        gp = en(nc.semaphore("gp"))
        ph = en(nc.semaphore("ph"))
        qt = en(nc.semaphore("qt"))
        yr = en(nc.semaphore("yr"))
        qh = en(nc.semaphore("qh"))
        qi = en(nc.semaphore("qi"))
        qm = en(nc.semaphore("qm"))
        ql = en(nc.semaphore("ql"))
        blk = en(nc.Block())
        ARB = AR.bitcast(BF16)
        nc.const_aps.register(F32, LN_EPS, ck[:, 0:1])
        nc.const_aps.register(F32, 1.0, ck[:, 1:2])
        SEM = dict(qs=qs, qg=qg, qc=qc, mm=mm, ev=ev, sp=sp, gp=gp, ph=ph, qt=qt,
                   yr=yr, qh=qh, qi=qi, qm=qm, ql=ql)
        nbar = [0]

        barrier_marks = []

        def barrier():
            nbar[0] += 1
            target = 5 * nbar[0]
            qsv, qgv, qcv, gpv_b, qtv = (get("qs"), get("qg"), get("qc"),
                                         get("gp"), get("qt"))
            qhv_b = get("qh")
            yrv_b, qlv_b, qiv_b, qmv_b = (get("yr"), get("ql"), get("qi"),
                                          get("qm"))
            for eng in ENGS:
                @pr.on(eng)
                def _(e, eng=eng, target=target, qsv=qsv, qgv=qgv, qcv=qcv,
                      gpv_b=gpv_b, qtv=qtv, qhv_b=qhv_b):
                    if eng == "sync":
                        e.wait_ge(qs, qsv)
                        e.wait_ge(qc, qcv)
                        e.wait_ge(qt, qtv)
                        e.wait_ge(yr, yrv_b)
                        e.wait_ge(ql, qlv_b)
                    elif eng == "gpsimd":
                        e.wait_ge(qg, qgv)
                        e.wait_ge(qh, qhv_b)
                        e.wait_ge(qi, qiv_b)
                        e.wait_ge(qm, qmv_b)
                        e.wait_ge(gp, gpv_b)
                    elif eng == "scalar":
                        e.wait_ge(qc, qcv)
                    e.drain().then_inc(ph, 1)
                    e.wait_ge(ph, target)
            barrier_marks.append({eng: len(pr.q[eng]) for eng in ENGS})

        def dma(eng, outap, inap, sem="qs"):
            add(sem, 16)
            s = SEM[sem]
            @pr.on(eng)
            def _(e, outap=outap, inap=inap, s=s):
                e.dma_start(out=outap, in_=inap).then_inc(s, 16)

        def dmat(outap, inap, sem="qs"):
            add(sem, 16)
            s = SEM[sem]
            @pr.on("sync")
            def _(e, outap=outap, inap=inap, s=s):
                e.dma_start_transpose(out=outap, in_=inap).then_inc(s, 16)

        VOFF = {}
        _o = 0
        for nmc, ncols in [("A", 16), ("Dsk", 16), ("bdt", 16), ("cvb", 16),
                           ("lng", 16), ("lnb", 16), ("g1", 8), ("bt1", 8),
                           ("b1c", 32), ("g2", 32), ("bt2", 32), ("b2c", 8),
                           ("Wc", 8), ("Wb", 8), ("W2c", 8), ("W2b", 8),
                           ("m1", 1), ("m2", 1), ("hm1", 1), ("hm2", 1),
                           ("hm1n", 1), ("hm2n", 1), ("Wcn", 8), ("W2cn", 8),
                           ("W2bb", 8), ("l1r", 8)]:
            VOFF[nmc] = _o
            _o += ncols
        assert _o <= 400

        def VC(nmc, n=1, j=0):
            o = VOFF[nmc] + j
            return vcol[:, o:o + n]

        stage_specs = [(conv_b, 16, "cvb"), (b_dt, 16, "bdt"), (ln_g, 16, "lng"),
                       (ln_b, 16, "lnb"), (ln1_g, 8, "g1"), (ln1_b, 8, "bt1"),
                       (b1, 32, "b1c"), (ln2_g, 32, "g2"), (ln2_b, 32, "bt2"),
                       (b2, 8, "b2c")]
        o_st = 128 * len(stage_specs)

        for i, (ap_, rows, _) in enumerate(stage_specs):
            dma("sync", AR[0:rows, i * 128:i * 128 + 128], ap_[:, :])
        dma("sync", AR[0:1, o_st:o_st + 16], A_log[:, :])
        dma("sync", AR[0:1, o_st + 16:o_st + 32], D_skip[:, :])
        dma("sync", cw[:, :], conv_w[:, :])
        dma("gpsimd", wdtb[:, :], W_dt[:, :], sem="qg")

        @pr.on("gpsimd")
        def _(e):
            e.memset(ones1[:], 1.0).then_inc(gp, 1)
            e.memset(ck[:, 0:1], LN_EPS).then_inc(gp, 1)
            e.memset(ck[:, 1:2], 1.0).then_inc(gp, 1)
            e.memset(onesf[:], 1.0).then_inc(gp, 1)
            e.memset(ones1b[:], 1.0).then_inc(gp, 1)
            e.memset(onescb[:], 1.0).then_inc(gp, 1)
            e.memset(cmask[0:128, 0:G].rearrange("p (a b) -> p a b", b=CHUNK)[:, :, 1:CHUNK],
                     1.0).then_inc(gp, 1)
            e.memset(cmask[:, 0:G:CHUNK], 0.0).then_inc(gp, 1)
            e.memset(carry[:], 0.0).then_inc(gp, 1)
            e.iota(AR[:, o_st + 128:o_st + 256].bitcast(I32),
                   pattern=[[-1, 128]], base=0, channel_multiplier=1).then_inc(gp, 1)
        add("gp", 10)

        gpv = get("gp")

        @pr.on("vector")
        def _(e, gpv=gpv):
            e.wait_ge(gp, gpv)
            e.tensor_single_scalar(identf[:], AR[:, o_st + 128:o_st + 256].bitcast(I32),
                                   0, op=ALU.is_equal)
            e.drain()
            e.tensor_copy(identb[:], identf[:])
            e.drain().then_inc(ev, 1)
        add("ev")

        qsv = get("qs")

        @pr.on("scalar")
        def _(e, qsv=qsv):
            e.wait_ge(qs, qsv)
            e.activation(AR[0:1, o_st + 32:o_st + 48], AR[0:1, o_st:o_st + 16],
                         AF.Exp).then_inc(sp, 1)
        add("sp")

        spv = get("sp")

        @pr.on("vector")
        def _(e, spv=spv):
            e.wait_ge(sp, spv)
            e.tensor_scalar_mul(AR[0:1, o_st + 32:o_st + 48], AR[0:1, o_st + 32:o_st + 48], -1.0)
            e.drain().then_inc(ev, 1)
        add("ev")

        evv = get("ev")
        mmv0 = get("mm")

        @pr.on("tensor")
        def _(e, evv=evv, mmv0=mmv0, qsv=get("qs")):
            e.wait_ge(ev, evv)
            e.wait_ge(qs, qsv)
            for i, (_a, rows, _n) in enumerate(stage_specs):
                if i > 0:
                    e.wait_ge(ev, evv + i)
                e.transpose(pG[:, 0:rows], AR[0:rows, i * 128:i * 128 + 128],
                            identf[0:rows, 0:rows])
                e.drain().then_inc(mm, 1)
            e.wait_ge(ev, evv + len(stage_specs))
            e.matmul(pH[:, 0:16], ones1[:], AR[0:1, o_st + 32:o_st + 48], start=True, stop=True)
            e.matmul(pH[:, 16:32], ones1[:], AR[0:1, o_st + 16:o_st + 32], start=True, stop=True)
            e.drain().then_inc(mm, 1)
        add("mm", len(stage_specs) + 1)

        @pr.on("vector")
        def _(e, mmv0=mmv0, specs=tuple(stage_specs)):
            for i, (_a, rows, nmc) in enumerate(specs):
                e.wait_ge(mm, mmv0 + i + 1)
                e.tensor_copy(vcol[:, VOFF[nmc]:VOFF[nmc] + rows], pG[:, 0:rows])
                e.drain().then_inc(ev, 1)
            e.wait_ge(mm, mmv0 + len(specs) + 1)
            e.tensor_copy(VC("A", 16), pH[:, 0:16])
            e.tensor_copy(VC("Dsk", 16), pH[:, 16:32])
            e.drain().then_inc(ev, 1)
        add("ev", len(stage_specs) + 1)

        barrier()

        QW, QSUM = 0, 8192
        wtiles = [(w1T, k, 4096, k) for k in range(8)] + \
                 [(w2T, k, 1024, 8 + k) for k in range(32)]
        for i, (wt, k, wid, col) in enumerate(wtiles):
            par = i % 2
            if i >= 1:
                evv = get("ev")

                @pr.on("sync")
                def _(e, evv=evv):
                    e.wait_ge(ev, evv)
            dma("sync", AR[:, QW + par * 4096:QW + par * 4096 + wid],
                wt[k * 128:(k + 1) * 128, :])
            qsv = get("qs")

            @pr.on("vector")
            def _(e, par=par, wid=wid, col=col, qsv=qsv):
                e.wait_ge(qs, qsv)
                e.tensor_reduce(AR[:, QSUM + col:QSUM + col + 1],
                                AR[:, QW + par * 4096:QW + par * 4096 + wid],
                                axis=mybir.AxisListType.X, op=ALU.add,
                                apply_absolute_value=True)
                e.drain().then_inc(ev, 1)
            add("ev")

        evv = get("ev")

        @pr.on("vector")
        def _(e, evv=evv):
            e.wait_ge(ev, evv)
            e.tensor_reduce(AR[:, QSUM + 40:QSUM + 41], AR[:, QSUM:QSUM + 8],
                            axis=mybir.AxisListType.X, op=ALU.add)
            e.tensor_reduce(AR[:, QSUM + 41:QSUM + 42], AR[:, QSUM + 8:QSUM + 40],
                            axis=mybir.AxisListType.X, op=ALU.add)
            e.drain().then_inc(ev, 1)
        add("ev")
        evv = get("ev")

        @pr.on("tensor")
        def _(e, evv=evv):
            e.wait_ge(ev, evv)
            e.matmul(pG[0:1, 0:2], onesf[:], AR[:, QSUM + 40:QSUM + 42],
                     start=True, stop=True)
            e.drain().then_inc(mm, 1)
        add("mm")
        mmv = get("mm")

        @pr.on("vector")
        def _(e, mmv=mmv):
            e.wait_ge(mm, mmv)
            R = AR[0:1, QSUM + 44:QSUM + 56]
            e.tensor_scalar(R[:, 0:1], pG[0:1, 0:1], 1.0 / (FH * D), 1e-5,
                            op0=ALU.mult, op1=ALU.max)
            e.tensor_scalar(R[:, 1:2], pG[0:1, 1:2], 1.0 / (FH * D), 1e-5,
                            op0=ALU.mult, op1=ALU.max)
            e.drain()
            e.tensor_scalar_mul(R[:, 2:3], R[:, 0:1], 0.5)
            e.tensor_scalar_mul(R[:, 3:4], R[:, 1:2], 0.5)
            e.tensor_scalar_mul(R[:, 4:5], R[:, 0:1], -0.5)
            e.tensor_scalar_mul(R[:, 5:6], R[:, 1:2], -0.5)
            e.drain().then_inc(ev, 1)
        add("ev")
        evv = get("ev")

        @pr.on("tensor")
        def _(e, evv=evv):
            e.wait_ge(ev, evv)
            e.matmul(pG[:, 8:14], ones1[:], AR[0:1, QSUM + 44:QSUM + 50],
                     start=True, stop=True)
            e.drain().then_inc(mm, 1)
        add("mm")
        mmv = get("mm")

        @pr.on("vector")
        def _(e, mmv=mmv):
            e.wait_ge(mm, mmv)
            for j, nmc in enumerate(["m1", "m2", "hm1", "hm2", "hm1n", "hm2n"]):
                e.tensor_copy(VC(nmc), pG[:, 8 + j:9 + j])
            e.drain().then_inc(ev, 1)
        add("ev")

        barrier()

        XTM, XSQ1, XNB1, RWS = 0, 2048, 4096, 5120
        ev_n1 = {}
        sp_sq1 = {}
        for tt in range(TOT // 128):
            p1 = tt % 2
            psem = "qs" if p1 == 0 else "qc"
            slot = XTM + p1 * 1024
            sqs = XSQ1 + p1 * 1024
            nsl = XNB1 + p1 * 512
            R0 = RWS + p1 * 16

            lsem = "qg" if p1 == 0 else "qh"

            @pr.on("gpsimd")
            def _(e, gate=ev_n1.get(tt - 2, 0)):
                if gate:
                    e.wait_ge(ev, gate)
            dma("gpsimd", AR[:, slot:slot + 1024],
                xs[tt * 128:(tt + 1) * 128, :], sem=lsem)
            qlv = get(lsem)

            @pr.on("scalar")
            def _(e, tt=tt, qlv=qlv, qsm=SEM[lsem], slot=slot, sqs=sqs, R0=R0,
                  gate=ev_n1.get(tt - 2, 0)):
                e.wait_ge(qsm, qlv)
                if gate:
                    e.wait_ge(ev, gate)
                XT = AR[:, slot:slot + 1024]
                e.activation(AR[:, sqs:sqs + 1024], XT, AF.Copy,
                             accum_out=AR[:, R0:R0 + 1])
                e.drain()
                e.activation(AR[:, sqs:sqs + 1024], XT, AF.Square,
                             accum_out=AR[:, R0 + 1:R0 + 2])
                e.drain().then_inc(sp, 1)
            add("sp")
            sp_sq1[tt] = get("sp")

            @pr.on("vector")
            def _(e, tt=tt, spv=sp_sq1[tt], R0=R0):
                e.wait_ge(sp, spv)
                MU = AR[:, R0:R0 + 1]
                S2 = AR[:, R0 + 1:R0 + 2]
                mu = AR[:, R0 + 2:R0 + 3]
                va = AR[:, R0 + 3:R0 + 4]
                e.tensor_scalar_mul(mu, MU, 1.0 / D)
                e.drain()
                e.tensor_scalar_mul(va, S2, 1.0 / D)
                e.drain()
                mus = AR[:, R0 + 6:R0 + 7]
                e.tensor_mul(mus, mu, mu)
                e.drain()
                e.tensor_sub(va, va, mus)
                e.drain().then_inc(ev, 1)
            add("ev")
            evv = get("ev")

            @pr.on("scalar")
            def _(e, evv=evv, R0=R0):
                e.wait_ge(ev, evv)
                e.activation(AR[:, R0 + 4:R0 + 5], AR[:, R0 + 3:R0 + 4],
                             AF.Sqrt, bias=LN_EPS)
                e.drain().then_inc(sp, 1)
            add("sp")
            spv = get("sp")
            qfv = get(psem)

            @pr.on("vector")
            def _(e, tt=tt, spv=spv, R0=R0, slot=slot, nsl=nsl, qfv=qfv,
                  qsm=SEM[psem]):
                e.wait_ge(sp, spv)
                e.wait_ge(qsm, qfv)
                rs = AR[:, R0 + 5:R0 + 6]
                e.reciprocal(rs, AR[:, R0 + 4:R0 + 5])
                e.drain()
                e.tensor_scalar(ARB[:, 2 * nsl:2 * nsl + 1024],
                                AR[:, slot:slot + 1024],
                                AR[:, R0 + 2:R0 + 3], rs,
                                op0=ALU.subtract, op1=ALU.mult)
                e.drain().then_inc(ev, 1)
            add("ev")
            ev_n1[tt] = get("ev")
            evv = get("ev")

            @pr.on("sync")
            def _(e, evv=evv):
                e.wait_ge(ev, evv)
            dma("sync", sp_xn[tt * 128:(tt + 1) * 128, :],
                ARB[:, 2 * nsl:2 * nsl + 1024], sem=psem)

        barrier()

        WB1, XNBG, XEW, XTL, CVT, XCS, ZS, XCB, XPWB = (
            0, 16384, 18432, 18952, 19016, 19528, 20040, 20552, 24648)

        for k in range(8):
            dma("gpsimd", ARB[:, 2 * WB1 + k * 4096:2 * WB1 + (k + 1) * 4096],
                W_in[k * 128:(k + 1) * 128, :], sem="qg")
        dma("gpsimd", ARB[:, 2 * XPWB:2 * XPWB + 768].rearrange("p (k c) -> p k c", c=48),
            W_xp.rearrange("(k p) c -> p k c", p=128), sem="qg")

        @pr.on("gpsimd")
        def _(e):
            e.memset(AR[:, XTL:XTL + 64], 0.0).then_inc(gp, 1)
        add("gp")

        def WB1t(k, mt):
            o = 2 * WB1 + k * 4096 + mt * 128
            return ARB[:, o:o + 128]

        def XCBt(ct):
            o = 2 * XCB + ct * 512
            return ARB[:, o:o + 512]

        qgv_w = get("qg")
        gpv_t = get("gp")

        QWAq, QPq, QNq, WQSq = 28672, 32768, 36864, 40960
        ev_q6 = {}
        evq6 = get("ev")

        def emit_quant1(k):
            @pr.on("sync")
            def _(e, evv=ev_q6.get(k - 1, evq6)):
                e.wait_ge(ev, evv)
            dma("sync", AR[:, QWAq:QWAq + 4096], w1T[k * 128:(k + 1) * 128, :],
                sem="ql")
            qsv = get("ql")

            @pr.on("vector")
            def _(e, k=k, qsv=qsv):
                e.wait_ge(ql, qsv)
                e.tensor_scalar(AR[:, QPq:QPq + 4096], AR[:, QWAq:QWAq + 4096],
                                VC("hm1"), VC("m1"), op0=ALU.is_gt, op1=ALU.mult)
                e.tensor_scalar(AR[:, QNq:QNq + 4096], AR[:, QWAq:QWAq + 4096],
                                VC("hm1n"), VC("m1"), op0=ALU.is_lt, op1=ALU.mult)
                e.drain()
                e.tensor_tensor(ARB[:, 2 * WQSq:2 * WQSq + 4096],
                                AR[:, QPq:QPq + 4096], AR[:, QNq:QNq + 4096],
                                op=ALU.subtract)
                e.drain().then_inc(ev, 1)
            add("ev")
            ev_q6[k] = get("ev")
            evv = get("ev")

            @pr.on("sync")
            def _(e, evv=evv):
                e.wait_ge(ev, evv)
            dma("sync", wq1d[k, :, :], ARB[:, 2 * WQSq:2 * WQSq + 4096], sem="ql")

        CVT2, XCS2b, ZS2b = 25088, 25600, 26112
        XNBG2 = 26624
        ev_psfree = {}
        ev_zpsfree = {}
        sp_silu = {}
        sp_zsilu = {}
        qfull23 = {}
        mm_glast = {}

        for g in range(NG):
            gown = g >= 2
            XNBp = XNBG if g % 2 == 0 else XNBG2
            tsem = "qt" if g % 2 == 0 else "yr"

            @pr.on("sync")
            def _(e, gate=mm_glast.get(g - 2, 0)):
                if gate:
                    e.wait_ge(mm, gate)
            for f in range(8):
                dmat(ARB[:, 2 * XNBp + f * 512:2 * XNBp + (f + 1) * 512],
                     sp_xn[g * G:(g + 1) * G, f * 128:(f + 1) * 128], sem=tsem)
            qgv = get(tsem)

            for ct in range(NCH):
                gct = g * NCH + ct
                p2 = ct % 2
                psem = "qs" if p2 == 0 else "qc"
                CVp = CVT if p2 == 0 else CVT2
                XCp = XCS if p2 == 0 else XCS2b
                ZSp = ZS if p2 == 0 else ZS2b
                @pr.on("tensor")
                def _(e, ct=ct, qgv=qgv, evv=ev_psfree.get(gct - 2, 0), gpv_t=gpv_t,
                      XNBp=XNBp, tsm=SEM[tsem]):
                    e.wait_ge(qg, qgv_w)
                    if ct == 0:
                        e.wait_ge(tsm, qgv)
                    e.wait_ge(gp, gpv_t)
                    if evv:
                        e.wait_ge(ev, evv)
                    pb = pA if ct % 2 == 0 else pB
                    for k in range(8):
                        e.matmul(pb[:, 0:512], WB1t(k, ct),
                                 ARB[:, 2 * XNBp + k * 512:2 * XNBp + (k + 1) * 512],
                                 start=(k == 0), stop=(k == 7))
                    e.drain().then_inc(mm, 1)
                add("mm")
                mmv = get("mm")

                @pr.on("vector")
                def _(e, ct=ct, mmv=mmv, spcv=sp_silu.get(gct - 2, 0), CVp=CVp):
                    e.wait_ge(mm, mmv)
                    if spcv:
                        e.wait_ge(sp, spcv)
                    pb = pA if ct % 2 == 0 else pB
                    XEWv = AR[:, XEW:XEW + 520]
                    e.tensor_copy(XEWv[:, 3:515], pb[:, 0:512])
                    e.drain().then_inc(ev, 1)
                    e.tensor_copy(XEWv[:, 0:3], AR[:, XTL + ct * 4:XTL + ct * 4 + 3])
                    e.drain()
                    CV = AR[:, CVp:CVp + 512]
                    e.tensor_scalar_mul(CV, XEWv[:, 0:512], cw[:, ct * 4:ct * 4 + 1])
                    e.drain()
                    e.scalar_tensor_tensor(CV, XEWv[:, 1:513], cw[:, ct * 4 + 1:ct * 4 + 2],
                                           CV, op0=ALU.mult, op1=ALU.add)
                    e.drain()
                    e.scalar_tensor_tensor(CV, XEWv[:, 2:514], cw[:, ct * 4 + 2:ct * 4 + 3],
                                           CV, op0=ALU.mult, op1=ALU.add)
                    e.drain()
                    e.scalar_tensor_tensor(CV, XEWv[:, 3:515], cw[:, ct * 4 + 3:ct * 4 + 4],
                                           CV, op0=ALU.mult, op1=ALU.add)
                    e.tensor_copy(AR[:, XTL + ct * 4:XTL + ct * 4 + 3], XEWv[:, 512:515])
                    e.drain().then_inc(ev, 1)
                add("ev", 2)
                ev_psfree[gct] = get("ev") - 1
                evv = get("ev")

                qsv_x = (get(psem) if ct < 2 else qfull23.get(gct - 2, 0))

                @pr.on("scalar")
                def _(e, ct=ct, evv=evv, qsv_x=qsv_x,
                      qsm=SEM[psem], CVp=CVp, XCp=XCp):
                    e.wait_ge(ev, evv)
                    if qsv_x:
                        e.wait_ge(qsm, qsv_x)
                    e.activation(AR[:, XCp:XCp + 512], AR[:, CVp:CVp + 512], AF.Silu,
                                 bias=VC("cvb", 1, ct))
                    e.drain()
                    e.copy(XCBt(ct), AR[:, XCp:XCp + 512]).then_inc(sp, 1)
                add("sp")
                sp_silu[gct] = get("sp")
                spv = get("sp")

                @pr.on("sync")
                def _(e, spv=spv):
                    e.wait_ge(sp, spv)
                dma("sync", sp_xc[ct, :, g * G:(g + 1) * G], AR[:, XCp:XCp + 512],
                    sem=psem)

                if gown:
                    @pr.on("tensor")
                    def _(e, ct=ct, spz=sp_zsilu.get(gct - 2, 0), XNBp=XNBp):
                        if spz:
                            e.wait_ge(sp, spz)
                        pz = pC if ct % 2 == 0 else pD
                        for k in range(8):
                            e.matmul(pz[:, 0:512], WB1t(k, 16 + ct),
                                     ARB[:, 2 * XNBp + k * 512:2 * XNBp + (k + 1) * 512],
                                     start=(k == 0), stop=(k == 7))
                        e.drain().then_inc(mm, 1)
                    add("mm")
                    mmv = get("mm")

                    qsv_z = (get(psem) if ct < 2 else qfull23.get(gct - 2, 0))

                    @pr.on("scalar")
                    def _(e, ct=ct, mmv=mmv, qsv=qsv_z,
                          qsm=SEM[psem], ZSp=ZSp):
                        e.wait_ge(mm, mmv)
                        if qsv:
                            e.wait_ge(qsm, qsv)
                        pz = pC if ct % 2 == 0 else pD
                        e.activation(AR[:, ZSp:ZSp + 512], pz[:, 0:512],
                                     AF.Silu).then_inc(sp, 1)
                        e.drain()
                    add("sp")
                    sp_zsilu[gct] = get("sp")
                    spv = get("sp")

                    @pr.on("sync")
                    def _(e, spv=spv):
                        e.wait_ge(sp, spv)
                    dma("sync", sp_z[ct, :, (g - 2) * G:(g - 1) * G], AR[:, ZSp:ZSp + 512],
                        sem=psem)
                qfull23[gct] = get(psem)

            mm_glast[g] = get("mm")
            for kq in ([2 * (g - 1), 2 * (g - 1) + 1] if 1 <= g <= 4 else []):
                emit_quant1(kq)
            spv = get("sp")

            @pr.on("tensor")
            def _(e, g=g, spv=spv):
                e.wait_ge(sp, spv)
                for p_i, pz in enumerate((pE, pF, pG)):
                    for k in range(NCH):
                        e.matmul(pz[0:16, 0:512],
                                 ARB[:, 2 * XPWB + k * 48 + p_i * 16:2 * XPWB + k * 48 + (p_i + 1) * 16],
                                 XCBt(k), start=(k == 0), stop=(k == NCH - 1))
                e.drain().then_inc(mm, 1)
            add("mm")
            mmv = get("mm")

            @pr.on("vector")
            def _(e, g=g, mmv=mmv):
                e.wait_ge(mm, mmv)
                e.tensor_copy(projp[0:16, g * G:(g + 1) * G], pE[0:16, 0:512])
                e.tensor_copy(projp[32:48, g * G:(g + 1) * G], pF[0:16, 0:512])
                e.tensor_copy(projp[64:80, g * G:(g + 1) * G], pG[0:16, 0:512])
                e.drain().then_inc(ev, 1)
            add("ev")

        barrier()

        qlv_q = get("ql")

        UBB, BREP, CREP, SALL, DECR = 0, 4096, 8192, 12288, 20480
        XCS2, DTH4, CUMD, U4 = 36864, 37888, 38912, 39936
        YA4, ZS4, Y24, DTBG, DC0, Y2B, SQ2 = (40960, 41472, 42496, 43008, 43264,
                                              43280, 43792)

        TM4 = DECR + 8192

        def ubbv(p=0):
            return AR[:, DECR:DECR + 8192].rearrange("p (n t) -> p n t", n=16)

        def sallv():
            return AR[:, SALL:SALL + 8192].rearrange("p (n t) -> p n t", n=16)

        F4XN, F4XT, F4SH, F4HB = 256, 2304, 2816, 3840
        mm_tr4 = {}
        ev_n4 = {}
        ev_ev4 = {}
        evfb4 = {}
        sp_st4 = {}
        sp_gel4 = {}
        mm_w14 = {}
        qm_t4 = {}
        qi_s4 = {}
        mm_w1l4 = {}

        @pr.on("vector")
        def _(e, qlv_q=qlv_q):
            e.wait_ge(ql, qlv_q)

        ev_u = {}
        ev_scan = {}
        ev_y2 = {}
        gp_pool = {}
        gp_mult = {}
        sp_soft = {}
        sp_dec = {}
        qs_y2sp = {}
        mm_st = {}

        for tb in range(NG):
            town = tb >= 2
            ts0 = tb * G
            mmv = get("mm")
            evfl = get("ev")
            gpfl = get("gp")
            qsfl = get("qs")

            qtfl = get("qt")

            @pr.on("gpsimd")
            def _(e, mmv=mmv, evfl=evfl, qtfl=qtfl, gpfl=gpfl):
                e.wait_ge(mm, mmv)
                e.wait_ge(ev, evfl)
                e.wait_ge(gp, gpfl)
                e.wait_ge(qt, qtfl)
            dma("gpsimd", ARB[0:1, 2 * DECR:2 * DECR + 8192],
                projp[32:48, ts0:ts0 + G], sem="qg")
            if town:
                dma("gpsimd", ARB[0:1, 2 * DECR + 8192:2 * DECR + 16384],
                    projp[64:80, ts0:ts0 + G], sem="qg")
            qgv = get("qg")
            evv = get("ev")

            @pr.on("tensor")
            def _(e, qgv=qgv, evv=evv):
                e.wait_ge(qg, qgv)
                for n in range(16):
                    if evv + n - 1 > 0:
                        e.wait_ge(ev, evv + n - 1)
                    pb = pE if n % 2 == 0 else pF
                    e.matmul(pb[:, 0:512], ones1b[:],
                             ARB[0:1, 2 * DECR + n * 512:2 * DECR + (n + 1) * 512],
                             start=True, stop=True)
                    e.drain().then_inc(mm, 1)
            add("mm", 16)
            mmv = get("mm") - 16

            @pr.on("vector")
            def _(e, mmv=mmv):
                for n in range(16):
                    e.wait_ge(mm, mmv + n + 1)
                    pb = pE if n % 2 == 0 else pF
                    e.tensor_copy(ARB[:, 2 * BREP + n * 512:2 * BREP + (n + 1) * 512],
                                  pb[:, 0:512])
                    e.drain().then_inc(ev, 1)
            add("ev", 16)
            if town:
                evv = get("ev")

                @pr.on("tensor")
                def _(e, evv=evv):
                    for n in range(16):
                        e.wait_ge(ev, evv + n - 1)
                        pb = pE if n % 2 == 0 else pF
                        e.matmul(pb[:, 0:512], ones1b[:],
                                 ARB[0:1, 2 * DECR + 8192 + n * 512:2 * DECR + 8192 + (n + 1) * 512],
                                 start=True, stop=True)
                        e.drain().then_inc(mm, 1)
                add("mm", 16)
                mmv = get("mm") - 16

                @pr.on("vector")
                def _(e, mmv=mmv):
                    for n in range(16):
                        e.wait_ge(mm, mmv + n + 1)
                        pb = pE if n % 2 == 0 else pF
                        e.tensor_copy(ARB[:, 2 * CREP + n * 512:2 * CREP + (n + 1) * 512],
                                      pb[:, 0:512])
                        e.drain().then_inc(ev, 1)
                add("ev", 16)

            evv = get("ev")
            mmdt = get("mm")

            @pr.on("scalar")
            def _(e, ts0=ts0, evv=evv, mmdt=mmdt):
                e.wait_ge(ev, evv)
                e.wait_ge(mm, mmdt)
                e.copy(ARB[0:16, 2 * DTBG:2 * DTBG + 512],
                       projp[0:16, ts0:ts0 + G]).then_inc(sp, 1)
            add("sp")
            spdtb = get("sp")

            def emit_front(ct):
                gidx = tb * NCH + ct
                p = ct % 2
                qsem = "qs" if p == 0 else "qc"
                gate = ev_y2.get(gidx - 2, ev_u.get(gidx - 2, 0))

                @pr.on("sync")
                def _(e, gate=gate):
                    if gate:
                        e.wait_ge(ev, gate)
                dma("sync", AR[:, XCS2 + p * 512:XCS2 + (p + 1) * 512],
                    sp_xc[ct, :, ts0:ts0 + G], sem=qsem)
                if town:
                    dma("sync", AR[:, ZS4 + p * 512:ZS4 + (p + 1) * 512],
                        sp_z[ct, :, ts0 - WIN:ts0 - WIN + G], sem=qsem)
                qsv = get(qsem)

                spg = sp_soft.get(gidx - 2, spdtb)

                @pr.on("tensor")
                def _(e, ct=ct, p=p, spg=max(spg, spdtb)):
                    e.wait_ge(sp, spg)
                    pb = pA if p == 0 else pB
                    e.matmul(pb[:, 0:512], wdtb[0:16, ct * 128:(ct + 1) * 128],
                             ARB[0:16, 2 * DTBG:2 * DTBG + 512],
                             start=True, stop=True)
                    e.drain().then_inc(mm, 1)
                add("mm")
                mmv = get("mm")

                gu = ev_scan.get(gidx - 2, ev_u.get(gidx - 2, 0))

                @pr.on("scalar")
                def _(e, ct=ct, p=p, mmv=mmv, gu=gu):
                    e.wait_ge(mm, mmv)
                    if gu:
                        e.wait_ge(ev, gu)
                    pb = pA if p == 0 else pB
                    T = AR[:, CUMD + p * 512:CUMD + (p + 1) * 512]
                    e.activation(T, pb[:, 0:512], AF.Exp, bias=VC("bdt", 1, ct))
                    e.drain()
                    e.activation(AR[:, DTH4 + p * 512:DTH4 + (p + 1) * 512], T,
                                 AF.Ln, bias=1.0).then_inc(sp, 1)
                add("sp")
                sp_soft[gidx] = get("sp")

                @pr.on("vector")
                def _(e, ct=ct, p=p, spv=sp_soft[gidx]):
                    e.wait_ge(sp, spv)
                    e.tensor_tensor_scan(AR[:, CUMD + p * 512:CUMD + (p + 1) * 512],
                                         cmask[:, :],
                                         AR[:, DTH4 + p * 512:DTH4 + (p + 1) * 512],
                                         0.0, op0=ALU.mult, op1=ALU.add)
                    e.drain().then_inc(ev, 1)
                add("ev")
                evcum = get("ev")

                @pr.on("scalar")
                def _(e, ct=ct, p=p, evv=evcum):
                    e.wait_ge(ev, evv)
                    T = AR[:, CUMD + p * 512:CUMD + (p + 1) * 512]
                    e.activation(T, T, AF.Exp, scale=VC("A", 1, ct)).then_inc(sp, 1)
                add("sp")
                sp_dec[gidx] = get("sp")

                @pr.on("vector")
                def _(e, ct=ct, p=p, spd=sp_dec[gidx], sps=sp_soft[gidx],
                      qsv=qsv, qsm=SEM[qsem]):
                    e.wait_ge(sp, sps)
                    e.wait_ge(qsm, qsv)
                    e.tensor_mul(AR[:, U4 + p * 512:U4 + (p + 1) * 512],
                                 AR[:, DTH4 + p * 512:DTH4 + (p + 1) * 512],
                                 AR[:, XCS2 + p * 512:XCS2 + (p + 1) * 512])
                    e.drain()
                    e.tensor_tensor(ubbv(p),
                                    AR[:, U4 + p * 512:U4 + (p + 1) * 512]
                                    .unsqueeze(1).broadcast_to((128, 16, 512)),
                                    ARB[:, 2 * BREP:2 * BREP + 8192]
                                    .rearrange("p (n t) -> p n t", n=16), op=ALU.mult)
                    e.drain()
                    e.wait_ge(sp, spd)
                    T = AR[:, CUMD + p * 512:CUMD + (p + 1) * 512]
                    e.tensor_copy(AR[:, DC0 + p * 8:DC0 + p * 8 + 1], T[:, 0:1])
                    e.drain()
                    e.memset(T[:, 0:1], 0.0)
                    e.drain()
                    e.scalar_tensor_tensor(ubbv(p)[:, :, 0], carry[:, ct * 16:ct * 16 + 16],
                                           AR[:, DC0 + p * 8:DC0 + p * 8 + 1],
                                           ubbv(p)[:, :, 0], op0=ALU.mult, op1=ALU.add)
                    e.drain().then_inc(ev, 1)
                add("ev")
                ev_u[gidx] = get("ev")

                @pr.on("vector")
                def _(e, ct=ct, p=p):
                    T = AR[:, CUMD + p * 512:CUMD + (p + 1) * 512]
                    for n in range(16):
                        e.tensor_tensor_scan(AR[:, SALL + n * 512:SALL + (n + 1) * 512],
                                             T,
                                             AR[:, DECR + n * 512:DECR + (n + 1) * 512],
                                             0.0, op0=ALU.mult, op1=ALU.add)
                    e.drain()
                    e.tensor_copy(carry[:, ct * 16:ct * 16 + 16], sallv()[:, :, 511])
                    e.drain().then_inc(ev, 1)
                add("ev")
                ev_scan[gidx] = get("ev")

                if town:
                    qy = get(qsem)

                    @pr.on("vector")
                    def _(e, ct=ct, p=p, qy=qy, qsm=SEM[qsem],
                          mmg=mm_st.get(gidx - 2, 0)):
                        TMv = AR[:, TM4:TM4 + 8192]
                        e.tensor_mul(TMv, AR[:, SALL:SALL + 8192],
                                     ARB[:, 2 * CREP:2 * CREP + 8192])
                        e.drain()
                        e.tensor_add(TMv[:, 0:4096], TMv[:, 0:4096], TMv[:, 4096:8192])
                        e.drain()
                        e.tensor_add(TMv[:, 0:2048], TMv[:, 0:2048], TMv[:, 2048:4096])
                        e.drain()
                        e.tensor_add(TMv[:, 0:1024], TMv[:, 0:1024], TMv[:, 1024:2048])
                        e.drain()
                        e.tensor_add(AR[:, YA4:YA4 + 512], TMv[:, 0:512], TMv[:, 512:1024])
                        e.drain()
                        Y2v = AR[:, Y24:Y24 + 512]
                        e.scalar_tensor_tensor(Y2v, AR[:, XCS2 + p * 512:XCS2 + (p + 1) * 512],
                                               VC("Dsk", 1, ct), AR[:, YA4:YA4 + 512],
                                               op0=ALU.mult, op1=ALU.add)
                        e.drain()
                        e.tensor_mul(Y2v, Y2v, AR[:, ZS4 + p * 512:ZS4 + (p + 1) * 512])
                        e.drain()
                        e.wait_ge(qsm, qy)
                        e.tensor_copy(ARB[:, 2 * (Y2B + p * 256):2 * (Y2B + p * 256) + 512], Y2v)
                        e.drain()
                        if mmg:
                            e.wait_ge(mm, mmg)
                        SQp = SQ2 if p == 0 else 0
                        e.tensor_mul(ARB[:, 2 * SQp:2 * SQp + 512], Y2v, Y2v)
                        e.drain().then_inc(ev, 1)
                    add("ev")
                ev_y2[gidx] = get("ev")

            def emit_tail(ct):
                gidx = tb * NCH + ct
                p = ct % 2
                qsem = "qs" if p == 0 else "qc"
                if town:
                    @pr.on("tensor")
                    def _(e, ct=ct, p=p, spv=ev_y2[gidx]):
                        e.wait_ge(ev, spv)
                        SQp = SQ2 if p == 0 else 0
                        e.matmul(pC[0:1, 0:512], onescb[:],
                                 ARB[:, 2 * (Y2B + p * 256):2 * (Y2B + p * 256) + 512],
                                 start=(ct == 0), stop=(ct == NCH - 1))
                        e.matmul(pD[0:1, 0:512], onescb[:], ARB[:, 2 * SQp:2 * SQp + 512],
                                 start=(ct == 0), stop=(ct == NCH - 1))
                        e.drain().then_inc(mm, 1)
                    add("mm")
                    mm_st[gidx] = get("mm")

                    @pr.on("sync")
                    def _(e, mmv=mm_st[gidx]):
                        e.wait_ge(mm, mmv)
                    dma("sync", sp_y2[ct, :, ts0 - WIN:ts0 - WIN + G],
                        ARB[:, 2 * (Y2B + p * 256):2 * (Y2B + p * 256) + 512], sem=qsem)
                    qs_y2sp[gidx] = get(qsem)

            def emit_ln1(gi, tt):
                tg = gi * 4 + tt

                @pr.on("sync")
                def _(e, g=mm_tr4.get(tg - 1, 0), tg=tg):
                    if tg == 0:
                        e.wait_ge(ql, qlv_q)
                    if g:
                        e.wait_ge(mm, g)
                dma("sync", ARB[:, 2 * F4XT:2 * F4XT + 1024],
                    sp_xn[WIN + gi * G + tt * 128:WIN + gi * G + (tt + 1) * 128, :],
                    sem="yr")
                yv = get("yr")

                @pr.on("scalar")
                def _(e, yv=yv, tt=tt, mg=mm_w1l4.get(gi - 1, 0),
                      spg=sp_gel4.get((gi - 1) * 32 + 31, 0),
                      qig=qi_s4.get((gi - 1) * 32 + 31, 0)):
                    e.wait_ge(yr, yv)
                    if tt == 0 and mg:
                        e.wait_ge(mm, mg)
                    if tt == 0 and spg:
                        e.wait_ge(sp, spg)
                    if tt == 0 and qig:
                        e.wait_ge(qh, qig)
                    XT = ARB[:, 2 * F4XT:2 * F4XT + 1024]
                    e.activation(AR[:, F4SH:F4SH + 1024], XT, AF.Copy,
                                 accum_out=VC("l1r", 1, 0))
                    e.drain()
                    e.activation(AR[:, F4SH:F4SH + 1024], XT, AF.Square,
                                 accum_out=VC("l1r", 1, 1))
                    e.drain()
                    e.activation(VC("l1r", 1, 2), VC("l1r", 1, 0), AF.Copy,
                                 scale=1.0 / D)
                    e.drain()
                    e.activation(VC("l1r", 1, 6), VC("l1r", 1, 2), AF.Square)
                    e.drain()
                    e.activation(VC("l1r", 1, 7), VC("l1r", 1, 6), AF.Copy,
                                 scale=-1.0, bias=LN_EPS)
                    e.drain()
                    e.activation(VC("l1r", 1, 4), VC("l1r", 1, 1), AF.Sqrt,
                                 scale=1.0 / D, bias=VC("l1r", 1, 7))
                    e.drain().then_inc(sp, 1)
                add("sp")
                sp_st4[tg] = get("sp")
                spv4 = get("sp")

                @pr.on("vector")
                def _(e, spv=spv4):
                    e.wait_ge(sp, spv)
                    rs = VC("l1r", 1, 5)
                    e.reciprocal(rs, VC("l1r", 1, 4))
                    e.drain()
                    XT = ARB[:, 2 * F4XT:2 * F4XT + 1024]
                    e.tensor_scalar(XT, XT, VC("l1r", 1, 2), rs,
                                    op0=ALU.subtract, op1=ALU.mult)
                    e.drain().then_inc(ev, 1)
                add("ev")
                ev_n4[tg] = get("ev")
                evb = get("ev")

                @pr.on("tensor")
                def _(e, evn=ev_n4[tg], evb=evb, gpr=ev_ev4.get(tg - 1, 0),
                      spg=sp_gel4.get(gi * 32 + 31, 0) if tt == 0 else 0):
                    e.wait_ge(ev, evn)
                    if gpr:
                        e.wait_ge(ev, gpr)
                    if spg:
                        e.wait_ge(sp, spg)
                    for f in range(8):
                        if evb + f - 1 > 0:
                            e.wait_ge(ev, evb + f - 1)
                        pb = pG if f % 2 == 0 else pH
                        e.transpose(pb[:, 0:64].bitcast(BF16),
                                    ARB[:, 2 * F4XT + f * 128:2 * F4XT + (f + 1) * 128],
                                    identb[:])
                        e.drain().then_inc(mm, 1)
                add("mm", 8)
                mm_tr4[tg] = get("mm")
                mmv4 = get("mm") - 8

                @pr.on("vector")
                def _(e, tt=tt, mmv=mmv4, mg=mm_w1l4.get(gi - 1, 0), tg=tg):
                    if tg % 4 == 0 and mg:
                        e.wait_ge(mm, mg)
                    for f in range(8):
                        e.wait_ge(mm, mmv + f + 1)
                        pb = pG if f % 2 == 0 else pH
                        e.tensor_scalar(
                            ARB[:, 2 * F4XN + f * 512 + tt * 128:2 * F4XN + f * 512 + (tt + 1) * 128],
                            pb[:, 0:64].bitcast(BF16), VC("g1", 1, f),
                            VC("bt1", 1, f), op0=ALU.mult, op1=ALU.add)
                        e.drain().then_inc(ev, 1)
                add("ev", 8)
                ev_ev4[tg] = get("ev")
                if tt == 3:
                    evfb4[gi] = get("ev")

            def emit_mt(gi, mt):
                gmt = gi * 32 + mt

                @pr.on("gpsimd")
                def _(e, mt=mt, spg=sp_st4.get(gi * 4 + 3, 0),
                      mg=mm_w14.get(gmt - 1, 0)):
                    if mt == 0:
                        e.wait_ge(sp, spg)
                    if mg:
                        e.wait_ge(mm, mg)
                dma("gpsimd",
                    ARB[:, 2 * F4SH:2 * F4SH + 1024]
                    .rearrange("p (k c) -> p k c", c=128),
                    wq1d[:, :, mt * 128:(mt + 1) * 128]
                    .rearrange("k p c -> p k c"), sem="qm")
                qm_t4[gmt] = get("qm")

                @pr.on("tensor")
                def _(e, mt=mt, qmv=qm_t4[gmt], evf=evfb4[gi],
                      spfree=sp_gel4.get(gmt - 2, 0)):
                    e.wait_ge(qm, qmv)
                    if mt < 2:
                        e.wait_ge(ev, evf)
                    if spfree:
                        e.wait_ge(sp, spfree)
                    pb = pG if mt % 2 == 0 else pH
                    for k in range(8):
                        e.matmul(pb[:, 0:512],
                                 ARB[:, 2 * F4SH + k * 128:2 * F4SH + (k + 1) * 128],
                                 ARB[:, 2 * F4XN + k * 512:2 * F4XN + (k + 1) * 512],
                                 start=(k == 0), stop=(k == 7))
                    e.drain().then_inc(mm, 1)
                add("mm")
                mm_w14[gmt] = get("mm")
                if mt == 31:
                    mm_w1l4[gi] = get("mm")

                HB4p = F4HB if mt % 2 == 0 else 3584

                hsem = "qi" if mt % 2 == 0 else "qh"

                @pr.on("scalar")
                def _(e, mt=mt, mmv=mm_w14[gmt], qiv=qi_s4.get(gmt - 2, 0),
                      HB4p=HB4p, hsm=SEM[hsem]):
                    e.wait_ge(mm, mmv)
                    if qiv:
                        e.wait_ge(hsm, qiv)
                    pb = pG if mt % 2 == 0 else pH
                    e.activation(ARB[:, 2 * HB4p:2 * HB4p + 512], pb[:, 0:512],
                                 AF.Gelu, bias=VC("b1c", 1, mt))
                    e.drain().then_inc(sp, 1)
                add("sp")
                sp_gel4[gmt] = get("sp")

                @pr.on("gpsimd")
                def _(e, spv=sp_gel4[gmt]):
                    e.wait_ge(sp, spv)
                dma("gpsimd", sp_h[mt, :, gi * G:(gi + 1) * G],
                    ARB[:, 2 * HB4p:2 * HB4p + 512], sem=hsem)
                qi_s4[gmt] = get(hsem)

            for ctx in range(NCH + 1):
                if ctx < NCH:
                    emit_front(ctx)
                if ctx >= 1:
                    emit_tail(ctx - 1)
                if town and ctx < NCH:
                    gi6 = tb - 2
                    if ctx < 4:
                        emit_ln1(gi6, ctx)
                    else:
                        mtb = (ctx - 4) * 3
                        for j in range(3):
                            if mtb + j < 32:
                                emit_mt(gi6, mtb + j)

            if town:
                mmv = get("mm")
                evv = get("ev")

                @pr.on("vector")
                def _(e, mmv=mmv, evv=evv):
                    e.wait_ge(mm, mmv)
                    e.wait_ge(ev, evv)
                    MU = AR[0:1, DECR:DECR + 512]
                    S2 = AR[0:1, DECR + 512:DECR + 1024]
                    VA = AR[0:1, DECR + 1024:DECR + 1536]
                    e.tensor_scalar_mul(MU, pC[0:1, 0:512], 1.0 / DI)
                    e.tensor_scalar_mul(S2, pD[0:1, 0:512], 1.0 / DI)
                    e.drain()
                    e.tensor_mul(VA, MU, MU)
                    e.drain()
                    e.tensor_sub(VA, S2, VA)
                    e.drain().then_inc(ev, 1)
                add("ev")
                evv = get("ev")

                @pr.on("scalar")
                def _(e, evv=evv):
                    e.wait_ge(ev, evv)
                    e.activation(AR[0:1, DECR + 1536:DECR + 2048],
                                 AR[0:1, DECR + 1024:DECR + 1536], AF.Sqrt,
                                 bias=LN_EPS).then_inc(sp, 1)
                add("sp")
                spv = get("sp")

                @pr.on("vector")
                def _(e, spv=spv):
                    e.wait_ge(sp, spv)
                    e.reciprocal(AR[0:1, DECR + 512:DECR + 1024],
                                 AR[0:1, DECR + 1536:DECR + 2048])
                    e.drain().then_inc(ev, 1)
                add("ev")
                evv = get("ev")

                @pr.on("sync")
                def _(e, evv=evv):
                    e.wait_ge(ev, evv)
                dma("sync", sp_st[0:1, ts0 - WIN:ts0 - WIN + G], AR[0:1, DECR:DECR + 512],
                    sem="qt")
                dma("sync", sp_st[1:2, ts0 - WIN:ts0 - WIN + G],
                    AR[0:1, DECR + 512:DECR + 1024], sem="qt")
            if p4bar:
                barrier()


        WOB, Y2G, MRS, RSR, MIXG, RW5, QW2 = (0, 8192, 12288, 12800, 13312,
                                              17408, 20480)

        evv = get("ev")

        @pr.on("vector")
        def _(e, evv=evv):
            e.wait_ge(ev, evv)
            rv = rb2.rearrange("p (k w) -> p k w", w=16)
            e.memset(rb2[:, :], 0.0)
            e.drain()
            e.tensor_copy(rv[:, 0:16, 0], VC("lng", 16))
            e.tensor_copy(rv[:, 0:16, 1], VC("lnb", 16))
            e.drain().then_inc(ev, 1)
        add("ev")

        qtP5 = get("qt")
        qcP5 = get("qc")
        evP4 = get("ev")
        mmP4 = get("mm")
        qiP4, qmP4, yrP4, spP4 = get("qi"), get("qm"), get("yr"), get("sp")
        qhP4 = get("qh")
        QW2B = 18432
        ev_cast5 = {}
        for k in range(NCH):
            QWp = QW2 if k % 2 == 0 else QW2B
            lsem = "qs" if k % 2 == 0 else "qc"

            @pr.on("sync")
            def _(e, evv=ev_cast5.get(k - 2, evP4), qtP5=qtP5, qcP5=qcP5, k=k):
                e.wait_ge(ev, evv)
                if k < 2:
                    e.wait_ge(qt, qtP5)
                    e.wait_ge(qc, qcP5)
            dma("sync", AR[:, QWp:QWp + 1024], W_out[k * 128:(k + 1) * 128, :],
                sem=lsem)
            qsv = get(lsem)

            @pr.on("vector")
            def _(e, k=k, qsv=qsv, QWp=QWp, qsm=SEM[lsem], mmP4=mmP4):
                if k < 2:
                    e.wait_ge(mm, mmP4)
                    e.wait_ge(qi, qiP4)
                    e.wait_ge(qh, qhP4)
                    e.wait_ge(qm, qmP4)
                    e.wait_ge(yr, yrP4)
                    e.wait_ge(sp, spP4)
                e.wait_ge(qsm, qsv)
                e.tensor_copy(ARB[:, 2 * WOB + k * 1024:2 * WOB + (k + 1) * 1024],
                              AR[:, QWp:QWp + 1024])
                e.drain().then_inc(ev, 1)
            add("ev")
            ev_cast5[k] = get("ev")
        evv = get("ev")

        evb5 = get("ev")

        @pr.on("tensor")
        def _(e, evv=evv, evb5=evb5):
            e.wait_ge(ev, evv)
            for m in range(8):
                if m >= 2:
                    e.wait_ge(ev, evb5 + m - 1)
                pb = pE if m % 2 == 0 else pF
                for k in range(NCH):
                    e.matmul(pb[0:128, 0:16],
                             ARB[:, 2 * WOB + k * 1024 + m * 128:2 * WOB + k * 1024 + (m + 1) * 128],
                             rb2.rearrange("p (k w) -> p k w", w=16)[:, k, :],
                             start=(k == 0), stop=(k == NCH - 1))
                e.drain().then_inc(mm, 1)
        add("mm", 8)
        mmv = get("mm") - 8

        @pr.on("vector")
        def _(e, mmv=mmv):
            for m in range(8):
                e.wait_ge(mm, mmv + m + 1)
                pb = pE if m % 2 == 0 else pF
                e.tensor_copy(VC("Wc", 1, m), pb[:, 0:1])
                e.tensor_copy(VC("Wb", 1, m), pb[:, 1:2])
                e.drain().then_inc(ev, 1)
            e.tensor_scalar_mul(VC("Wcn", 8), VC("Wc", 8), -1.0)
            e.drain()
            for k in range(NCH):
                e.tensor_scalar_mul(ARB[:, 2 * WOB + k * 1024:2 * WOB + (k + 1) * 1024],
                                    ARB[:, 2 * WOB + k * 1024:2 * WOB + (k + 1) * 1024],
                                    VC("lng", 1, k))
            e.drain().then_inc(ev, 1)
        add("ev", 9)
        if p4bar:
            barrier()

        qcP4 = get("qc")
        qtP4 = get("qt")
        Y2G2 = 21504
        evprep5 = get("ev")
        ts5 = {}
        mm_ml5 = {}

        def y2buf(gi):
            return Y2G if gi % 2 == 0 else Y2G2

        def emit_y2rel(gi, gate_mm):
            tsem = "qt" if gi % 2 == 0 else "yr"
            Yb = y2buf(gi)

            @pr.on("sync")
            def _(e, gate_mm=gate_mm):
                if gate_mm:
                    e.wait_ge(mm, gate_mm)
            for ct in range(NCH):
                dma("sync", ARB[:, 2 * Yb + ct * 512:2 * Yb + (ct + 1) * 512],
                    sp_y2[ct, :, gi * G:(gi + 1) * G], sem=tsem)
            ts5[gi] = get(tsem)

        for gi in range(4):
            ts0 = gi * G
            evv = get("ev")

            @pr.on("sync")
            def _(e, evv=evv, qcP4=qcP4, qtP4=qtP4):
                e.wait_ge(ev, evv)
                e.wait_ge(qc, qcP4)
                e.wait_ge(qt, qtP4)
            if gi == 0:
                emit_y2rel(0, 0)
            dma("sync", AR[0:1, RW5:RW5 + 512], sp_st[0:1, ts0:ts0 + G])
            dma("sync", AR[0:1, RW5 + 512:RW5 + 1024], sp_st[1:2, ts0:ts0 + G])
            qsv = get("qs")

            @pr.on("vector")
            def _(e, qsv=qsv):
                e.wait_ge(qs, qsv)
                e.tensor_mul(AR[0:1, RW5:RW5 + 512], AR[0:1, RW5:RW5 + 512],
                             AR[0:1, RW5 + 512:RW5 + 1024])
                e.drain().then_inc(ev, 1)
            add("ev")
            evv = get("ev")

            @pr.on("tensor")
            def _(e, evv=evv):
                e.wait_ge(ev, evv)
                e.matmul(pF[:, 0:512], ones1[:], AR[0:1, RW5:RW5 + 512],
                         start=True, stop=True)
                e.matmul(pG[:, 0:512], ones1[:], AR[0:1, RW5 + 512:RW5 + 1024],
                         start=True, stop=True)
                e.drain().then_inc(mm, 1)
            add("mm")
            mmv = get("mm")

            @pr.on("vector")
            def _(e, mmv=mmv):
                e.wait_ge(mm, mmv)
                e.tensor_copy(AR[:, MRS:MRS + 512], pF[:, 0:512])
                e.tensor_copy(AR[:, RSR:RSR + 512], pG[:, 0:512])
                e.drain().then_inc(ev, 1)
            add("ev")
            evrep = get("ev")

            ev_fold5 = {}
            Yb5 = y2buf(gi)
            tsm5 = SEM["qt" if gi % 2 == 0 else "yr"]
            PB5 = (pA, pB, pC, pD)
            for m in range(8):
                evv = ev_fold5.get(m - 4, evprep5)

                @pr.on("tensor")
                def _(e, m=m, evv=evv, tsv=ts5[gi], tsm5=tsm5, Yb5=Yb5):
                    if m < 4:
                        e.wait_ge(tsm5, tsv)
                    e.wait_ge(ev, evv)
                    pb = PB5[m % 4]
                    for k in range(NCH):
                        e.matmul(pb[:, 0:512],
                                 ARB[:, 2 * WOB + k * 1024 + m * 128:2 * WOB + k * 1024 + (m + 1) * 128],
                                 ARB[:, 2 * Yb5 + k * 512:2 * Yb5 + (k + 1) * 512],
                                 start=(k == 0), stop=(k == NCH - 1))
                    e.drain().then_inc(mm, 1)
                add("mm")
                mmv = get("mm")

                @pr.on("vector")
                def _(e, m=m, mmv=mmv, evrep=evrep):
                    e.wait_ge(mm, mmv)
                    e.wait_ge(ev, evrep)
                    pb = PB5[m % 4]
                    MX = AR[:, MIXG + m * 512:MIXG + (m + 1) * 512]
                    e.tensor_tensor(MX, pb[:, 0:512], AR[:, RSR:RSR + 512], op=ALU.mult)
                    e.drain()
                    e.scalar_tensor_tensor(MX, AR[:, MRS:MRS + 512], VC("Wcn", 1, m),
                                           MX, op0=ALU.mult, op1=ALU.add)
                    e.drain()
                    e.tensor_scalar_add(MX, MX, VC("Wb", 1, m))
                    e.drain().then_inc(ev, 1)
                add("ev")
                ev_fold5[m] = get("ev")
            mm_ml5[gi] = get("mm")
            if gi < 3:
                emit_y2rel(gi + 1, mm_ml5.get(gi - 1, 0))
            evv = get("ev")

            @pr.on("sync")
            def _(e, evv=evv):
                e.wait_ge(ev, evv)
            for f in range(8):
                dma("sync", sp_mix[f, :, ts0:ts0 + G],
                    AR[:, MIXG + f * 512:MIXG + (f + 1) * 512])
            if p4bar:
                barrier()

        barrier()

        barrier()

        WQ2, HGB, R17, R27, MIX2, OG, XTOK = (0, 16384, 24576, 25088, 25600,
                                              29696, 30208)
        QWA2, QP2, QN2 = 31232, 33280, 34304
        OT7 = 31232
        RW7 = 30208
        RW7 = 33280

        evv = get("ev")

        @pr.on("vector")
        def _(e, evv=evv):
            e.wait_ge(ev, evv)
            rv = rb2.rearrange("p (k w) -> p k w", w=16)
            e.memset(rb2[:, :], 0.0)
            e.drain()
            e.tensor_copy(rv[:, 0:32, 0], VC("g2", 32))
            e.tensor_copy(rv[:, 0:32, 1], VC("bt2", 32))
            e.drain().then_inc(ev, 1)
        add("ev")

        ev_q7 = {}
        evq0 = get("ev")
        for k in range(32):
            QWp = QWA2 if k % 2 == 0 else QWA2 + 1024
            lsem = "qs" if k % 2 == 0 else "qc"

            @pr.on("sync")
            def _(e, evv=ev_q7.get(k - 2, evq0)):
                e.wait_ge(ev, evv)
            dma("sync", AR[:, QWp:QWp + 1024], w2T[k * 128:(k + 1) * 128, :],
                sem=lsem)
            qsv = get(lsem)

            @pr.on("vector")
            def _(e, k=k, qsv=qsv, QWp=QWp, qsm=SEM[lsem]):
                e.wait_ge(qsm, qsv)
                e.tensor_scalar(AR[:, QP2:QP2 + 1024], AR[:, QWp:QWp + 1024],
                                VC("hm2"), VC("m2"), op0=ALU.is_gt, op1=ALU.mult)
                e.tensor_scalar(AR[:, QN2:QN2 + 1024], AR[:, QWp:QWp + 1024],
                                VC("hm2n"), VC("m2"), op0=ALU.is_lt, op1=ALU.mult)
                e.drain()
                e.tensor_tensor(ARB[:, 2 * WQ2 + k * 1024:2 * WQ2 + (k + 1) * 1024],
                                AR[:, QP2:QP2 + 1024], AR[:, QN2:QN2 + 1024],
                                op=ALU.subtract)
                e.drain().then_inc(ev, 1)
            add("ev")
            ev_q7[k] = get("ev")
        evv = get("ev")

        evb7 = get("ev")

        @pr.on("tensor")
        def _(e, evv=evv, evb7=evb7):
            e.wait_ge(ev, evv)
            for m in range(8):
                if m >= 2:
                    e.wait_ge(ev, evb7 + m - 1)
                pb = pE if m % 2 == 0 else pF
                for k in range(32):
                    e.matmul(pb[0:128, 0:16],
                             ARB[:, 2 * WQ2 + k * 1024 + m * 128:2 * WQ2 + k * 1024 + (m + 1) * 128],
                             rb2.rearrange("p (k w) -> p k w", w=16)[:, k, :],
                             start=(k == 0), stop=(k == 31))
                e.drain().then_inc(mm, 1)
        add("mm", 8)
        mmv = get("mm") - 8

        @pr.on("vector")
        def _(e, mmv=mmv):
            for m in range(8):
                e.wait_ge(mm, mmv + m + 1)
                pb = pE if m % 2 == 0 else pF
                e.tensor_copy(VC("W2c", 1, m), pb[:, 0:1])
                e.tensor_copy(VC("W2b", 1, m), pb[:, 1:2])
                e.drain().then_inc(ev, 1)
            e.tensor_scalar_mul(VC("W2cn", 8), VC("W2c", 8), -1.0)
            e.tensor_add(VC("W2bb", 8), VC("W2b", 8), VC("b2c", 8))
            e.drain()
            for k in range(32):
                e.tensor_scalar_mul(ARB[:, 2 * WQ2 + k * 1024:2 * WQ2 + (k + 1) * 1024],
                                    ARB[:, 2 * WQ2 + k * 1024:2 * WQ2 + (k + 1) * 1024],
                                    VC("g2", 1, k))
            e.drain().then_inc(ev, 1)
        add("ev", 9)

        ev_add8 = {}
        st_q8 = {}
        mm_m7 = {}
        for gi in range(4):
            ts0 = gi * G
            evv = get("ev")

            @pr.on("gpsimd")
            def _(e, g7=mm_m7.get(gi - 1, 0)):
                if g7:
                    e.wait_ge(mm, g7)
            for ct in range(32):
                dma("gpsimd", ARB[:, 2 * HGB + ct * 512:2 * HGB + (ct + 1) * 512],
                    sp_h[ct, :, ts0:ts0 + G], sem="qg")
            qg7 = get("qg")

            @pr.on("sync")
            def _(e, evv=evv):
                e.wait_ge(ev, evv)
            for f in range(8):
                dma("sync", AR[:, MIX2 + f * 512:MIX2 + (f + 1) * 512],
                    sp_mix[f, :, ts0:ts0 + G])
            qsv = get("qs")

            SQH7 = 36352
            sq7_sp = {}
            st7_mm = {}
            for ct in range(32):
                p7 = ct % 2
                sqs = SQH7 + p7 * 256

                @pr.on("scalar")
                def _(e, ct=ct, sqs=sqs, qsv=qg7, mmg=st7_mm.get(ct - 2, 0)):
                    if ct == 0:
                        e.wait_ge(qg, qsv)
                    if mmg:
                        e.wait_ge(mm, mmg)
                    e.activation(ARB[:, 2 * sqs:2 * sqs + 512],
                                 ARB[:, 2 * HGB + ct * 512:2 * HGB + (ct + 1) * 512],
                                 AF.Square)
                    e.drain().then_inc(sp, 1)
                add("sp")
                sq7_sp[ct] = get("sp")

                @pr.on("tensor")
                def _(e, ct=ct, sqs=sqs, spv=sq7_sp[ct], qsv=qg7):
                    if ct == 0:
                        e.wait_ge(qg, qsv)
                    e.matmul(pC[0:1, 0:512], onescb[:],
                             ARB[:, 2 * HGB + ct * 512:2 * HGB + (ct + 1) * 512],
                             start=(ct == 0), stop=(ct == 31))
                    e.wait_ge(sp, spv)
                    e.matmul(pD[0:1, 0:512], onescb[:], ARB[:, 2 * sqs:2 * sqs + 512],
                             start=(ct == 0), stop=(ct == 31))
                    e.drain().then_inc(mm, 1)
                add("mm")
                st7_mm[ct] = get("mm")
            mmv = get("mm")

            @pr.on("vector")
            def _(e, mmv=mmv, qsv=qsv):
                e.wait_ge(mm, mmv)
                e.wait_ge(qs, qsv)
                MU = AR[0:1, RW7:RW7 + 512]
                S2 = AR[0:1, RW7 + 512:RW7 + 1024]
                VA = AR[0:1, 36864:37376]
                e.tensor_scalar_mul(MU, pC[0:1, 0:512], 1.0 / FH)
                e.tensor_scalar_mul(S2, pD[0:1, 0:512], 1.0 / FH)
                e.drain()
                e.tensor_mul(VA, MU, MU)
                e.drain()
                e.tensor_sub(VA, S2, VA)
                e.drain().then_inc(ev, 1)
            add("ev")
            evv = get("ev")

            @pr.on("scalar")
            def _(e, evv=evv):
                e.wait_ge(ev, evv)
                e.activation(AR[0:1, 37376:37888],
                             AR[0:1, 36864:37376], AF.Sqrt, bias=LN_EPS)
                e.drain().then_inc(sp, 1)
            add("sp")
            spv = get("sp")

            @pr.on("vector")
            def _(e, spv=spv):
                e.wait_ge(sp, spv)
                e.reciprocal(AR[0:1, RW7 + 512:RW7 + 1024], AR[0:1, 37376:37888])
                e.drain()
                e.tensor_mul(AR[0:1, RW7:RW7 + 512], AR[0:1, RW7:RW7 + 512],
                             AR[0:1, RW7 + 512:RW7 + 1024])
                e.drain().then_inc(ev, 1)
            add("ev")
            evv = get("ev")

            @pr.on("tensor")
            def _(e, evv=evv):
                e.wait_ge(ev, evv)
                e.matmul(pE[:, 0:512], ones1[:], AR[0:1, RW7:RW7 + 512], start=True, stop=True)
                e.matmul(pF[:, 0:512], ones1[:], AR[0:1, RW7 + 512:RW7 + 1024],
                         start=True, stop=True)
                e.drain().then_inc(mm, 1)
            add("mm")
            mmv = get("mm")

            @pr.on("vector")
            def _(e, mmv=mmv):
                e.wait_ge(mm, mmv)
                e.tensor_copy(AR[:, R17:R17 + 512], pE[:, 0:512])
                e.tensor_copy(AR[:, R27:R27 + 512], pF[:, 0:512])
                e.drain().then_inc(ev, 1)
            add("ev")
            evrep = get("ev")

            ev_fold = {}
            PB7 = (pA, pB, pE, pF)
            for m in range(8):
                evv = ev_fold.get(m - 4, evrep)

                @pr.on("tensor")
                def _(e, m=m, evv=evv):
                    e.wait_ge(ev, evv)
                    pb = PB7[m % 4]
                    for k in range(32):
                        e.matmul(pb[:, 0:512],
                                 ARB[:, 2 * WQ2 + k * 1024 + m * 128:2 * WQ2 + k * 1024 + (m + 1) * 128],
                                 ARB[:, 2 * HGB + k * 512:2 * HGB + (k + 1) * 512],
                                 start=(k == 0), stop=(k == 31))
                    e.drain().then_inc(mm, 1)
                add("mm")
                mmv = get("mm")

                @pr.on("vector")
                def _(e, m=m, mmv=mmv, evrep=evrep):
                    e.wait_ge(mm, mmv)
                    e.wait_ge(ev, evrep)
                    pb = PB7[m % 4]
                    OV = AR[:, OG:OG + 512]
                    e.tensor_tensor(OV, pb[:, 0:512], AR[:, R27:R27 + 512], op=ALU.mult)
                    e.drain()
                    e.scalar_tensor_tensor(OV, AR[:, R17:R17 + 512], VC("W2cn", 1, m),
                                           OV, op0=ALU.mult, op1=ALU.add)
                    e.drain()
                    e.tensor_scalar_add(OV, OV, VC("W2bb", 1, m))
                    e.drain()
                    e.tensor_add(AR[:, MIX2 + m * 512:MIX2 + (m + 1) * 512],
                                 AR[:, MIX2 + m * 512:MIX2 + (m + 1) * 512], OV)
                    e.drain().then_inc(ev, 1)
                add("ev")
                ev_fold[m] = get("ev")
            mm_m7[gi] = get("mm")
            evsum = get("ev")

            XTOK2, OT72 = 34304, 35328
            for tsub in range(4):
                g8 = gi * 4 + tsub
                XTp = XTOK if tsub % 2 == 0 else XTOK2
                OTp = OT7 if tsub % 2 == 0 else OT72
                ssem = "qt" if tsub % 2 == 0 else "qc"

                @pr.on("sync")
                def _(e, gate=ev_add8.get(g8 - 2, 0)):
                    if gate:
                        e.wait_ge(ev, gate)
                dma("sync", AR[:, XTp:XTp + 1024],
                    xs[WIN + ts0 + tsub * 128:WIN + ts0 + tsub * 128 + 128, :],
                    sem=ssem)
                xlq = get(ssem)

                evbase8 = get("ev")

                @pr.on("tensor")
                def _(e, tsub=tsub, evsum=evsum, evbase8=evbase8):
                    e.wait_ge(ev, evsum)
                    for f in range(8):
                        if evbase8 + f - 1 > 0:
                            e.wait_ge(ev, evbase8 + f - 1)
                        pb = pG if f % 2 == 0 else pH
                        e.transpose(pb[:, 0:128],
                                    AR[:, MIX2 + f * 512 + tsub * 128:MIX2 + f * 512 + tsub * 128 + 128],
                                    identf[:])
                        e.drain().then_inc(mm, 1)
                add("mm", 8)
                mmv = get("mm") - 8

                @pr.on("vector")
                def _(e, tsub=tsub, mmv=mmv, OTp=OTp, xlq=xlq, stm=SEM[ssem]):
                    e.wait_ge(stm, xlq)
                    for f in range(8):
                        e.wait_ge(mm, mmv + f + 1)
                        pb = pG if f % 2 == 0 else pH
                        e.tensor_copy(AR[:, OTp + f * 128:OTp + (f + 1) * 128],
                                      pb[:, 0:128])
                        e.drain().then_inc(ev, 1)
                add("ev", 8)

                @pr.on("vector")
                def _(e, OTp=OTp, XTp=XTp):
                    e.tensor_add(AR[:, OTp:OTp + 1024], AR[:, OTp:OTp + 1024],
                                 AR[:, XTp:XTp + 1024])
                    e.drain().then_inc(ev, 1)
                add("ev")
                ev_add8[g8] = get("ev")
                evv = get("ev")

                @pr.on("sync")
                def _(e, evv=evv):
                    e.wait_ge(ev, evv)
                dma("sync", out[ts0 + tsub * 128:ts0 + tsub * 128 + 128, :],
                    AR[:, OTp:OTp + 1024], sem=ssem)
                st_q8[g8] = get(ssem)

        barrier()
        if debug:
            for nmout, src in [(dbg_y2, sp_y2), (dbg_st, sp_st), (dbg_st2, sp_st2),
                               (dbg_mix, sp_mix), (dbg_xc, sp_xc),
                               (dbg_h, sp_h), (dbg_xn, sp_xn)]:
                dma("sync", nmout[:], src[:])
            qsv = get("qs")

            @pr.on("sync")
            def _(e, qsv=qsv):
                e.wait_ge(qs, qsv)
        _unused = (debug,)

        for eng in ENGS:
            qq = pr.q[eng]
            if cut is not None and cut < len(barrier_marks):
                qq = qq[:barrier_marks[cut][eng]]
            getattr(blk, eng)(lambda e, qq=qq: [f(e) for f in qq])

    return nc


_NC_CACHE = [None]


def kernel(**inputs):
    x = np.asarray(inputs["L0_attn_x"])
    B, L, _ = x.shape

    def rs(v, r):
        return np.ascontiguousarray(np.asarray(v).reshape(r, 128))

    base = {
        "W_in": np.ascontiguousarray(inputs["L0_attn_W_in"]),
        "conv_w": np.ascontiguousarray(
            np.asarray(inputs["L0_attn_conv_w"]).reshape(16, 128, 4)
            .transpose(1, 0, 2).reshape(128, 64)),
        "conv_b": rs(inputs["L0_attn_conv_b"], 16),
        "W_xp": np.ascontiguousarray(inputs["L0_attn_W_xproj"]),
        "W_dt": np.ascontiguousarray(inputs["L0_attn_W_dt"]),
        "b_dt": rs(inputs["L0_attn_b_dt"], 16),
        "A_log": np.ascontiguousarray(np.asarray(inputs["L0_attn_A_log"]).reshape(1, NH)),
        "D_skip": np.ascontiguousarray(np.asarray(inputs["L0_attn_D_skip"]).reshape(1, NH)),
        "W_out": np.ascontiguousarray(inputs["L0_attn_W_out"]),
        "ln_g": rs(inputs["L0_attn_ln_g"], 16),
        "ln_b": rs(inputs["L0_attn_ln_b"], 16),
        "ln1_g": rs(inputs["L0_ffn_ln1_g"], 8),
        "ln1_b": rs(inputs["L0_ffn_ln1_b"], 8),
        "w1T": np.ascontiguousarray(np.asarray(inputs["L0_ffn_w1"]).T),
        "b1": rs(inputs["L0_ffn_b1"], 32),
        "ln2_g": rs(inputs["L0_ffn_ln2_g"], 32),
        "ln2_b": rs(inputs["L0_ffn_ln2_b"], 32),
        "w2T": np.ascontiguousarray(np.asarray(inputs["L0_ffn_w2"]).T),
        "b2": rs(inputs["L0_ffn_b2"], 8),
    }
    in_maps = []
    for c in range(8):
        b, h = c // 2, c % 2
        s0 = h * OWN
        xsl = np.zeros((TOT, D), np.float32)
        lo = s0 - WIN
        src_lo = max(lo, 0)
        xsl[src_lo - lo:, :] = x[b, src_lo:s0 + OWN, :]
        in_maps.append(dict(base, xs=xsl))

    if _NC_CACHE[0] is None:
        _NC_CACHE[0] = build()
    res = run_bass_kernel_spmd(_NC_CACHE[0], in_maps, core_ids=list(range(8)))
    outp = np.empty((B, L, D), np.float32)
    for c in range(8):
        b, h = c // 2, c % 2
        outp[b, h * OWN:(h + 1) * OWN, :] = res.results[c]["out"]
    return outp


if __name__ == "__main__":
    nc = build()
    print("built ok;", "SBUF part bytes:", nc.SBUF_PARTITION_SIZE_BYTES)


# version 31
# speedup vs baseline: 1.1086x; 1.1055x over previous
import numpy as np
import concourse.bass as bass
import concourse.mybir as mybir
from concourse.bass_utils import run_bass_kernel_spmd

F32 = mybir.dt.float32
BF16 = mybir.dt.bfloat16
I32 = mybir.dt.int32
AF = mybir.ActivationFunctionType
ALU = mybir.AluOpType

D = 1024
DI = 2048
NH = 16
DS = 16
NCH = 16
CHUNK = 64
WIN = 512
NWG = WIN // 512
OWN = 2048
TOT = WIN + OWN
G = 512
NG = TOT // G
FH = 4096
LN_EPS = 1e-5
ARF = 44048

ENGS = ("sync", "gpsimd", "tensor", "vector", "scalar")


class P:

    def __init__(self):
        self.q = {e: [] for e in ENGS}
        self.cnt = {}

    def on(self, eng):
        def deco(fn):
            self.q[eng].append(fn)
        return deco

    def add(self, name, k=1):
        self.cnt[name] = self.cnt.get(name, 0) + k
        return self.cnt[name]

    def get(self, name):
        return self.cnt.get(name, 0)


def build(debug=False, cut=None, p4bar=False):
    nc = bass.Bass("TRN2", target_bir_lowering=False)

    xs = nc.dram_tensor("xs", [TOT, D], F32, kind="ExternalInput").ap()
    W_in = nc.dram_tensor("W_in", [D, 2 * DI], F32, kind="ExternalInput").ap()
    conv_w = nc.dram_tensor("conv_w", [128, 64], F32, kind="ExternalInput").ap()
    conv_b = nc.dram_tensor("conv_b", [DI // 128, 128], F32, kind="ExternalInput").ap()
    W_xp = nc.dram_tensor("W_xp", [DI, 48], F32, kind="ExternalInput").ap()
    W_dt = nc.dram_tensor("W_dt", [NH, DI], F32, kind="ExternalInput").ap()
    b_dt = nc.dram_tensor("b_dt", [DI // 128, 128], F32, kind="ExternalInput").ap()
    A_log = nc.dram_tensor("A_log", [1, NH], F32, kind="ExternalInput").ap()
    D_skip = nc.dram_tensor("D_skip", [1, NH], F32, kind="ExternalInput").ap()
    W_out = nc.dram_tensor("W_out", [DI, D], F32, kind="ExternalInput").ap()
    ln_g = nc.dram_tensor("ln_g", [DI // 128, 128], F32, kind="ExternalInput").ap()
    ln_b = nc.dram_tensor("ln_b", [DI // 128, 128], F32, kind="ExternalInput").ap()
    ln1_g = nc.dram_tensor("ln1_g", [D // 128, 128], F32, kind="ExternalInput").ap()
    ln1_b = nc.dram_tensor("ln1_b", [D // 128, 128], F32, kind="ExternalInput").ap()
    w1T = nc.dram_tensor("w1T", [D, FH], F32, kind="ExternalInput").ap()
    b1 = nc.dram_tensor("b1", [FH // 128, 128], F32, kind="ExternalInput").ap()
    ln2_g = nc.dram_tensor("ln2_g", [FH // 128, 128], F32, kind="ExternalInput").ap()
    ln2_b = nc.dram_tensor("ln2_b", [FH // 128, 128], F32, kind="ExternalInput").ap()
    w2T = nc.dram_tensor("w2T", [FH, D], F32, kind="ExternalInput").ap()
    b2 = nc.dram_tensor("b2", [D // 128, 128], F32, kind="ExternalInput").ap()
    out = nc.dram_tensor("out", [OWN, D], F32, kind="ExternalOutput").ap()

    sp_xn = nc.dram_tensor("sp_xn", [TOT, D], BF16, kind="Internal").ap()
    sp_xc = nc.dram_tensor("sp_xc", [NCH, 128, TOT], F32, kind="Internal").ap()
    sp_z = nc.dram_tensor("sp_z", [NCH, 128, OWN], F32, kind="Internal").ap()
    sp_y2 = nc.dram_tensor("sp_y2", [NCH, 128, OWN], BF16, kind="Internal").ap()
    sp_mix = nc.dram_tensor("sp_mix", [D // 128, 128, OWN], F32, kind="Internal").ap()
    sp_h = nc.dram_tensor("sp_h", [FH // 128, 128, OWN], BF16, kind="Internal").ap()
    wq1d = nc.dram_tensor("wq1d", [8, 128, FH], BF16, kind="Internal").ap()
    sp_st = nc.dram_tensor("sp_st", [2, OWN], F32, kind="Internal").ap()
    sp_st2 = nc.dram_tensor("sp_st2", [2, OWN], F32, kind="Internal").ap()
    if debug:
        dbg_y2 = nc.dram_tensor("dbg_y2", [NCH, 128, OWN], BF16, kind="ExternalOutput").ap()
        dbg_st = nc.dram_tensor("dbg_st", [2, OWN], F32, kind="ExternalOutput").ap()
        dbg_st2 = nc.dram_tensor("dbg_st2", [2, OWN], F32, kind="ExternalOutput").ap()
        dbg_mix = nc.dram_tensor("dbg_mix", [D // 128, 128, OWN], F32, kind="ExternalOutput").ap()
        dbg_xc = nc.dram_tensor("dbg_xc", [NCH, 128, TOT], F32, kind="ExternalOutput").ap()
        dbg_h = nc.dram_tensor("dbg_h", [FH // 128, 128, OWN], BF16, kind="ExternalOutput").ap()
        dbg_xn = nc.dram_tensor("dbg_xn", [TOT, D], BF16, kind="ExternalOutput").ap()

    pr = P()
    add, get = pr.add, pr.get

    import contextlib
    with contextlib.ExitStack() as stk:
        en = stk.enter_context
        AR = en(nc.sbuf_tensor("AR", [128, ARF], F32))
        projp = en(nc.sbuf_tensor("projp", [128, TOT], F32))
        identf = en(nc.sbuf_tensor("identf", [128, 128], F32))
        identb = en(nc.sbuf_tensor("identb", [128, 128], BF16))
        ones1 = en(nc.sbuf_tensor("ones1", [1, 128], F32))
        onesf = en(nc.sbuf_tensor("onesf", [128, 1], F32))
        ones1b = en(nc.sbuf_tensor("ones1b", [1, 128], BF16))
        onescb = en(nc.sbuf_tensor("onescb", [128, 1], BF16))
        vcol = en(nc.sbuf_tensor("vcol", [128, 400], F32))
        rb2 = en(nc.sbuf_tensor("rb2", [128, 512], BF16))
        cw = en(nc.sbuf_tensor("cw", [128, 64], F32))
        carry = en(nc.sbuf_tensor("carry", [128, 256], F32))
        wdtb = en(nc.sbuf_tensor("wdtb", [16, DI], BF16))
        cmask = en(nc.sbuf_tensor("cmask", [128, G], F32))
        ck = en(nc.sbuf_tensor("ck", [128, 2], F32))
        pA = en(nc.psum_tensor("pA", [128, 512], F32))
        pB = en(nc.psum_tensor("pB", [128, 512], F32))
        pC = en(nc.psum_tensor("pC", [128, 512], F32))
        pD = en(nc.psum_tensor("pD", [128, 512], F32))
        pE = en(nc.psum_tensor("pE", [128, 512], F32))
        pF = en(nc.psum_tensor("pF", [128, 512], F32))
        pG = en(nc.psum_tensor("pG", [128, 512], F32))
        pH = en(nc.psum_tensor("pH", [128, 512], F32))
        qs = en(nc.semaphore("qs"))
        qg = en(nc.semaphore("qg"))
        qc = en(nc.semaphore("qc"))
        mm = en(nc.semaphore("mm"))
        ev = en(nc.semaphore("ev"))
        sp = en(nc.semaphore("sp"))
        gp = en(nc.semaphore("gp"))
        ph = en(nc.semaphore("ph"))
        qt = en(nc.semaphore("qt"))
        yr = en(nc.semaphore("yr"))
        qh = en(nc.semaphore("qh"))
        qi = en(nc.semaphore("qi"))
        qm = en(nc.semaphore("qm"))
        ql = en(nc.semaphore("ql"))
        blk = en(nc.Block())
        ARB = AR.bitcast(BF16)
        nc.const_aps.register(F32, LN_EPS, ck[:, 0:1])
        nc.const_aps.register(F32, 1.0, ck[:, 1:2])
        SEM = dict(qs=qs, qg=qg, qc=qc, mm=mm, ev=ev, sp=sp, gp=gp, ph=ph, qt=qt,
                   yr=yr, qh=qh, qi=qi, qm=qm, ql=ql)
        nbar = [0]

        barrier_marks = []

        def barrier():
            nbar[0] += 1
            target = 5 * nbar[0]
            qsv, qgv, qcv, gpv_b, qtv = (get("qs"), get("qg"), get("qc"),
                                         get("gp"), get("qt"))
            qhv_b = get("qh")
            yrv_b, qlv_b, qiv_b, qmv_b = (get("yr"), get("ql"), get("qi"),
                                          get("qm"))
            for eng in ENGS:
                @pr.on(eng)
                def _(e, eng=eng, target=target, qsv=qsv, qgv=qgv, qcv=qcv,
                      gpv_b=gpv_b, qtv=qtv, qhv_b=qhv_b):
                    if eng == "sync":
                        e.wait_ge(qs, qsv)
                        e.wait_ge(qc, qcv)
                        e.wait_ge(qt, qtv)
                        e.wait_ge(yr, yrv_b)
                        e.wait_ge(ql, qlv_b)
                    elif eng == "gpsimd":
                        e.wait_ge(qg, qgv)
                        e.wait_ge(qh, qhv_b)
                        e.wait_ge(qi, qiv_b)
                        e.wait_ge(qm, qmv_b)
                        e.wait_ge(gp, gpv_b)
                    elif eng == "scalar":
                        e.wait_ge(qc, qcv)
                    e.drain().then_inc(ph, 1)
                    e.wait_ge(ph, target)
            barrier_marks.append({eng: len(pr.q[eng]) for eng in ENGS})

        def dma(eng, outap, inap, sem="qs"):
            add(sem, 16)
            s = SEM[sem]
            @pr.on(eng)
            def _(e, outap=outap, inap=inap, s=s):
                e.dma_start(out=outap, in_=inap).then_inc(s, 16)

        def dmat(outap, inap, sem="qs"):
            add(sem, 16)
            s = SEM[sem]
            @pr.on("sync")
            def _(e, outap=outap, inap=inap, s=s):
                e.dma_start_transpose(out=outap, in_=inap).then_inc(s, 16)

        VOFF = {}
        _o = 0
        for nmc, ncols in [("A", 16), ("Dsk", 16), ("bdt", 16), ("cvb", 16),
                           ("lng", 16), ("lnb", 16), ("g1", 8), ("bt1", 8),
                           ("b1c", 32), ("g2", 32), ("bt2", 32), ("b2c", 8),
                           ("Wc", 8), ("Wb", 8), ("W2c", 8), ("W2b", 8),
                           ("m1", 1), ("m2", 1), ("hm1", 1), ("hm2", 1),
                           ("hm1n", 1), ("hm2n", 1), ("Wcn", 8), ("W2cn", 8),
                           ("W2bb", 8), ("l1r", 8)]:
            VOFF[nmc] = _o
            _o += ncols
        assert _o <= 400

        def VC(nmc, n=1, j=0):
            o = VOFF[nmc] + j
            return vcol[:, o:o + n]

        stage_specs = [(conv_b, 16, "cvb"), (b_dt, 16, "bdt"), (ln_g, 16, "lng"),
                       (ln_b, 16, "lnb"), (ln1_g, 8, "g1"), (ln1_b, 8, "bt1"),
                       (b1, 32, "b1c"), (ln2_g, 32, "g2"), (ln2_b, 32, "bt2"),
                       (b2, 8, "b2c")]
        o_st = 128 * len(stage_specs)

        for i, (ap_, rows, _) in enumerate(stage_specs):
            dma("sync", AR[0:rows, i * 128:i * 128 + 128], ap_[:, :])
        dma("sync", AR[0:1, o_st:o_st + 16], A_log[:, :])
        dma("sync", AR[0:1, o_st + 16:o_st + 32], D_skip[:, :])
        dma("sync", cw[:, :], conv_w[:, :])
        dma("gpsimd", wdtb[:, :], W_dt[:, :], sem="qg")

        @pr.on("gpsimd")
        def _(e):
            e.memset(ones1[:], 1.0).then_inc(gp, 1)
            e.memset(ck[:, 0:1], LN_EPS).then_inc(gp, 1)
            e.memset(ck[:, 1:2], 1.0).then_inc(gp, 1)
            e.memset(onesf[:], 1.0).then_inc(gp, 1)
            e.memset(ones1b[:], 1.0).then_inc(gp, 1)
            e.memset(onescb[:], 1.0).then_inc(gp, 1)
            e.memset(cmask[0:128, 0:G].rearrange("p (a b) -> p a b", b=CHUNK)[:, :, 1:CHUNK],
                     1.0).then_inc(gp, 1)
            e.memset(cmask[:, 0:G:CHUNK], 0.0).then_inc(gp, 1)
            e.memset(carry[:], 0.0).then_inc(gp, 1)
            e.iota(AR[:, o_st + 128:o_st + 256].bitcast(I32),
                   pattern=[[-1, 128]], base=0, channel_multiplier=1).then_inc(gp, 1)
        add("gp", 10)

        gpv = get("gp")

        @pr.on("vector")
        def _(e, gpv=gpv):
            e.wait_ge(gp, gpv)
            e.tensor_single_scalar(identf[:], AR[:, o_st + 128:o_st + 256].bitcast(I32),
                                   0, op=ALU.is_equal)
            e.drain()
            e.tensor_copy(identb[:], identf[:])
            e.drain().then_inc(ev, 1)
        add("ev")

        qsv = get("qs")

        @pr.on("scalar")
        def _(e, qsv=qsv):
            e.wait_ge(qs, qsv)
            e.activation(AR[0:1, o_st + 32:o_st + 48], AR[0:1, o_st:o_st + 16],
                         AF.Exp).then_inc(sp, 1)
        add("sp")

        spv = get("sp")

        @pr.on("vector")
        def _(e, spv=spv):
            e.wait_ge(sp, spv)
            e.tensor_scalar_mul(AR[0:1, o_st + 32:o_st + 48], AR[0:1, o_st + 32:o_st + 48], -1.0)
            e.drain().then_inc(ev, 1)
        add("ev")

        evv = get("ev")
        mmv0 = get("mm")

        @pr.on("tensor")
        def _(e, evv=evv, mmv0=mmv0, qsv=get("qs")):
            e.wait_ge(ev, evv)
            e.wait_ge(qs, qsv)
            for i, (_a, rows, _n) in enumerate(stage_specs):
                if i > 0:
                    e.wait_ge(ev, evv + i)
                e.transpose(pG[:, 0:rows], AR[0:rows, i * 128:i * 128 + 128],
                            identf[0:rows, 0:rows])
                e.drain().then_inc(mm, 1)
            e.wait_ge(ev, evv + len(stage_specs))
            e.matmul(pH[:, 0:16], ones1[:], AR[0:1, o_st + 32:o_st + 48], start=True, stop=True)
            e.matmul(pH[:, 16:32], ones1[:], AR[0:1, o_st + 16:o_st + 32], start=True, stop=True)
            e.drain().then_inc(mm, 1)
        add("mm", len(stage_specs) + 1)

        @pr.on("vector")
        def _(e, mmv0=mmv0, specs=tuple(stage_specs)):
            for i, (_a, rows, nmc) in enumerate(specs):
                e.wait_ge(mm, mmv0 + i + 1)
                e.tensor_copy(vcol[:, VOFF[nmc]:VOFF[nmc] + rows], pG[:, 0:rows])
                e.drain().then_inc(ev, 1)
            e.wait_ge(mm, mmv0 + len(specs) + 1)
            e.tensor_copy(VC("A", 16), pH[:, 0:16])
            e.tensor_copy(VC("Dsk", 16), pH[:, 16:32])
            e.drain().then_inc(ev, 1)
        add("ev", len(stage_specs) + 1)

        barrier()

        QW, QSUM = 0, 8192
        wtiles = [(w1T, k, 4096, k) for k in range(8)] + \
                 [(w2T, k, 1024, 8 + k) for k in range(32)]
        for i, (wt, k, wid, col) in enumerate(wtiles):
            par = i % 2
            if i >= 1:
                evv = get("ev")

                @pr.on("sync")
                def _(e, evv=evv):
                    e.wait_ge(ev, evv)
            dma("sync", AR[:, QW + par * 4096:QW + par * 4096 + wid],
                wt[k * 128:(k + 1) * 128, :])
            qsv = get("qs")

            @pr.on("vector")
            def _(e, par=par, wid=wid, col=col, qsv=qsv):
                e.wait_ge(qs, qsv)
                e.tensor_reduce(AR[:, QSUM + col:QSUM + col + 1],
                                AR[:, QW + par * 4096:QW + par * 4096 + wid],
                                axis=mybir.AxisListType.X, op=ALU.add,
                                apply_absolute_value=True)
                e.drain().then_inc(ev, 1)
            add("ev")

        evv = get("ev")

        @pr.on("vector")
        def _(e, evv=evv):
            e.wait_ge(ev, evv)
            e.tensor_reduce(AR[:, QSUM + 40:QSUM + 41], AR[:, QSUM:QSUM + 8],
                            axis=mybir.AxisListType.X, op=ALU.add)
            e.tensor_reduce(AR[:, QSUM + 41:QSUM + 42], AR[:, QSUM + 8:QSUM + 40],
                            axis=mybir.AxisListType.X, op=ALU.add)
            e.drain().then_inc(ev, 1)
        add("ev")
        evv = get("ev")

        @pr.on("tensor")
        def _(e, evv=evv):
            e.wait_ge(ev, evv)
            e.matmul(pG[0:1, 0:2], onesf[:], AR[:, QSUM + 40:QSUM + 42],
                     start=True, stop=True)
            e.drain().then_inc(mm, 1)
        add("mm")
        mmv = get("mm")

        @pr.on("vector")
        def _(e, mmv=mmv):
            e.wait_ge(mm, mmv)
            R = AR[0:1, QSUM + 44:QSUM + 56]
            e.tensor_scalar(R[:, 0:1], pG[0:1, 0:1], 1.0 / (FH * D), 1e-5,
                            op0=ALU.mult, op1=ALU.max)
            e.tensor_scalar(R[:, 1:2], pG[0:1, 1:2], 1.0 / (FH * D), 1e-5,
                            op0=ALU.mult, op1=ALU.max)
            e.drain()
            e.tensor_scalar_mul(R[:, 2:3], R[:, 0:1], 0.5)
            e.tensor_scalar_mul(R[:, 3:4], R[:, 1:2], 0.5)
            e.tensor_scalar_mul(R[:, 4:5], R[:, 0:1], -0.5)
            e.tensor_scalar_mul(R[:, 5:6], R[:, 1:2], -0.5)
            e.drain().then_inc(ev, 1)
        add("ev")
        evv = get("ev")

        @pr.on("tensor")
        def _(e, evv=evv):
            e.wait_ge(ev, evv)
            e.matmul(pG[:, 8:14], ones1[:], AR[0:1, QSUM + 44:QSUM + 50],
                     start=True, stop=True)
            e.drain().then_inc(mm, 1)
        add("mm")
        mmv = get("mm")

        @pr.on("vector")
        def _(e, mmv=mmv):
            e.wait_ge(mm, mmv)
            for j, nmc in enumerate(["m1", "m2", "hm1", "hm2", "hm1n", "hm2n"]):
                e.tensor_copy(VC(nmc), pG[:, 8 + j:9 + j])
            e.drain().then_inc(ev, 1)
        add("ev")

        barrier()

        XTM, XSQ1, XNB1, RWS = 0, 2048, 4096, 5120
        ev_n1 = {}
        sp_sq1 = {}
        for tt in range(TOT // 128):
            p1 = tt % 2
            psem = "qs" if p1 == 0 else "qc"
            slot = XTM + p1 * 1024
            sqs = XSQ1 + p1 * 1024
            nsl = XNB1 + p1 * 512
            R0 = RWS + p1 * 16

            lsem = "qg" if p1 == 0 else "qh"

            @pr.on("gpsimd")
            def _(e, gate=ev_n1.get(tt - 2, 0)):
                if gate:
                    e.wait_ge(ev, gate)
            dma("gpsimd", AR[:, slot:slot + 1024],
                xs[tt * 128:(tt + 1) * 128, :], sem=lsem)
            qlv = get(lsem)

            @pr.on("scalar")
            def _(e, tt=tt, qlv=qlv, qsm=SEM[lsem], slot=slot, sqs=sqs, R0=R0,
                  gate=ev_n1.get(tt - 2, 0)):
                e.wait_ge(qsm, qlv)
                if gate:
                    e.wait_ge(ev, gate)
                XT = AR[:, slot:slot + 1024]
                e.activation(AR[:, sqs:sqs + 1024], XT, AF.Copy,
                             accum_out=AR[:, R0:R0 + 1])
                e.drain()
                e.activation(AR[:, sqs:sqs + 1024], XT, AF.Square,
                             accum_out=AR[:, R0 + 1:R0 + 2])
                e.drain().then_inc(sp, 1)
            add("sp")
            sp_sq1[tt] = get("sp")

            @pr.on("vector")
            def _(e, tt=tt, spv=sp_sq1[tt], R0=R0):
                e.wait_ge(sp, spv)
                MU = AR[:, R0:R0 + 1]
                S2 = AR[:, R0 + 1:R0 + 2]
                mu = AR[:, R0 + 2:R0 + 3]
                va = AR[:, R0 + 3:R0 + 4]
                e.tensor_scalar_mul(mu, MU, 1.0 / D)
                e.drain()
                e.tensor_scalar_mul(va, S2, 1.0 / D)
                e.drain()
                mus = AR[:, R0 + 6:R0 + 7]
                e.tensor_mul(mus, mu, mu)
                e.drain()
                e.tensor_sub(va, va, mus)
                e.drain().then_inc(ev, 1)
            add("ev")
            evv = get("ev")

            @pr.on("scalar")
            def _(e, evv=evv, R0=R0):
                e.wait_ge(ev, evv)
                e.activation(AR[:, R0 + 4:R0 + 5], AR[:, R0 + 3:R0 + 4],
                             AF.Sqrt, bias=LN_EPS)
                e.drain().then_inc(sp, 1)
            add("sp")
            spv = get("sp")
            qfv = get(psem)

            @pr.on("vector")
            def _(e, tt=tt, spv=spv, R0=R0, slot=slot, nsl=nsl, qfv=qfv,
                  qsm=SEM[psem]):
                e.wait_ge(sp, spv)
                e.wait_ge(qsm, qfv)
                rs = AR[:, R0 + 5:R0 + 6]
                e.reciprocal(rs, AR[:, R0 + 4:R0 + 5])
                e.drain()
                e.tensor_scalar(ARB[:, 2 * nsl:2 * nsl + 1024],
                                AR[:, slot:slot + 1024],
                                AR[:, R0 + 2:R0 + 3], rs,
                                op0=ALU.subtract, op1=ALU.mult)
                e.drain().then_inc(ev, 1)
            add("ev")
            ev_n1[tt] = get("ev")
            evv = get("ev")

            @pr.on("sync")
            def _(e, evv=evv):
                e.wait_ge(ev, evv)
            dma("sync", sp_xn[tt * 128:(tt + 1) * 128, :],
                ARB[:, 2 * nsl:2 * nsl + 1024], sem=psem)

        barrier()

        WB1, XNBG, XEW, XTL, CVT, XCS, ZS, XCB, XPWB = (
            0, 16384, 18432, 18952, 19016, 19528, 20040, 20552, 24648)

        for k in range(8):
            dma("gpsimd", ARB[:, 2 * WB1 + k * 4096:2 * WB1 + (k + 1) * 4096],
                W_in[k * 128:(k + 1) * 128, :], sem="qg")
        dma("gpsimd", ARB[:, 2 * XPWB:2 * XPWB + 768].rearrange("p (k c) -> p k c", c=48),
            W_xp.rearrange("(k p) c -> p k c", p=128), sem="qg")

        @pr.on("gpsimd")
        def _(e):
            e.memset(AR[:, XTL:XTL + 64], 0.0).then_inc(gp, 1)
        add("gp")

        def WB1t(k, mt):
            o = 2 * WB1 + k * 4096 + mt * 128
            return ARB[:, o:o + 128]

        def XCBt(ct):
            o = 2 * XCB + ct * 512
            return ARB[:, o:o + 512]

        qgv_w = get("qg")
        gpv_t = get("gp")

        QWAq, QPq, QNq, WQSq = 28672, 32768, 36864, 40960
        ev_q6 = {}
        evq6 = get("ev")

        def emit_quant1(k):
            @pr.on("sync")
            def _(e, evv=ev_q6.get(k - 1, evq6)):
                e.wait_ge(ev, evv)
            dma("sync", AR[:, QWAq:QWAq + 4096], w1T[k * 128:(k + 1) * 128, :],
                sem="ql")
            qsv = get("ql")

            @pr.on("vector")
            def _(e, k=k, qsv=qsv):
                e.wait_ge(ql, qsv)
                e.tensor_scalar(AR[:, QPq:QPq + 4096], AR[:, QWAq:QWAq + 4096],
                                VC("hm1"), VC("m1"), op0=ALU.is_gt, op1=ALU.mult)
                e.tensor_scalar(AR[:, QNq:QNq + 4096], AR[:, QWAq:QWAq + 4096],
                                VC("hm1n"), VC("m1"), op0=ALU.is_lt, op1=ALU.mult)
                e.drain()
                e.tensor_tensor(ARB[:, 2 * WQSq:2 * WQSq + 4096],
                                AR[:, QPq:QPq + 4096], AR[:, QNq:QNq + 4096],
                                op=ALU.subtract)
                e.drain().then_inc(ev, 1)
            add("ev")
            ev_q6[k] = get("ev")
            evv = get("ev")

            @pr.on("sync")
            def _(e, evv=evv):
                e.wait_ge(ev, evv)
            dma("sync", wq1d[k, :, :], ARB[:, 2 * WQSq:2 * WQSq + 4096], sem="ql")

        CVT2, XCS2b, ZS2b = 25088, 25600, 26112
        XNBG2 = 26624
        ev_psfree = {}
        ev_zpsfree = {}
        sp_silu = {}
        sp_zsilu = {}
        qfull23 = {}
        mm_glast = {}

        for g in range(NG):
            gown = g >= NWG
            XNBp = XNBG if g % 2 == 0 else XNBG2
            tsem = "qt" if g % 2 == 0 else "yr"

            @pr.on("sync")
            def _(e, gate=mm_glast.get(g - 2, 0)):
                if gate:
                    e.wait_ge(mm, gate)
            for f in range(8):
                dmat(ARB[:, 2 * XNBp + f * 512:2 * XNBp + (f + 1) * 512],
                     sp_xn[g * G:(g + 1) * G, f * 128:(f + 1) * 128], sem=tsem)
            qgv = get(tsem)

            for ct in range(NCH):
                gct = g * NCH + ct
                p2 = ct % 2
                psem = "qs" if p2 == 0 else "qc"
                CVp = CVT if p2 == 0 else CVT2
                XCp = XCS if p2 == 0 else XCS2b
                ZSp = ZS if p2 == 0 else ZS2b
                @pr.on("tensor")
                def _(e, ct=ct, qgv=qgv, evv=ev_psfree.get(gct - 2, 0), gpv_t=gpv_t,
                      XNBp=XNBp, tsm=SEM[tsem]):
                    e.wait_ge(qg, qgv_w)
                    if ct == 0:
                        e.wait_ge(tsm, qgv)
                    e.wait_ge(gp, gpv_t)
                    if evv:
                        e.wait_ge(ev, evv)
                    pb = pA if ct % 2 == 0 else pB
                    for k in range(8):
                        e.matmul(pb[:, 0:512], WB1t(k, ct),
                                 ARB[:, 2 * XNBp + k * 512:2 * XNBp + (k + 1) * 512],
                                 start=(k == 0), stop=(k == 7))
                    e.drain().then_inc(mm, 1)
                add("mm")
                mmv = get("mm")

                @pr.on("vector")
                def _(e, ct=ct, mmv=mmv, spcv=sp_silu.get(gct - 2, 0), CVp=CVp):
                    e.wait_ge(mm, mmv)
                    if spcv:
                        e.wait_ge(sp, spcv)
                    pb = pA if ct % 2 == 0 else pB
                    XEWv = AR[:, XEW:XEW + 520]
                    e.tensor_copy(XEWv[:, 3:515], pb[:, 0:512])
                    e.drain().then_inc(ev, 1)
                    e.tensor_copy(XEWv[:, 0:3], AR[:, XTL + ct * 4:XTL + ct * 4 + 3])
                    e.drain()
                    CV = AR[:, CVp:CVp + 512]
                    e.tensor_scalar_mul(CV, XEWv[:, 0:512], cw[:, ct * 4:ct * 4 + 1])
                    e.drain()
                    e.scalar_tensor_tensor(CV, XEWv[:, 1:513], cw[:, ct * 4 + 1:ct * 4 + 2],
                                           CV, op0=ALU.mult, op1=ALU.add)
                    e.drain()
                    e.scalar_tensor_tensor(CV, XEWv[:, 2:514], cw[:, ct * 4 + 2:ct * 4 + 3],
                                           CV, op0=ALU.mult, op1=ALU.add)
                    e.drain()
                    e.scalar_tensor_tensor(CV, XEWv[:, 3:515], cw[:, ct * 4 + 3:ct * 4 + 4],
                                           CV, op0=ALU.mult, op1=ALU.add)
                    e.tensor_copy(AR[:, XTL + ct * 4:XTL + ct * 4 + 3], XEWv[:, 512:515])
                    e.drain().then_inc(ev, 1)
                add("ev", 2)
                ev_psfree[gct] = get("ev") - 1
                evv = get("ev")

                qsv_x = (get(psem) if ct < 2 else qfull23.get(gct - 2, 0))

                @pr.on("scalar")
                def _(e, ct=ct, evv=evv, qsv_x=qsv_x,
                      qsm=SEM[psem], CVp=CVp, XCp=XCp):
                    e.wait_ge(ev, evv)
                    if qsv_x:
                        e.wait_ge(qsm, qsv_x)
                    e.activation(AR[:, XCp:XCp + 512], AR[:, CVp:CVp + 512], AF.Silu,
                                 bias=VC("cvb", 1, ct))
                    e.drain()
                    e.copy(XCBt(ct), AR[:, XCp:XCp + 512]).then_inc(sp, 1)
                add("sp")
                sp_silu[gct] = get("sp")
                spv = get("sp")

                @pr.on("sync")
                def _(e, spv=spv):
                    e.wait_ge(sp, spv)
                dma("sync", sp_xc[ct, :, g * G:(g + 1) * G], AR[:, XCp:XCp + 512],
                    sem=psem)

                if gown:
                    @pr.on("tensor")
                    def _(e, ct=ct, spz=sp_zsilu.get(gct - 2, 0), XNBp=XNBp):
                        if spz:
                            e.wait_ge(sp, spz)
                        pz = pC if ct % 2 == 0 else pD
                        for k in range(8):
                            e.matmul(pz[:, 0:512], WB1t(k, 16 + ct),
                                     ARB[:, 2 * XNBp + k * 512:2 * XNBp + (k + 1) * 512],
                                     start=(k == 0), stop=(k == 7))
                        e.drain().then_inc(mm, 1)
                    add("mm")
                    mmv = get("mm")

                    qsv_z = (get(psem) if ct < 2 else qfull23.get(gct - 2, 0))

                    @pr.on("scalar")
                    def _(e, ct=ct, mmv=mmv, qsv=qsv_z,
                          qsm=SEM[psem], ZSp=ZSp):
                        e.wait_ge(mm, mmv)
                        if qsv:
                            e.wait_ge(qsm, qsv)
                        pz = pC if ct % 2 == 0 else pD
                        e.activation(AR[:, ZSp:ZSp + 512], pz[:, 0:512],
                                     AF.Silu).then_inc(sp, 1)
                        e.drain()
                    add("sp")
                    sp_zsilu[gct] = get("sp")
                    spv = get("sp")

                    @pr.on("sync")
                    def _(e, spv=spv):
                        e.wait_ge(sp, spv)
                    dma("sync", sp_z[ct, :, (g - NWG) * G:(g - NWG + 1) * G], AR[:, ZSp:ZSp + 512],
                        sem=psem)
                qfull23[gct] = get(psem)

            mm_glast[g] = get("mm")
            for kq in ([2 * (g - 1), 2 * (g - 1) + 1] if 1 <= g <= 4 else []):
                emit_quant1(kq)
            spv = get("sp")

            @pr.on("tensor")
            def _(e, g=g, spv=spv):
                e.wait_ge(sp, spv)
                for p_i, pz in enumerate((pE, pF, pG)):
                    for k in range(NCH):
                        e.matmul(pz[0:16, 0:512],
                                 ARB[:, 2 * XPWB + k * 48 + p_i * 16:2 * XPWB + k * 48 + (p_i + 1) * 16],
                                 XCBt(k), start=(k == 0), stop=(k == NCH - 1))
                e.drain().then_inc(mm, 1)
            add("mm")
            mmv = get("mm")

            @pr.on("vector")
            def _(e, g=g, mmv=mmv):
                e.wait_ge(mm, mmv)
                e.tensor_copy(projp[0:16, g * G:(g + 1) * G], pE[0:16, 0:512])
                e.tensor_copy(projp[32:48, g * G:(g + 1) * G], pF[0:16, 0:512])
                e.tensor_copy(projp[64:80, g * G:(g + 1) * G], pG[0:16, 0:512])
                e.drain().then_inc(ev, 1)
            add("ev")

        barrier()

        qlv_q = get("ql")

        UBB, BREP, CREP, SALL, DECR = 0, 4096, 8192, 12288, 20480
        XCS2, DTH4, CUMD, U4 = 36864, 37888, 38912, 39936
        YA4, ZS4, Y24, DTBG, DC0, Y2B, SQ2 = (40960, 41472, 42496, 43008, 43264,
                                              43280, 43792)

        TM4 = DECR + 8192

        def ubbv(p=0):
            return AR[:, DECR:DECR + 8192].rearrange("p (n t) -> p n t", n=16)

        def sallv():
            return AR[:, SALL:SALL + 8192].rearrange("p (n t) -> p n t", n=16)

        F4XN, F4XT, F4SH, F4HB = 256, 2304, 2816, 3840
        mm_tr4 = {}
        ev_n4 = {}
        ev_ev4 = {}
        evfb4 = {}
        sp_st4 = {}
        sp_gel4 = {}
        mm_w14 = {}
        qm_t4 = {}
        qi_s4 = {}
        mm_w1l4 = {}

        @pr.on("vector")
        def _(e, qlv_q=qlv_q):
            e.wait_ge(ql, qlv_q)

        ev_u = {}
        ev_scan = {}
        ev_y2 = {}
        gp_pool = {}
        gp_mult = {}
        sp_soft = {}
        sp_dec = {}
        qs_y2sp = {}
        mm_st = {}

        for tb in range(NG):
            town = tb >= NWG
            ts0 = tb * G
            mmv = get("mm")
            evfl = get("ev")
            gpfl = get("gp")
            qsfl = get("qs")

            qtfl = get("qt")

            @pr.on("gpsimd")
            def _(e, mmv=mmv, evfl=evfl, qtfl=qtfl, gpfl=gpfl):
                e.wait_ge(mm, mmv)
                e.wait_ge(ev, evfl)
                e.wait_ge(gp, gpfl)
                e.wait_ge(qt, qtfl)
            dma("gpsimd", ARB[0:1, 2 * DECR:2 * DECR + 8192],
                projp[32:48, ts0:ts0 + G], sem="qg")
            if town:
                dma("gpsimd", ARB[0:1, 2 * DECR + 8192:2 * DECR + 16384],
                    projp[64:80, ts0:ts0 + G], sem="qg")
            qgv = get("qg")
            evv = get("ev")

            @pr.on("tensor")
            def _(e, qgv=qgv, evv=evv):
                e.wait_ge(qg, qgv)
                for n in range(16):
                    if evv + n - 1 > 0:
                        e.wait_ge(ev, evv + n - 1)
                    pb = pE if n % 2 == 0 else pF
                    e.matmul(pb[:, 0:512], ones1b[:],
                             ARB[0:1, 2 * DECR + n * 512:2 * DECR + (n + 1) * 512],
                             start=True, stop=True)
                    e.drain().then_inc(mm, 1)
            add("mm", 16)
            mmv = get("mm") - 16

            @pr.on("vector")
            def _(e, mmv=mmv):
                for n in range(16):
                    e.wait_ge(mm, mmv + n + 1)
                    pb = pE if n % 2 == 0 else pF
                    e.tensor_copy(ARB[:, 2 * BREP + n * 512:2 * BREP + (n + 1) * 512],
                                  pb[:, 0:512])
                    e.drain().then_inc(ev, 1)
            add("ev", 16)
            if town:
                evv = get("ev")

                @pr.on("tensor")
                def _(e, evv=evv):
                    for n in range(16):
                        e.wait_ge(ev, evv + n - 1)
                        pb = pE if n % 2 == 0 else pF
                        e.matmul(pb[:, 0:512], ones1b[:],
                                 ARB[0:1, 2 * DECR + 8192 + n * 512:2 * DECR + 8192 + (n + 1) * 512],
                                 start=True, stop=True)
                        e.drain().then_inc(mm, 1)
                add("mm", 16)
                mmv = get("mm") - 16

                @pr.on("vector")
                def _(e, mmv=mmv):
                    for n in range(16):
                        e.wait_ge(mm, mmv + n + 1)
                        pb = pE if n % 2 == 0 else pF
                        e.tensor_copy(ARB[:, 2 * CREP + n * 512:2 * CREP + (n + 1) * 512],
                                      pb[:, 0:512])
                        e.drain().then_inc(ev, 1)
                add("ev", 16)

            evv = get("ev")
            mmdt = get("mm")

            @pr.on("scalar")
            def _(e, ts0=ts0, evv=evv, mmdt=mmdt):
                e.wait_ge(ev, evv)
                e.wait_ge(mm, mmdt)
                e.copy(ARB[0:16, 2 * DTBG:2 * DTBG + 512],
                       projp[0:16, ts0:ts0 + G]).then_inc(sp, 1)
            add("sp")
            spdtb = get("sp")

            def emit_front(ct):
                gidx = tb * NCH + ct
                p = ct % 2
                qsem = "qs" if p == 0 else "qc"
                gate = ev_y2.get(gidx - 2, ev_u.get(gidx - 2, 0))

                @pr.on("sync")
                def _(e, gate=gate):
                    if gate:
                        e.wait_ge(ev, gate)
                dma("sync", AR[:, XCS2 + p * 512:XCS2 + (p + 1) * 512],
                    sp_xc[ct, :, ts0:ts0 + G], sem=qsem)
                if town:
                    dma("sync", AR[:, ZS4 + p * 512:ZS4 + (p + 1) * 512],
                        sp_z[ct, :, ts0 - WIN:ts0 - WIN + G], sem=qsem)
                qsv = get(qsem)

                spg = sp_soft.get(gidx - 2, spdtb)

                @pr.on("tensor")
                def _(e, ct=ct, p=p, spg=max(spg, spdtb)):
                    e.wait_ge(sp, spg)
                    pb = pA if p == 0 else pB
                    e.matmul(pb[:, 0:512], wdtb[0:16, ct * 128:(ct + 1) * 128],
                             ARB[0:16, 2 * DTBG:2 * DTBG + 512],
                             start=True, stop=True)
                    e.drain().then_inc(mm, 1)
                add("mm")
                mmv = get("mm")

                gu = ev_scan.get(gidx - 2, ev_u.get(gidx - 2, 0))

                @pr.on("scalar")
                def _(e, ct=ct, p=p, mmv=mmv, gu=gu):
                    e.wait_ge(mm, mmv)
                    if gu:
                        e.wait_ge(ev, gu)
                    pb = pA if p == 0 else pB
                    T = AR[:, CUMD + p * 512:CUMD + (p + 1) * 512]
                    e.activation(T, pb[:, 0:512], AF.Exp, bias=VC("bdt", 1, ct))
                    e.drain()
                    e.activation(AR[:, DTH4 + p * 512:DTH4 + (p + 1) * 512], T,
                                 AF.Ln, bias=1.0).then_inc(sp, 1)
                add("sp")
                sp_soft[gidx] = get("sp")

                @pr.on("vector")
                def _(e, ct=ct, p=p, spv=sp_soft[gidx]):
                    e.wait_ge(sp, spv)
                    e.tensor_tensor_scan(AR[:, CUMD + p * 512:CUMD + (p + 1) * 512],
                                         cmask[:, :],
                                         AR[:, DTH4 + p * 512:DTH4 + (p + 1) * 512],
                                         0.0, op0=ALU.mult, op1=ALU.add)
                    e.drain().then_inc(ev, 1)
                add("ev")
                evcum = get("ev")

                @pr.on("scalar")
                def _(e, ct=ct, p=p, evv=evcum):
                    e.wait_ge(ev, evv)
                    T = AR[:, CUMD + p * 512:CUMD + (p + 1) * 512]
                    e.activation(T, T, AF.Exp, scale=VC("A", 1, ct)).then_inc(sp, 1)
                add("sp")
                sp_dec[gidx] = get("sp")

                @pr.on("vector")
                def _(e, ct=ct, p=p, spd=sp_dec[gidx], sps=sp_soft[gidx],
                      qsv=qsv, qsm=SEM[qsem]):
                    e.wait_ge(sp, sps)
                    e.wait_ge(qsm, qsv)
                    e.tensor_mul(AR[:, U4 + p * 512:U4 + (p + 1) * 512],
                                 AR[:, DTH4 + p * 512:DTH4 + (p + 1) * 512],
                                 AR[:, XCS2 + p * 512:XCS2 + (p + 1) * 512])
                    e.drain()
                    e.tensor_tensor(ubbv(p),
                                    AR[:, U4 + p * 512:U4 + (p + 1) * 512]
                                    .unsqueeze(1).broadcast_to((128, 16, 512)),
                                    ARB[:, 2 * BREP:2 * BREP + 8192]
                                    .rearrange("p (n t) -> p n t", n=16), op=ALU.mult)
                    e.drain()
                    e.wait_ge(sp, spd)
                    T = AR[:, CUMD + p * 512:CUMD + (p + 1) * 512]
                    e.tensor_copy(AR[:, DC0 + p * 8:DC0 + p * 8 + 1], T[:, 0:1])
                    e.drain()
                    e.memset(T[:, 0:1], 0.0)
                    e.drain()
                    e.scalar_tensor_tensor(ubbv(p)[:, :, 0], carry[:, ct * 16:ct * 16 + 16],
                                           AR[:, DC0 + p * 8:DC0 + p * 8 + 1],
                                           ubbv(p)[:, :, 0], op0=ALU.mult, op1=ALU.add)
                    e.drain().then_inc(ev, 1)
                add("ev")
                ev_u[gidx] = get("ev")

                @pr.on("vector")
                def _(e, ct=ct, p=p):
                    T = AR[:, CUMD + p * 512:CUMD + (p + 1) * 512]
                    for n in range(16):
                        e.tensor_tensor_scan(AR[:, SALL + n * 512:SALL + (n + 1) * 512],
                                             T,
                                             AR[:, DECR + n * 512:DECR + (n + 1) * 512],
                                             0.0, op0=ALU.mult, op1=ALU.add)
                    e.drain()
                    e.tensor_copy(carry[:, ct * 16:ct * 16 + 16], sallv()[:, :, 511])
                    e.drain().then_inc(ev, 1)
                add("ev")
                ev_scan[gidx] = get("ev")

                if town:
                    qy = get(qsem)

                    @pr.on("vector")
                    def _(e, ct=ct, p=p, qy=qy, qsm=SEM[qsem],
                          mmg=mm_st.get(gidx - 2, 0)):
                        TMv = AR[:, TM4:TM4 + 8192]
                        e.tensor_mul(TMv, AR[:, SALL:SALL + 8192],
                                     ARB[:, 2 * CREP:2 * CREP + 8192])
                        e.drain()
                        e.tensor_add(TMv[:, 0:4096], TMv[:, 0:4096], TMv[:, 4096:8192])
                        e.drain()
                        e.tensor_add(TMv[:, 0:2048], TMv[:, 0:2048], TMv[:, 2048:4096])
                        e.drain()
                        e.tensor_add(TMv[:, 0:1024], TMv[:, 0:1024], TMv[:, 1024:2048])
                        e.drain()
                        e.tensor_add(AR[:, YA4:YA4 + 512], TMv[:, 0:512], TMv[:, 512:1024])
                        e.drain()
                        Y2v = AR[:, Y24:Y24 + 512]
                        e.scalar_tensor_tensor(Y2v, AR[:, XCS2 + p * 512:XCS2 + (p + 1) * 512],
                                               VC("Dsk", 1, ct), AR[:, YA4:YA4 + 512],
                                               op0=ALU.mult, op1=ALU.add)
                        e.drain()
                        e.tensor_mul(Y2v, Y2v, AR[:, ZS4 + p * 512:ZS4 + (p + 1) * 512])
                        e.drain()
                        e.wait_ge(qsm, qy)
                        e.tensor_copy(ARB[:, 2 * (Y2B + p * 256):2 * (Y2B + p * 256) + 512], Y2v)
                        e.drain()
                        if mmg:
                            e.wait_ge(mm, mmg)
                        SQp = SQ2 if p == 0 else 0
                        e.tensor_mul(ARB[:, 2 * SQp:2 * SQp + 512], Y2v, Y2v)
                        e.drain().then_inc(ev, 1)
                    add("ev")
                ev_y2[gidx] = get("ev")

            def emit_tail(ct):
                gidx = tb * NCH + ct
                p = ct % 2
                qsem = "qs" if p == 0 else "qc"
                if town:
                    @pr.on("tensor")
                    def _(e, ct=ct, p=p, spv=ev_y2[gidx]):
                        e.wait_ge(ev, spv)
                        SQp = SQ2 if p == 0 else 0
                        e.matmul(pC[0:1, 0:512], onescb[:],
                                 ARB[:, 2 * (Y2B + p * 256):2 * (Y2B + p * 256) + 512],
                                 start=(ct == 0), stop=(ct == NCH - 1))
                        e.matmul(pD[0:1, 0:512], onescb[:], ARB[:, 2 * SQp:2 * SQp + 512],
                                 start=(ct == 0), stop=(ct == NCH - 1))
                        e.drain().then_inc(mm, 1)
                    add("mm")
                    mm_st[gidx] = get("mm")

                    @pr.on("sync")
                    def _(e, mmv=mm_st[gidx]):
                        e.wait_ge(mm, mmv)
                    dma("sync", sp_y2[ct, :, ts0 - WIN:ts0 - WIN + G],
                        ARB[:, 2 * (Y2B + p * 256):2 * (Y2B + p * 256) + 512], sem=qsem)
                    qs_y2sp[gidx] = get(qsem)

            def emit_ln1(gi, tt):
                tg = gi * 4 + tt

                @pr.on("sync")
                def _(e, g=mm_tr4.get(tg - 1, 0), tg=tg):
                    if tg == 0:
                        e.wait_ge(ql, qlv_q)
                    if g:
                        e.wait_ge(mm, g)
                dma("sync", ARB[:, 2 * F4XT:2 * F4XT + 1024],
                    sp_xn[WIN + gi * G + tt * 128:WIN + gi * G + (tt + 1) * 128, :],
                    sem="yr")
                yv = get("yr")

                @pr.on("scalar")
                def _(e, yv=yv, tt=tt, mg=mm_w1l4.get(gi - 1, 0)):
                    e.wait_ge(yr, yv)
                    if tt == 0 and mg:
                        e.wait_ge(mm, mg)
                    XT = ARB[:, 2 * F4XT:2 * F4XT + 1024]
                    e.activation(AR[:, F4SH:F4SH + 1024], XT, AF.Copy,
                                 accum_out=VC("l1r", 1, 0))
                    e.drain()
                    e.activation(AR[:, F4SH:F4SH + 1024], XT, AF.Square,
                                 accum_out=VC("l1r", 1, 1))
                    e.drain()
                    e.activation(VC("l1r", 1, 2), VC("l1r", 1, 0), AF.Copy,
                                 scale=1.0 / D)
                    e.drain()
                    e.activation(VC("l1r", 1, 6), VC("l1r", 1, 2), AF.Square)
                    e.drain()
                    e.activation(VC("l1r", 1, 7), VC("l1r", 1, 6), AF.Copy,
                                 scale=-1.0, bias=LN_EPS)
                    e.drain()
                    e.activation(VC("l1r", 1, 4), VC("l1r", 1, 1), AF.Sqrt,
                                 scale=1.0 / D, bias=VC("l1r", 1, 7))
                    e.drain().then_inc(sp, 1)
                add("sp")
                sp_st4[tg] = get("sp")
                spv4 = get("sp")

                @pr.on("vector")
                def _(e, spv=spv4):
                    e.wait_ge(sp, spv)
                    rs = VC("l1r", 1, 5)
                    e.reciprocal(rs, VC("l1r", 1, 4))
                    e.drain()
                    XT = ARB[:, 2 * F4XT:2 * F4XT + 1024]
                    e.tensor_scalar(XT, XT, VC("l1r", 1, 2), rs,
                                    op0=ALU.subtract, op1=ALU.mult)
                    e.drain().then_inc(ev, 1)
                add("ev")
                ev_n4[tg] = get("ev")
                evb = get("ev")

                @pr.on("tensor")
                def _(e, evn=ev_n4[tg], evb=evb, gpr=ev_ev4.get(tg - 1, 0),
                      spg=sp_gel4.get(gi * 32 + 31, 0) if tt == 0 else 0):
                    e.wait_ge(ev, evn)
                    if gpr:
                        e.wait_ge(ev, gpr)
                    if spg:
                        e.wait_ge(sp, spg)
                    for f in range(8):
                        if evb + f - 1 > 0:
                            e.wait_ge(ev, evb + f - 1)
                        pb = pG if f % 2 == 0 else pH
                        e.transpose(pb[:, 0:64].bitcast(BF16),
                                    ARB[:, 2 * F4XT + f * 128:2 * F4XT + (f + 1) * 128],
                                    identb[:])
                        e.drain().then_inc(mm, 1)
                add("mm", 8)
                mm_tr4[tg] = get("mm")
                mmv4 = get("mm") - 8

                @pr.on("vector")
                def _(e, tt=tt, mmv=mmv4, mg=mm_w1l4.get(gi - 1, 0), tg=tg):
                    if tg % 4 == 0 and mg:
                        e.wait_ge(mm, mg)
                    for f in range(8):
                        e.wait_ge(mm, mmv + f + 1)
                        pb = pG if f % 2 == 0 else pH
                        e.tensor_scalar(
                            ARB[:, 2 * F4XN + f * 512 + tt * 128:2 * F4XN + f * 512 + (tt + 1) * 128],
                            pb[:, 0:64].bitcast(BF16), VC("g1", 1, f),
                            VC("bt1", 1, f), op0=ALU.mult, op1=ALU.add)
                        e.drain().then_inc(ev, 1)
                add("ev", 8)
                ev_ev4[tg] = get("ev")
                if tt == 3:
                    evfb4[gi] = get("ev")

            def emit_mt(gi, mt):
                gmt = gi * 32 + mt

                @pr.on("gpsimd")
                def _(e, mt=mt, spg=sp_st4.get(gi * 4 + 3, 0),
                      mg=mm_w14.get(gmt - 1, 0)):
                    if mt == 0:
                        e.wait_ge(sp, spg)
                    if mg:
                        e.wait_ge(mm, mg)
                dma("gpsimd",
                    ARB[:, 2 * F4SH:2 * F4SH + 1024]
                    .rearrange("p (k c) -> p k c", c=128),
                    wq1d[:, :, mt * 128:(mt + 1) * 128]
                    .rearrange("k p c -> p k c"), sem="qm")
                qm_t4[gmt] = get("qm")

                @pr.on("tensor")
                def _(e, mt=mt, qmv=qm_t4[gmt], evf=evfb4[gi],
                      spfree=sp_gel4.get(gmt - 2, 0)):
                    e.wait_ge(qm, qmv)
                    if mt < 2:
                        e.wait_ge(ev, evf)
                    if spfree:
                        e.wait_ge(sp, spfree)
                    pb = pG if mt % 2 == 0 else pH
                    for k in range(8):
                        e.matmul(pb[:, 0:512],
                                 ARB[:, 2 * F4SH + k * 128:2 * F4SH + (k + 1) * 128],
                                 ARB[:, 2 * F4XN + k * 512:2 * F4XN + (k + 1) * 512],
                                 start=(k == 0), stop=(k == 7))
                    e.drain().then_inc(mm, 1)
                add("mm")
                mm_w14[gmt] = get("mm")
                if mt == 31:
                    mm_w1l4[gi] = get("mm")

                @pr.on("scalar")
                def _(e, mt=mt, mmv=mm_w14[gmt], qiv=qi_s4.get(gmt - 1, 0)):
                    e.wait_ge(mm, mmv)
                    if qiv:
                        e.wait_ge(qi, qiv)
                    pb = pG if mt % 2 == 0 else pH
                    e.activation(ARB[:, 2 * F4HB:2 * F4HB + 512], pb[:, 0:512],
                                 AF.Gelu, bias=VC("b1c", 1, mt))
                    e.drain().then_inc(sp, 1)
                add("sp")
                sp_gel4[gmt] = get("sp")

                @pr.on("gpsimd")
                def _(e, spv=sp_gel4[gmt]):
                    e.wait_ge(sp, spv)
                dma("gpsimd", sp_h[mt, :, gi * G:(gi + 1) * G],
                    ARB[:, 2 * F4HB:2 * F4HB + 512], sem="qi")
                qi_s4[gmt] = get("qi")

            for ctx in range(NCH + 1):
                if ctx < NCH:
                    emit_front(ctx)
                if ctx >= 1:
                    emit_tail(ctx - 1)
                if town and ctx < NCH:
                    gi6 = tb - NWG
                    if ctx < 4:
                        emit_ln1(gi6, ctx)
                    else:
                        mtb = (ctx - 4) * 3
                        for j in range(3):
                            if mtb + j < 32:
                                emit_mt(gi6, mtb + j)

            if town:
                mmv = get("mm")
                evv = get("ev")

                @pr.on("vector")
                def _(e, mmv=mmv, evv=evv):
                    e.wait_ge(mm, mmv)
                    e.wait_ge(ev, evv)
                    MU = AR[0:1, DECR:DECR + 512]
                    S2 = AR[0:1, DECR + 512:DECR + 1024]
                    VA = AR[0:1, DECR + 1024:DECR + 1536]
                    e.tensor_scalar_mul(MU, pC[0:1, 0:512], 1.0 / DI)
                    e.tensor_scalar_mul(S2, pD[0:1, 0:512], 1.0 / DI)
                    e.drain()
                    e.tensor_mul(VA, MU, MU)
                    e.drain()
                    e.tensor_sub(VA, S2, VA)
                    e.drain().then_inc(ev, 1)
                add("ev")
                evv = get("ev")

                @pr.on("scalar")
                def _(e, evv=evv):
                    e.wait_ge(ev, evv)
                    e.activation(AR[0:1, DECR + 1536:DECR + 2048],
                                 AR[0:1, DECR + 1024:DECR + 1536], AF.Sqrt,
                                 bias=LN_EPS).then_inc(sp, 1)
                add("sp")
                spv = get("sp")

                @pr.on("vector")
                def _(e, spv=spv):
                    e.wait_ge(sp, spv)
                    e.reciprocal(AR[0:1, DECR + 512:DECR + 1024],
                                 AR[0:1, DECR + 1536:DECR + 2048])
                    e.drain().then_inc(ev, 1)
                add("ev")
                evv = get("ev")

                @pr.on("sync")
                def _(e, evv=evv):
                    e.wait_ge(ev, evv)
                dma("sync", sp_st[0:1, ts0 - WIN:ts0 - WIN + G], AR[0:1, DECR:DECR + 512],
                    sem="qt")
                dma("sync", sp_st[1:2, ts0 - WIN:ts0 - WIN + G],
                    AR[0:1, DECR + 512:DECR + 1024], sem="qt")
            if p4bar:
                barrier()


        WOB, Y2G, MRS, RSR, MIXG, RW5, QW2 = (0, 8192, 12288, 12800, 13312,
                                              17408, 20480)

        evv = get("ev")

        @pr.on("vector")
        def _(e, evv=evv):
            e.wait_ge(ev, evv)
            rv = rb2.rearrange("p (k w) -> p k w", w=16)
            e.memset(rb2[:, :], 0.0)
            e.drain()
            e.tensor_copy(rv[:, 0:16, 0], VC("lng", 16))
            e.tensor_copy(rv[:, 0:16, 1], VC("lnb", 16))
            e.drain().then_inc(ev, 1)
        add("ev")

        qtP5 = get("qt")
        qcP5 = get("qc")
        evP4 = get("ev")
        mmP4 = get("mm")
        qiP4, qmP4, yrP4, spP4 = get("qi"), get("qm"), get("yr"), get("sp")
        QW2B = 18432
        ev_cast5 = {}
        for k in range(NCH):
            QWp = QW2 if k % 2 == 0 else QW2B
            lsem = "qs" if k % 2 == 0 else "qc"

            @pr.on("sync")
            def _(e, evv=ev_cast5.get(k - 2, evP4), qtP5=qtP5, qcP5=qcP5, k=k):
                e.wait_ge(ev, evv)
                if k < 2:
                    e.wait_ge(qt, qtP5)
                    e.wait_ge(qc, qcP5)
            dma("sync", AR[:, QWp:QWp + 1024], W_out[k * 128:(k + 1) * 128, :],
                sem=lsem)
            qsv = get(lsem)

            @pr.on("vector")
            def _(e, k=k, qsv=qsv, QWp=QWp, qsm=SEM[lsem], mmP4=mmP4):
                if k < 2:
                    e.wait_ge(mm, mmP4)
                    e.wait_ge(qi, qiP4)
                    e.wait_ge(qm, qmP4)
                    e.wait_ge(yr, yrP4)
                    e.wait_ge(sp, spP4)
                e.wait_ge(qsm, qsv)
                e.tensor_copy(ARB[:, 2 * WOB + k * 1024:2 * WOB + (k + 1) * 1024],
                              AR[:, QWp:QWp + 1024])
                e.drain().then_inc(ev, 1)
            add("ev")
            ev_cast5[k] = get("ev")
        evv = get("ev")

        evb5 = get("ev")

        @pr.on("tensor")
        def _(e, evv=evv, evb5=evb5):
            e.wait_ge(ev, evv)
            for m in range(8):
                if m >= 2:
                    e.wait_ge(ev, evb5 + m - 1)
                pb = pE if m % 2 == 0 else pF
                for k in range(NCH):
                    e.matmul(pb[0:128, 0:16],
                             ARB[:, 2 * WOB + k * 1024 + m * 128:2 * WOB + k * 1024 + (m + 1) * 128],
                             rb2.rearrange("p (k w) -> p k w", w=16)[:, k, :],
                             start=(k == 0), stop=(k == NCH - 1))
                e.drain().then_inc(mm, 1)
        add("mm", 8)
        mmv = get("mm") - 8

        @pr.on("vector")
        def _(e, mmv=mmv):
            for m in range(8):
                e.wait_ge(mm, mmv + m + 1)
                pb = pE if m % 2 == 0 else pF
                e.tensor_copy(VC("Wc", 1, m), pb[:, 0:1])
                e.tensor_copy(VC("Wb", 1, m), pb[:, 1:2])
                e.drain().then_inc(ev, 1)
            e.tensor_scalar_mul(VC("Wcn", 8), VC("Wc", 8), -1.0)
            e.drain()
            for k in range(NCH):
                e.tensor_scalar_mul(ARB[:, 2 * WOB + k * 1024:2 * WOB + (k + 1) * 1024],
                                    ARB[:, 2 * WOB + k * 1024:2 * WOB + (k + 1) * 1024],
                                    VC("lng", 1, k))
            e.drain().then_inc(ev, 1)
        add("ev", 9)
        if p4bar:
            barrier()

        qcP4 = get("qc")
        qtP4 = get("qt")
        Y2G2 = 21504
        evprep5 = get("ev")
        ts5 = {}
        mm_ml5 = {}

        def y2buf(gi):
            return Y2G if gi % 2 == 0 else Y2G2

        def emit_y2rel(gi, gate_mm):
            tsem = "qt" if gi % 2 == 0 else "yr"
            Yb = y2buf(gi)

            @pr.on("sync")
            def _(e, gate_mm=gate_mm):
                if gate_mm:
                    e.wait_ge(mm, gate_mm)
            for ct in range(NCH):
                dma("sync", ARB[:, 2 * Yb + ct * 512:2 * Yb + (ct + 1) * 512],
                    sp_y2[ct, :, gi * G:(gi + 1) * G], sem=tsem)
            ts5[gi] = get(tsem)

        for gi in range(4):
            ts0 = gi * G
            evv = get("ev")

            @pr.on("sync")
            def _(e, evv=evv, qcP4=qcP4, qtP4=qtP4):
                e.wait_ge(ev, evv)
                e.wait_ge(qc, qcP4)
                e.wait_ge(qt, qtP4)
            if gi == 0:
                emit_y2rel(0, 0)
            dma("sync", AR[0:1, RW5:RW5 + 512], sp_st[0:1, ts0:ts0 + G])
            dma("sync", AR[0:1, RW5 + 512:RW5 + 1024], sp_st[1:2, ts0:ts0 + G])
            qsv = get("qs")

            @pr.on("vector")
            def _(e, qsv=qsv):
                e.wait_ge(qs, qsv)
                e.tensor_mul(AR[0:1, RW5:RW5 + 512], AR[0:1, RW5:RW5 + 512],
                             AR[0:1, RW5 + 512:RW5 + 1024])
                e.drain().then_inc(ev, 1)
            add("ev")
            evv = get("ev")

            @pr.on("tensor")
            def _(e, evv=evv):
                e.wait_ge(ev, evv)
                e.matmul(pF[:, 0:512], ones1[:], AR[0:1, RW5:RW5 + 512],
                         start=True, stop=True)
                e.matmul(pG[:, 0:512], ones1[:], AR[0:1, RW5 + 512:RW5 + 1024],
                         start=True, stop=True)
                e.drain().then_inc(mm, 1)
            add("mm")
            mmv = get("mm")

            @pr.on("vector")
            def _(e, mmv=mmv):
                e.wait_ge(mm, mmv)
                e.tensor_copy(AR[:, MRS:MRS + 512], pF[:, 0:512])
                e.tensor_copy(AR[:, RSR:RSR + 512], pG[:, 0:512])
                e.drain().then_inc(ev, 1)
            add("ev")
            evrep = get("ev")

            ev_fold5 = {}
            Yb5 = y2buf(gi)
            tsm5 = SEM["qt" if gi % 2 == 0 else "yr"]
            PB5 = (pA, pB, pC, pD)
            for m in range(8):
                evv = ev_fold5.get(m - 4, evprep5)

                @pr.on("tensor")
                def _(e, m=m, evv=evv, tsv=ts5[gi], tsm5=tsm5, Yb5=Yb5):
                    if m < 4:
                        e.wait_ge(tsm5, tsv)
                    e.wait_ge(ev, evv)
                    pb = PB5[m % 4]
                    for k in range(NCH):
                        e.matmul(pb[:, 0:512],
                                 ARB[:, 2 * WOB + k * 1024 + m * 128:2 * WOB + k * 1024 + (m + 1) * 128],
                                 ARB[:, 2 * Yb5 + k * 512:2 * Yb5 + (k + 1) * 512],
                                 start=(k == 0), stop=(k == NCH - 1))
                    e.drain().then_inc(mm, 1)
                add("mm")
                mmv = get("mm")

                @pr.on("vector")
                def _(e, m=m, mmv=mmv, evrep=evrep):
                    e.wait_ge(mm, mmv)
                    e.wait_ge(ev, evrep)
                    pb = PB5[m % 4]
                    MX = AR[:, MIXG + m * 512:MIXG + (m + 1) * 512]
                    e.tensor_tensor(MX, pb[:, 0:512], AR[:, RSR:RSR + 512], op=ALU.mult)
                    e.drain()
                    e.scalar_tensor_tensor(MX, AR[:, MRS:MRS + 512], VC("Wcn", 1, m),
                                           MX, op0=ALU.mult, op1=ALU.add)
                    e.drain()
                    e.tensor_scalar_add(MX, MX, VC("Wb", 1, m))
                    e.drain().then_inc(ev, 1)
                add("ev")
                ev_fold5[m] = get("ev")
            mm_ml5[gi] = get("mm")
            if gi < 3:
                emit_y2rel(gi + 1, mm_ml5.get(gi - 1, 0))
            evv = get("ev")

            @pr.on("sync")
            def _(e, evv=evv):
                e.wait_ge(ev, evv)
            for f in range(8):
                dma("sync", sp_mix[f, :, ts0:ts0 + G],
                    AR[:, MIXG + f * 512:MIXG + (f + 1) * 512])
            if p4bar:
                barrier()

        barrier()

        barrier()

        WQ2, HGB, R17, R27, MIX2, OG, XTOK = (0, 16384, 24576, 25088, 25600,
                                              29696, 30208)
        QWA2, QP2, QN2 = 31232, 33280, 34304
        OT7 = 31232
        RW7 = 30208
        RW7 = 33280

        evv = get("ev")

        @pr.on("vector")
        def _(e, evv=evv):
            e.wait_ge(ev, evv)
            rv = rb2.rearrange("p (k w) -> p k w", w=16)
            e.memset(rb2[:, :], 0.0)
            e.drain()
            e.tensor_copy(rv[:, 0:32, 0], VC("g2", 32))
            e.tensor_copy(rv[:, 0:32, 1], VC("bt2", 32))
            e.drain().then_inc(ev, 1)
        add("ev")

        ev_q7 = {}
        evq0 = get("ev")
        for k in range(32):
            QWp = QWA2 if k % 2 == 0 else QWA2 + 1024
            lsem = "qs" if k % 2 == 0 else "qc"

            @pr.on("sync")
            def _(e, evv=ev_q7.get(k - 2, evq0)):
                e.wait_ge(ev, evv)
            dma("sync", AR[:, QWp:QWp + 1024], w2T[k * 128:(k + 1) * 128, :],
                sem=lsem)
            qsv = get(lsem)

            @pr.on("vector")
            def _(e, k=k, qsv=qsv, QWp=QWp, qsm=SEM[lsem]):
                e.wait_ge(qsm, qsv)
                e.tensor_scalar(AR[:, QP2:QP2 + 1024], AR[:, QWp:QWp + 1024],
                                VC("hm2"), VC("m2"), op0=ALU.is_gt, op1=ALU.mult)
                e.tensor_scalar(AR[:, QN2:QN2 + 1024], AR[:, QWp:QWp + 1024],
                                VC("hm2n"), VC("m2"), op0=ALU.is_lt, op1=ALU.mult)
                e.drain()
                e.tensor_tensor(ARB[:, 2 * WQ2 + k * 1024:2 * WQ2 + (k + 1) * 1024],
                                AR[:, QP2:QP2 + 1024], AR[:, QN2:QN2 + 1024],
                                op=ALU.subtract)
                e.drain().then_inc(ev, 1)
            add("ev")
            ev_q7[k] = get("ev")
        evv = get("ev")

        evb7 = get("ev")

        @pr.on("tensor")
        def _(e, evv=evv, evb7=evb7):
            e.wait_ge(ev, evv)
            for m in range(8):
                if m >= 2:
                    e.wait_ge(ev, evb7 + m - 1)
                pb = pE if m % 2 == 0 else pF
                for k in range(32):
                    e.matmul(pb[0:128, 0:16],
                             ARB[:, 2 * WQ2 + k * 1024 + m * 128:2 * WQ2 + k * 1024 + (m + 1) * 128],
                             rb2.rearrange("p (k w) -> p k w", w=16)[:, k, :],
                             start=(k == 0), stop=(k == 31))
                e.drain().then_inc(mm, 1)
        add("mm", 8)
        mmv = get("mm") - 8

        @pr.on("vector")
        def _(e, mmv=mmv):
            for m in range(8):
                e.wait_ge(mm, mmv + m + 1)
                pb = pE if m % 2 == 0 else pF
                e.tensor_copy(VC("W2c", 1, m), pb[:, 0:1])
                e.tensor_copy(VC("W2b", 1, m), pb[:, 1:2])
                e.drain().then_inc(ev, 1)
            e.tensor_scalar_mul(VC("W2cn", 8), VC("W2c", 8), -1.0)
            e.tensor_add(VC("W2bb", 8), VC("W2b", 8), VC("b2c", 8))
            e.drain()
            for k in range(32):
                e.tensor_scalar_mul(ARB[:, 2 * WQ2 + k * 1024:2 * WQ2 + (k + 1) * 1024],
                                    ARB[:, 2 * WQ2 + k * 1024:2 * WQ2 + (k + 1) * 1024],
                                    VC("g2", 1, k))
            e.drain().then_inc(ev, 1)
        add("ev", 9)

        ev_add8 = {}
        st_q8 = {}
        mm_m7 = {}
        for gi in range(4):
            ts0 = gi * G
            evv = get("ev")

            @pr.on("gpsimd")
            def _(e, g7=mm_m7.get(gi - 1, 0)):
                if g7:
                    e.wait_ge(mm, g7)
            for ct in range(32):
                dma("gpsimd", ARB[:, 2 * HGB + ct * 512:2 * HGB + (ct + 1) * 512],
                    sp_h[ct, :, ts0:ts0 + G], sem="qg")
            qg7 = get("qg")

            @pr.on("sync")
            def _(e, evv=evv):
                e.wait_ge(ev, evv)
            for f in range(8):
                dma("sync", AR[:, MIX2 + f * 512:MIX2 + (f + 1) * 512],
                    sp_mix[f, :, ts0:ts0 + G])
            qsv = get("qs")

            SQH7 = 36352
            sq7_sp = {}
            st7_mm = {}
            for ct in range(32):
                p7 = ct % 2
                sqs = SQH7 + p7 * 256

                @pr.on("scalar")
                def _(e, ct=ct, sqs=sqs, qsv=qg7, mmg=st7_mm.get(ct - 2, 0)):
                    if ct == 0:
                        e.wait_ge(qg, qsv)
                    if mmg:
                        e.wait_ge(mm, mmg)
                    e.activation(ARB[:, 2 * sqs:2 * sqs + 512],
                                 ARB[:, 2 * HGB + ct * 512:2 * HGB + (ct + 1) * 512],
                                 AF.Square)
                    e.drain().then_inc(sp, 1)
                add("sp")
                sq7_sp[ct] = get("sp")

                @pr.on("tensor")
                def _(e, ct=ct, sqs=sqs, spv=sq7_sp[ct], qsv=qg7):
                    if ct == 0:
                        e.wait_ge(qg, qsv)
                    e.matmul(pC[0:1, 0:512], onescb[:],
                             ARB[:, 2 * HGB + ct * 512:2 * HGB + (ct + 1) * 512],
                             start=(ct == 0), stop=(ct == 31))
                    e.wait_ge(sp, spv)
                    e.matmul(pD[0:1, 0:512], onescb[:], ARB[:, 2 * sqs:2 * sqs + 512],
                             start=(ct == 0), stop=(ct == 31))
                    e.drain().then_inc(mm, 1)
                add("mm")
                st7_mm[ct] = get("mm")
            mmv = get("mm")

            @pr.on("vector")
            def _(e, mmv=mmv, qsv=qsv):
                e.wait_ge(mm, mmv)
                e.wait_ge(qs, qsv)
                MU = AR[0:1, RW7:RW7 + 512]
                S2 = AR[0:1, RW7 + 512:RW7 + 1024]
                VA = AR[0:1, 36864:37376]
                e.tensor_scalar_mul(MU, pC[0:1, 0:512], 1.0 / FH)
                e.tensor_scalar_mul(S2, pD[0:1, 0:512], 1.0 / FH)
                e.drain()
                e.tensor_mul(VA, MU, MU)
                e.drain()
                e.tensor_sub(VA, S2, VA)
                e.drain().then_inc(ev, 1)
            add("ev")
            evv = get("ev")

            @pr.on("scalar")
            def _(e, evv=evv):
                e.wait_ge(ev, evv)
                e.activation(AR[0:1, 37376:37888],
                             AR[0:1, 36864:37376], AF.Sqrt, bias=LN_EPS)
                e.drain().then_inc(sp, 1)
            add("sp")
            spv = get("sp")

            @pr.on("vector")
            def _(e, spv=spv):
                e.wait_ge(sp, spv)
                e.reciprocal(AR[0:1, RW7 + 512:RW7 + 1024], AR[0:1, 37376:37888])
                e.drain()
                e.tensor_mul(AR[0:1, RW7:RW7 + 512], AR[0:1, RW7:RW7 + 512],
                             AR[0:1, RW7 + 512:RW7 + 1024])
                e.drain().then_inc(ev, 1)
            add("ev")
            evv = get("ev")

            @pr.on("tensor")
            def _(e, evv=evv):
                e.wait_ge(ev, evv)
                e.matmul(pE[:, 0:512], ones1[:], AR[0:1, RW7:RW7 + 512], start=True, stop=True)
                e.matmul(pF[:, 0:512], ones1[:], AR[0:1, RW7 + 512:RW7 + 1024],
                         start=True, stop=True)
                e.drain().then_inc(mm, 1)
            add("mm")
            mmv = get("mm")

            @pr.on("vector")
            def _(e, mmv=mmv):
                e.wait_ge(mm, mmv)
                e.tensor_copy(AR[:, R17:R17 + 512], pE[:, 0:512])
                e.tensor_copy(AR[:, R27:R27 + 512], pF[:, 0:512])
                e.drain().then_inc(ev, 1)
            add("ev")
            evrep = get("ev")

            ev_fold = {}
            PB7 = (pA, pB, pE, pF)
            for m in range(8):
                evv = ev_fold.get(m - 4, evrep)

                @pr.on("tensor")
                def _(e, m=m, evv=evv):
                    e.wait_ge(ev, evv)
                    pb = PB7[m % 4]
                    for k in range(32):
                        e.matmul(pb[:, 0:512],
                                 ARB[:, 2 * WQ2 + k * 1024 + m * 128:2 * WQ2 + k * 1024 + (m + 1) * 128],
                                 ARB[:, 2 * HGB + k * 512:2 * HGB + (k + 1) * 512],
                                 start=(k == 0), stop=(k == 31))
                    e.drain().then_inc(mm, 1)
                add("mm")
                mmv = get("mm")

                @pr.on("vector")
                def _(e, m=m, mmv=mmv, evrep=evrep):
                    e.wait_ge(mm, mmv)
                    e.wait_ge(ev, evrep)
                    pb = PB7[m % 4]
                    OV = AR[:, OG:OG + 512]
                    e.tensor_tensor(OV, pb[:, 0:512], AR[:, R27:R27 + 512], op=ALU.mult)
                    e.drain()
                    e.scalar_tensor_tensor(OV, AR[:, R17:R17 + 512], VC("W2cn", 1, m),
                                           OV, op0=ALU.mult, op1=ALU.add)
                    e.drain()
                    e.tensor_scalar_add(OV, OV, VC("W2bb", 1, m))
                    e.drain()
                    e.tensor_add(AR[:, MIX2 + m * 512:MIX2 + (m + 1) * 512],
                                 AR[:, MIX2 + m * 512:MIX2 + (m + 1) * 512], OV)
                    e.drain().then_inc(ev, 1)
                add("ev")
                ev_fold[m] = get("ev")
            mm_m7[gi] = get("mm")
            evsum = get("ev")

            XTOK2, OT72 = 34304, 35328
            for tsub in range(4):
                g8 = gi * 4 + tsub
                XTp = XTOK if tsub % 2 == 0 else XTOK2
                OTp = OT7 if tsub % 2 == 0 else OT72
                ssem = "qt" if tsub % 2 == 0 else "qc"

                @pr.on("sync")
                def _(e, gate=ev_add8.get(g8 - 2, 0)):
                    if gate:
                        e.wait_ge(ev, gate)
                dma("sync", AR[:, XTp:XTp + 1024],
                    xs[WIN + ts0 + tsub * 128:WIN + ts0 + tsub * 128 + 128, :],
                    sem=ssem)
                xlq = get(ssem)

                evbase8 = get("ev")

                @pr.on("tensor")
                def _(e, tsub=tsub, evsum=evsum, evbase8=evbase8):
                    e.wait_ge(ev, evsum)
                    for f in range(8):
                        if evbase8 + f - 1 > 0:
                            e.wait_ge(ev, evbase8 + f - 1)
                        pb = pG if f % 2 == 0 else pH
                        e.transpose(pb[:, 0:128],
                                    AR[:, MIX2 + f * 512 + tsub * 128:MIX2 + f * 512 + tsub * 128 + 128],
                                    identf[:])
                        e.drain().then_inc(mm, 1)
                add("mm", 8)
                mmv = get("mm") - 8

                @pr.on("vector")
                def _(e, tsub=tsub, mmv=mmv, OTp=OTp, xlq=xlq, stm=SEM[ssem]):
                    e.wait_ge(stm, xlq)
                    for f in range(8):
                        e.wait_ge(mm, mmv + f + 1)
                        pb = pG if f % 2 == 0 else pH
                        e.tensor_copy(AR[:, OTp + f * 128:OTp + (f + 1) * 128],
                                      pb[:, 0:128])
                        e.drain().then_inc(ev, 1)
                add("ev", 8)

                @pr.on("vector")
                def _(e, OTp=OTp, XTp=XTp):
                    e.tensor_add(AR[:, OTp:OTp + 1024], AR[:, OTp:OTp + 1024],
                                 AR[:, XTp:XTp + 1024])
                    e.drain().then_inc(ev, 1)
                add("ev")
                ev_add8[g8] = get("ev")
                evv = get("ev")

                @pr.on("sync")
                def _(e, evv=evv):
                    e.wait_ge(ev, evv)
                dma("sync", out[ts0 + tsub * 128:ts0 + tsub * 128 + 128, :],
                    AR[:, OTp:OTp + 1024], sem=ssem)
                st_q8[g8] = get(ssem)

        barrier()
        if debug:
            for nmout, src in [(dbg_y2, sp_y2), (dbg_st, sp_st), (dbg_st2, sp_st2),
                               (dbg_mix, sp_mix), (dbg_xc, sp_xc),
                               (dbg_h, sp_h), (dbg_xn, sp_xn)]:
                dma("sync", nmout[:], src[:])
            qsv = get("qs")

            @pr.on("sync")
            def _(e, qsv=qsv):
                e.wait_ge(qs, qsv)
        _unused = (debug,)

        for eng in ENGS:
            qq = pr.q[eng]
            if cut is not None and cut < len(barrier_marks):
                qq = qq[:barrier_marks[cut][eng]]
            getattr(blk, eng)(lambda e, qq=qq: [f(e) for f in qq])

    return nc


_NC_CACHE = [None]


def kernel(**inputs):
    x = np.asarray(inputs["L0_attn_x"])
    B, L, _ = x.shape

    def rs(v, r):
        return np.ascontiguousarray(np.asarray(v).reshape(r, 128))

    base = {
        "W_in": np.ascontiguousarray(inputs["L0_attn_W_in"]),
        "conv_w": np.ascontiguousarray(
            np.asarray(inputs["L0_attn_conv_w"]).reshape(16, 128, 4)
            .transpose(1, 0, 2).reshape(128, 64)),
        "conv_b": rs(inputs["L0_attn_conv_b"], 16),
        "W_xp": np.ascontiguousarray(inputs["L0_attn_W_xproj"]),
        "W_dt": np.ascontiguousarray(inputs["L0_attn_W_dt"]),
        "b_dt": rs(inputs["L0_attn_b_dt"], 16),
        "A_log": np.ascontiguousarray(np.asarray(inputs["L0_attn_A_log"]).reshape(1, NH)),
        "D_skip": np.ascontiguousarray(np.asarray(inputs["L0_attn_D_skip"]).reshape(1, NH)),
        "W_out": np.ascontiguousarray(inputs["L0_attn_W_out"]),
        "ln_g": rs(inputs["L0_attn_ln_g"], 16),
        "ln_b": rs(inputs["L0_attn_ln_b"], 16),
        "ln1_g": rs(inputs["L0_ffn_ln1_g"], 8),
        "ln1_b": rs(inputs["L0_ffn_ln1_b"], 8),
        "w1T": np.ascontiguousarray(np.asarray(inputs["L0_ffn_w1"]).T),
        "b1": rs(inputs["L0_ffn_b1"], 32),
        "ln2_g": rs(inputs["L0_ffn_ln2_g"], 32),
        "ln2_b": rs(inputs["L0_ffn_ln2_b"], 32),
        "w2T": np.ascontiguousarray(np.asarray(inputs["L0_ffn_w2"]).T),
        "b2": rs(inputs["L0_ffn_b2"], 8),
    }
    in_maps = []
    for c in range(8):
        b, h = c // 2, c % 2
        s0 = h * OWN
        xsl = np.zeros((TOT, D), np.float32)
        lo = s0 - WIN
        src_lo = max(lo, 0)
        xsl[src_lo - lo:, :] = x[b, src_lo:s0 + OWN, :]
        in_maps.append(dict(base, xs=xsl))

    if _NC_CACHE[0] is None:
        _NC_CACHE[0] = build()
    res = run_bass_kernel_spmd(_NC_CACHE[0], in_maps, core_ids=list(range(8)))
    outp = np.empty((B, L, D), np.float32)
    for c in range(8):
        b, h = c // 2, c % 2
        outp[b, h * OWN:(h + 1) * OWN, :] = res.results[c]["out"]
    return outp


if __name__ == "__main__":
    nc = build()
    print("built ok;", "SBUF part bytes:", nc.SBUF_PARTITION_SIZE_BYTES)
